# version 11
# baseline (speedup 1.0000x reference)
import contextlib
import dataclasses
import numpy as np
import ml_dtypes
import concourse.bass as bass
import concourse.mybir as mybir

F32 = mybir.dt.float32
BF16 = mybir.dt.bfloat16
AF = mybir.ActivationFunctionType
ALU = mybir.AluOpType
AX = mybir.AxisListType
NPBF = ml_dtypes.bfloat16

B, S, D, FF, E = 2, 8192, 1024, 4096, 8
T = 128
DCH = D // T
NOWN = 16
NOT = NOWN * T
NCHUNK = 8
KC = 1024
NWIN = FF // 512

C_1024, C_EPS, C_NEG1, C_TINY, C_8192, C_ONE, C_NEGBIG, C_ZERO, C_32, C_512, C_DFF, C_W2S, C_256, C_W2H = range(14)
CONST_VALS = [1.0 / 1024, 1e-5, -1.0, 1e-30, 1.0 / 8192, 1.0, -1e9, 0.0,
              32.0, 512.0, 4194304.0, 524288.0, 256.0, 262144.0]
NCONST = len(CONST_VALS)

ENGS = ("sync", "gpsimd", "tensor", "vector", "scalar")


class Sem:
    def __init__(self, nc, ctx, name):
        self.h = ctx.enter_context(nc.semaphore(name))
        self.n = 0

    def inc(self, k=1):
        self.n += k
        return lambda inst, h=self.h, k=k: inst.then_inc(h, k)

    def at(self):
        return (self.h, self.n)


class Ph:
    def __init__(self, name):
        self.name = name
        self.ops = {e: [] for e in ENGS}

    def op(self, eng, fn):
        self.ops[eng].append(fn)

    def wait(self, eng, *marks):
        for m in marks:
            if m is None:
                continue
            h, v = m
            if v > 0:
                self.ops[eng].append(lambda e, h=h, v=v: e.wait_ge(h, v))

    def emit(self, nc):
        with nc.Block(self.name) as blk:
            for eng in ENGS:
                fns = self.ops[eng]
                if fns:
                    getattr(blk, eng)(lambda e, fns=fns: [f(e) for f in fns])


class Pool:
    def __init__(self, items):
        self.items = list(items)
        self.marks = [None] * len(self.items)
        self.i = 0

    def take(self, p, eng):
        j = self.i % len(self.items)
        self.i += 1
        p.wait(eng, self.marks[j])
        return j, self.items[j]

    def release(self, j, sem):
        self.marks[j] = sem.at()


_RING_N = [0]


class Ring:

    def __init__(self, nc, ctx, name, slots):
        _RING_N[0] += 1
        name = f"{name}_{_RING_N[0]}"
        self.slots = slots
        self.sems = [Sem(nc, ctx, f"{name}{i}") for i in range(len(slots))]
        self.marks = [None] * len(slots)
        self.i = 0

    def next(self, p, eng):
        j = self.i % len(self.slots)
        self.i += 1
        p.wait(eng, self.marks[j])
        return j, self.slots[j], self.sems[j]

    def release(self, j, sem):
        self.marks[j] = sem.at()


class L0:

    def __init__(self, dbg=False, nt0=NOWN):
        self.dbg = dbg
        self.NT0 = nt0
        self.NTT = nt0 * T
        nc = bass.Bass("TRN2", target_bir_lowering=False)
        self.nc = nc
        self.ctx = contextlib.ExitStack()
        self.ctx.__enter__()

        inp = lambda n, s, d=F32: nc.dram_tensor(n, s, d, kind="ExternalInput").ap()
        outp = lambda n, s, d=F32: nc.dram_tensor(n, s, d, kind="ExternalOutput").ap()
        self.x_seq = inp("x_seq", [S, D])
        self.x_own = inp("x_own", [self.NTT, D])
        self.cmask = inp("cmask", [self.NT0, NCHUNK, T, KC], BF16)
        self.wt = inp("wt", [D, 2 * D])
        self.w1 = inp("w1", [E, D, FF])
        self.b1r = inp("b1r", [T, E * (FF // T)])
        self.w2 = inp("w2", [E, FF, D])
        self.b2 = inp("b2", [E, D])
        self.wr = inp("wr", [D, E])
        self.br = inp("br", [1, E])
        self.gmb = inp("gmb", [T, D])
        self.btb = inp("btb", [T, D])
        self.iota8 = inp("iota8", [1, 16])
        self.cst_in = inp("cst_in", [T, NCONST])
        self.idn_in = inp("idn", [T, T], BF16)
        self.on1_in = inp("on1", [1, T], BF16)
        self.x1o = outp("x1o", [NOT, D])
        if dbg:
            self.d_l = outp("d_l", [T, NOWN])
            self.d_mr = outp("d_mr", [T, NOWN])
            self.d_s0 = outp("d_s0", [T, NOWN * D], BF16)
            self.d_cw = outp("d_cw", [T, E])
            self.d_rw = outp("d_rw", [1, 2 * E])
            self.d_y = outp("d_y", [T, NOWN * D])
            self.d_xnfm = outp("d_xnfm", [T, DCH * NOT], BF16)
        self.scr8 = nc.dram_tensor("scr8", [1, E], F32, kind="Internal").ap()

        c = self.ctx
        sb = lambda n, s, d=F32: c.enter_context(nc.sbuf_tensor(n, s, d))
        pt = lambda n, s, d=F32: c.enter_context(nc.psum_tensor(n, s, d))
        self.cst = sb("cst", [T, NCONST])
        self.idn = sb("idn_s", [T, T], BF16)
        self.on1 = sb("on1_s", [1, T], BF16)
        self.b2s = sb("b2s", [E, D], BF16)
        self.wrs = sb("wrs", [T, DCH * E], BF16)
        self.brs = sb("brs", [1, E], BF16)
        self.meanbf = sb("meanbf", [T, DCH], BF16)
        self.xnfm = sb("xnfm", [T, DCH * self.NTT], BF16)
        self.r_own = sb("r_own", [T, self.NT0])
        self.nm_own = sb("nm_own", [T, self.NT0])
        self.r_kv = sb("r_kv", [T, 16])
        self.nm_kv = sb("nm_kv", [T, 16])
        self.O = sb("O", [T, self.NT0 * D])
        self.lbuf = sb("lbuf", [T, self.NT0])
        self.mrun = sb("mrun", [T, self.NT0])
        self.rsum = sb("rsum", [T, DCH])
        self.rtmp = sb("rtmp", [T, DCH])
        self.tsc = sb("tsc", [T, 20])
        self.tscl = sb("tscl", [T, 48])
        self.sxs = sb("sxs", [T, 18])
        self.sqs = sb("sqs", [T, 18])
        self.cwb = sb("cwb", [T, E])
        self.rw = sb("rw", [1, 2 * E])
        self.rws = sb("rws", [1, 32])
        self.bsum_b = sb("bsum_b", [1, D], BF16)
        self.bsb = sb("bsb", [T, D])
        self.cw_bf = sb("cw_bf", [1, E], BF16)
        self.cwcol = sb("cwcol", [E, 1], BF16)
        self.io8 = sb("io8", [1, 16])
        self.eidx = sb("eidx", [1, 34], mybir.dt.int32)
        self.eoff = sb("eoff", [1, 34])
        self.cwb2 = sb("cwb2", [T, 2])
        self.scr2 = nc.dram_tensor("scr2", [1, 2], F32, kind="Internal").ap()

        self.psb = [pt(f"psf{i}", [T, 512]) for i in range(5)]
        self.pt2 = [pt(f"pt2_{i}", [T, 1024], BF16) for i in range(2)]
        self.ps_r = pt("ps_r", [T, 512])

        self.s_c0 = Sem(nc, c, "s_c0")
        self.s_g0 = Sem(nc, c, "s_g0")
        self.s_sc = Sem(nc, c, "s_sc")
        self.s_b1 = Sem(nc, c, "s_b1")
        self.s_gb = Sem(nc, c, "s_gb")
        self.s_cw = Sem(nc, c, "s_cw")
        self.s_ix = Sem(nc, c, "s_ix")
        self.pe = Sem(nc, c, "pe")
        self.dv = Sem(nc, c, "dv")
        self.ac = Sem(nc, c, "ac")
        self.so = Sem(nc, c, "so")

        self.bank = Pool(list(range(5)))
        self.tslot = Pool(list(range(2)))

    NMPfx = ""
    B2S_LATE = False

    def fbank(self, j):
        return self.psb[j] if j < 5 else self.ps_r

    def sbt(self, stack, n, s, d=F32):
        return stack.enter_context(
            self.nc.sbuf_tensor(self.NMPfx + n, s, d))

    def cc(self, col):
        return self.cst[:, col:col + 1]

    def dma(self, p, eng, sem, dst, src, guard=None, **kw):
        if guard is not None:
            p.wait(eng, guard)
        incf = sem.inc(16)
        p.op(eng, lambda e, dst=dst, src=src, incf=incf, kw=kw:
             incf(e.dma_start(out=dst, in_=src, **kw)))

    STRICT_EDGES = True

    def v(self, p, fn, inc=False):
        incf = self.dv.inc(1)
        p.op("vector", lambda e, fn=fn, incf=incf: incf(fn(e)))
        if self.STRICT_EDGES:
            h, vv = self.dv.at()
            p.op("vector", lambda e, h=h, vv=vv: e.wait_ge(h, vv))

    def ve(self, p, fn):
        incf = self.dv.inc(1)
        p.op("vector", lambda e, fn=fn, incf=incf: incf(fn(e)))
        h, vv = self.dv.at()
        p.op("vector", lambda e, h=h, vv=vv: e.wait_ge(h, vv))

    def a(self, p, fn, inc=False):
        incf = self.ac.inc(1)
        p.op("scalar", lambda e, fn=fn, incf=incf: incf(fn(e)))
        if self.STRICT_EDGES:
            h, vv = self.ac.at()
            p.op("scalar", lambda e, h=h, vv=vv: e.wait_ge(h, vv))

    def t(self, p, fn, inc=False):
        if inc:
            incf = self.pe.inc(1)
            p.op("tensor", lambda e, fn=fn, incf=incf: incf(fn(e)))
        else:
            p.op("tensor", fn)

    def ln_chain(self, p, lo, n, r_dst, nm_dst):
        sx = self.sxs[:, lo:lo + n]
        sq = self.sqs[:, lo:lo + n]
        mu = self.tscl[:, 0:n]
        var = self.tscl[:, 24:24 + n]
        r = r_dst[:, lo:lo + n]
        nm = nm_dst[:, lo:lo + n]
        self.ve(p, lambda e: e.tensor_scalar(mu, sx, self.cc(C_1024), None,
                                             ALU.mult))
        self.ve(p, lambda e: e.tensor_scalar(var, sq, self.cc(C_1024),
                                             self.cc(C_EPS), ALU.mult, ALU.add))
        self.ve(p, lambda e: e.tensor_mul(sq, mu, mu))
        self.ve(p, lambda e: e.tensor_sub(var, var, sq))
        p.wait("scalar", self.dv.at())
        self.a(p, lambda e: e.activation(r, var, AF.Sqrt), inc=True)
        p.wait("vector", self.ac.at())
        self.ve(p, lambda e: e.reciprocal(r, r))
        self.ve(p, lambda e: e.tensor_mul(nm, mu, r))
        self.ve(p, lambda e: e.tensor_scalar(nm, nm, self.cc(C_NEG1), None,
                                             ALU.mult))

    def stats_tile(self, p, xo, col):
        self.v(p, lambda e: e.reduce_sum(self.sxs[:, col:col + 1], xo,
                                         axis=AX.X))
        self.v(p, lambda e: e.tensor_mul(self.sqscr[:], xo, xo))
        self.v(p, lambda e: e.reduce_sum(self.sqs[:, col:col + 1],
                                         self.sqscr[:], axis=AX.X), inc=True)

    def fm_tile(self, p, xo, r_col, nm_col, fm_dst, col_fn):
        p.wait("vector", self.pe.at())
        self.v(p, lambda e, xo=xo, r_col=r_col, nm_col=nm_col:
               e.tensor_scalar(self.xntm[:], xo, r_col, nm_col,
                               ALU.mult, ALU.add), inc=True)
        for dch in range(DCH):
            jt, q = self.tslot.take(p, "tensor")
            sub = self.pt2[q][:, 0:128]
            p.wait("tensor", self.dv.at())
            self.t(p, lambda e, dch=dch, sub=sub:
                   e.transpose(sub, self.xntm[:, dch * T:(dch + 1) * T],
                               self.idn[:]), inc=True)
            p.wait("scalar", self.pe.at())
            co = col_fn(dch)
            self.a(p, lambda e, sub=sub, fm_dst=fm_dst, co=co:
                   e.copy(fm_dst[:, co:co + T], sub), inc=True)
            self.tslot.release(jt, self.ac)

    def build(self, upto=6):
        nc = self.nc
        self.upto = upto
        self.p0()
        with contextlib.ExitStack() as s12:
            sb2 = lambda n, s, d=F32: self.sbt(s12, n, s, d)
            self.sqscr = sb2("sqscr", [T, D])
            self.wkv = sb2("wkv", [T, DCH * 2 * D], BF16)
            self.Ksb = sb2("Ksb", [T, DCH * KC], BF16)
            self.Vsb = sb2("Vsb", [T, 8 * D], BF16)
            self.kvfm = sb2("kvfm", [T, DCH * 512], BF16)
            self.xs = [sb2(f"xs{i}", [T, D]) for i in range(2)]
            self.xntm = sb2("xntm", [T, D], BF16)
            self.mks = [sb2(f"mk{i}", [T, KC], BF16) for i in range(3)]
            self.Psb = sb2("Psb", [T, 2 * KC], BF16)
            self.xsr = Ring(nc, self.ctx, "rxs", self.xs)
            self.mkr = Ring(nc, self.ctx, "rmk", self.mks)
            self.p1()
            if upto >= 2:
                self.p2()
        if upto < 2:
            return nc
        with contextlib.ExitStack() as s36:
            sb3 = lambda n, s, d=F32: self.sbt(s36, n, s, d)
            self.s0 = sb3("s0", [T, self.NT0 * D], BF16)
            self.xo2 = [sb3(f"xo2{i}", [T, D]) for i in range(2)]
            self.xo2r = Ring(nc, self.ctx, "rxo", self.xo2)
            self.xsc = sb3("xsc", [T, D])
            if upto >= 3:
                self.p3()
            if upto >= 4:
                pm = Ph("p4m")
                self.p4_main(pm)
                pm.emit(nc)
                self.p4()
            if upto < 5:
                return nc
            with contextlib.ExitStack() as s5:
                sb5 = lambda n, s, d=F32: self.sbt(s5, n, s, d)
                self.b1s = sb5("b1s", [T, E * (FF // T)])
                self.w1w = [sb5(f"w1w{i}", [T, DCH * 512], BF16) for i in range(2)]
                self.w2w = [sb5(f"w2w{i}", [T, 4 * D], BF16) for i in range(2)]
                self.w1r = Ring(nc, self.ctx, "rw1", self.w1w)
                self.w2r = Ring(nc, self.ctx, "rw2", self.w2w)
                self.hsb = [sb5(f"hsb{i}", [T, self.NTT], BF16)
                            for i in range(5)]
                self.p5()
            if upto < 6:
                return nc
            with contextlib.ExitStack() as s6:
                sb6 = lambda n, s, d=F32: self.sbt(s6, n, s, d)
                self.sqscr6 = sb6("sqscr6", [T, D])
                self.gscr = sb6("gscr", [T, D])
                self.gm = sb6("gm", [T, D])
                self.bt = sb6("bt", [T, D])
                self.p6()
        return nc

    def p0(self):
        p = Ph("p0")
        for dst, srca in [(self.cst, self.cst_in), (self.idn, self.idn_in),
                          (self.on1, self.on1_in), (self.io8, self.iota8)]:
            self.dma(p, "sync", self.s_c0, dst[:], srca)
        if not self.B2S_LATE:
            self.dma(p, "gpsimd", self.s_g0, self.b2s[:], self.b2)
        self.dma(p, "gpsimd", self.s_g0,
                 self.wrs[:].rearrange("p (c e) -> p c e", c=DCH),
                 self.wr.rearrange("(c p) e -> p c e", p=T, c=DCH))
        self.dma(p, "gpsimd", self.s_g0, self.brs[:], self.br)
        p.wait("vector", self.s_c0.at(), self.s_g0.at())
        p.wait("scalar", self.s_c0.at())
        p.wait("tensor", self.s_c0.at(), self.s_g0.at())
        p.emit(self.nc)

    def p1(self):
        p = Ph("p1")
        for i in range(self.NT0):
            j, xsl, xsem = self.xsr.next(p, "sync")
            self.dma(p, "sync", xsem, xsl[:],
                     self.x_own[i * T:(i + 1) * T, :])
            p.wait("vector", xsem.at())
            self.stats_tile(p, xsl[:], i)
            self.xsr.release(j, self.dv)
        self.ln_chain(p, 0, self.NT0, self.r_own, self.nm_own)
        for i in range(self.NT0):
            j, xsl, xsem = self.xsr.next(p, "sync")
            self.dma(p, "sync", xsem, xsl[:],
                     self.x_own[i * T:(i + 1) * T, :])
            p.wait("vector", xsem.at())
            self.fm_tile(p, xsl[:],
                         self.r_own[:, i:i + 1], self.nm_own[:, i:i + 1],
                         self.xnfm,
                         lambda dch, i=i: dch * self.NTT + i * T)
            self.xsr.release(j, self.dv)
        if self.dbg:
            p.wait("sync", self.ac.at())
            self.dma(p, "sync", self.so, self.d_xnfm, self.xnfm[:])
            p.wait("sync", self.so.at())
        p.emit(self.nc)

    def p2(self):
        nc = self.nc
        p = Ph("p2w")
        self.dma(p, "gpsimd", self.s_g0,
                 self.wkv[:].rearrange("p (c n) -> p c n", c=DCH),
                 self.wt.rearrange("(c p) n -> p c n", p=T, c=DCH))
        p.emit(nc)

        self.fpair = Pool([(0, 1), (2, 3), (4, 5)])
        self.fm_pre = []
        self.pv_mark = {}
        self.exp_mark = {}
        chunks = list(reversed(range(NCHUNK)))
        p = Ph("p2s0")
        for pr in range(4):
            self.pro_loadstats_part(p, chunks[0], 0, pr)
        p.emit(nc)
        for ci, c in enumerate(chunks):
            first = ci == 0
            par = ci % 2
            p = Ph(f"p2c{c}")
            mkq = {}

            def issue_mask(ii, p=p, c=c, mkq=mkq):
                jm, mkb, msem = self.mkr.next(p, "sync")
                self.dma(p, "sync", msem, mkb[:], self.cmask[ii, c, :, :])
                mkq[ii] = (jm, mkb, msem)
            issue_mask(0)
            if self.NT0 > 1:
                issue_mask(1)
            if first:
                p.wait("tensor", self.s_g0.at(), self.ac.at())
            self.pro_kv(p, c, par)
            if ci == NCHUNK - 1:
                kvm = (self.dv.at(), self.ac.at())
                self.p4_main(p)
            else:
                kvm = (self.dv.at(), self.ac.at())
            smq = None
            pvq = None
            for i in range(self.NT0):
                cur = self.flash_qk(p, i, c, first, mkq.pop(i), kvm)
                if smq is not None:
                    self.flash_sm(p, *smq)
                if i + 2 < self.NT0:
                    issue_mask(i + 2)
                if smq is not None:
                    if pvq is not None:
                        self.flash_pv(p, *pvq)
                    pvq = smq
                smq = cur
                if ci + 1 < NCHUNK:
                    if 4 <= i < 8:
                        self.pro_loadstats_part(p, chunks[ci + 1],
                                                (ci + 1) % 2, i - 4)
                    elif 10 <= i < 12:
                        self.pro_fm_load(p, chunks[ci + 1], i - 10)
            self.flash_sm(p, *smq)
            self.flash_pv(p, *pvq)
            self.flash_pv(p, *smq)
            p.emit(nc)
        for j in range(5):
            self.bank.marks[j] = self.dv.at()

    def pro_loadstats_part(self, p, c, par, pr):
        for t2 in range(2):
            tt = pr * 2 + t2
            j, xsl, xsem = self.xsr.next(p, "sync")
            r0 = c * KC + tt * T
            self.dma(p, "sync", xsem, xsl[:], self.x_seq[r0:r0 + T, :])
            p.wait("vector", xsem.at())
            self.stats_tile(p, xsl[:], par * 8 + tt)
            self.xsr.release(j, self.dv)

    def pro_fm_load(self, p, c, tt):
        j, xsl, xsem = self.xsr.next(p, "sync")
        r0 = c * KC + tt * T
        self.dma(p, "sync", xsem, xsl[:], self.x_seq[r0:r0 + T, :])
        self.fm_pre.append((j, xsl, xsem))

    def pro_kv(self, p, c, par):
        self.ln_chain(p, par * 8, 8, self.r_kv, self.nm_kv)
        for pr in range(4):
            sub = pr // 2
            tin = pr % 2
            for t2 in range(2):
                tt = pr * 2 + t2
                if self.fm_pre:
                    j, xsl, xsem = self.fm_pre.pop(0)
                else:
                    j, xsl, xsem = self.xsr.next(p, "sync")
                    r0 = c * KC + tt * T
                    self.dma(p, "sync", xsem, xsl[:],
                             self.x_seq[r0:r0 + T, :])
                p.wait("vector", xsem.at())
                self.fm_tile(p, xsl[:],
                             self.r_kv[:, par * 8 + tt:par * 8 + tt + 1],
                             self.nm_kv[:, par * 8 + tt:par * 8 + tt + 1],
                             self.kvfm,
                             lambda dch, tin=tin, t2=t2:
                             dch * 512 + (tin * 2 + t2) * T)
                self.xsr.release(j, self.dv)
            if pr % 2 == 1:
                self.kv_proj(p, c, sub)

    def kv_proj(self, p, c, sub):
        p.wait("vector", self.ac.at())
        for dch in range(DCH):
            self.v(p, lambda e, dch=dch: e.reduce_sum(
                self.rtmp[:, dch:dch + 1],
                self.kvfm[:, dch * 512:(dch + 1) * 512], axis=AX.X))
        if c == NCHUNK - 1 and sub == 0:
            self.ve(p, lambda e: e.tensor_copy(self.rsum[:], self.rtmp[:]))
        else:
            self.ve(p, lambda e: e.tensor_add(self.rsum[:], self.rsum[:],
                                              self.rtmp[:]))
        p.wait("tensor", self.ac.at())
        jp, pr_b = self.fpair.take(p, "tensor")
        A, B = self.fbank(pr_b[0]), self.fbank(pr_b[1])
        mAB = [None, None]
        for dk in range(DCH):
            bk = A if dk % 2 == 0 else B
            if mAB[dk % 2] is not None:
                p.wait("tensor", mAB[dk % 2])
            for dn in range(DCH):
                self.t(p, lambda e, dn=dn, dk=dk, bk=bk:
                       e.matmul(bk[:],
                                self.wkv[:, dn * 2048 + dk * T:
                                          dn * 2048 + dk * T + T],
                                self.kvfm[:, dn * 512:(dn + 1) * 512],
                                start=(dn == 0), stop=(dn == DCH - 1)),
                       inc=(dn == DCH - 1))
            p.wait("vector", self.pe.at())
            self.v(p, lambda e, bk=bk, dk=dk:
                   e.tensor_copy(self.Ksb[:, dk * KC + sub * 512:
                                          dk * KC + sub * 512 + 512], bk[:]),
                   inc=True)
            mAB[dk % 2] = self.dv.at()
        for t4 in range(4):
            ks = sub * 4 + t4
            p.wait("tensor", mAB[0], mAB[1])
            for dn in range(DCH):
                self.t(p, lambda e, dn=dn, t4=t4, A=A:
                       e.matmul(A[:],
                                self.kvfm[:, dn * 512 + t4 * T:
                                          dn * 512 + t4 * T + T],
                                self.wkv[:, dn * 2048 + 1024:dn * 2048 + 1536],
                                start=(dn == 0), stop=(dn == DCH - 1)))
                self.t(p, lambda e, dn=dn, t4=t4, B=B:
                       e.matmul(B[:],
                                self.kvfm[:, dn * 512 + t4 * T:
                                          dn * 512 + t4 * T + T],
                                self.wkv[:, dn * 2048 + 1536:dn * 2048 + 2048],
                                start=(dn == 0), stop=(dn == DCH - 1)),
                       inc=(dn == DCH - 1))
            p.wait("vector", self.pe.at())
            self.v(p, lambda e, A=A, ks=ks:
                   e.tensor_copy(self.Vsb[:, ks * D:ks * D + 512],
                                 A[:]))
            self.v(p, lambda e, B=B, ks=ks:
                   e.tensor_copy(self.Vsb[:, ks * D + 512:ks * D + 1024],
                                 B[:]), inc=True)
            mAB[0] = mAB[1] = self.dv.at()
        self.fpair.release(jp, self.dv)

    def flash_qk(self, p, i, c, first, mkh, kvm):
        jp, pr_b = self.fpair.take(p, "tensor")
        s0b, s1b = pr_b
        if i == 0:
            p.wait("tensor", *kvm)
        for s2, bi in ((0, s0b), (1, s1b)):
            for dn in range(DCH):
                self.t(p, lambda e, dn=dn, s2=s2, bi=bi, i=i:
                       e.matmul(self.fbank(bi)[:],
                                self.xnfm[:, dn * self.NTT + i * T:
                                          dn * self.NTT + i * T + T],
                                self.Ksb[:, dn * KC + s2 * 512:
                                         dn * KC + s2 * 512 + 512],
                                start=(dn == 0), stop=(dn == DCH - 1)),
                       inc=(dn == DCH - 1))
        qk_pe = self.pe.at()
        return (i, c, first, jp, s0b, s1b, mkh, qk_pe)

    def flash_sm(self, p, i, c, first, jp, s0b, s1b, mkh, qk_pe):
        par = i % 2
        negmn = self.tsc[:, 0 + par:1 + par]
        dold = self.tsc[:, 2 + par:3 + par]
        alpha = self.tsc[:, 4 + par:5 + par]
        lbs = self.tsc[:, 6 + par:7 + par]
        bm = self.tsc[:, 8 + par:9 + par]
        bm2 = self.tsc[:, 10 + 2 * par:12 + 2 * par]
        lb = self.tsc[:, 16 + 2 * par:18 + 2 * par]
        mr = self.mrun[:, i:i + 1]
        lc = self.lbuf[:, i:i + 1]
        oc = self.O[:, i * D:(i + 1) * D]
        Pv = self.Psb[:, par * KC:(par + 1) * KC]
        jm, mk, msem = mkh

        p.wait("vector", msem.at(), qk_pe)
        self.v(p, lambda e, s0b=s0b, mk=mk: e.tensor_add(
            self.fbank(s0b)[:], self.fbank(s0b)[:], mk[:, 0:512]))
        self.v(p, lambda e, s1b=s1b, mk=mk: e.tensor_add(
            self.fbank(s1b)[:], self.fbank(s1b)[:], mk[:, 512:1024]),
               inc=True)
        self.mkr.release(jm, self.dv)
        self.v(p, lambda e, s0b=s0b, bm2=bm2: e.reduce_max(
            bm2[:, 0:1], self.fbank(s0b)[:], axis=AX.X))
        self.v(p, lambda e, s1b=s1b, bm2=bm2: e.reduce_max(
            bm2[:, 1:2], self.fbank(s1b)[:], axis=AX.X))
        self.v(p, lambda e, bm=bm, bm2=bm2: e.reduce_max(bm, bm2, axis=AX.X))
        if first:
            self.ve(p, lambda e, mr=mr, bm=bm: e.tensor_copy(mr, bm))
        else:
            self.ve(p, lambda e, dold=dold, mr=mr, bm=bm:
                    e.tensor_sub(dold, mr, bm))
            self.ve(p, lambda e, dold=dold: e.tensor_scalar(
                dold, dold, self.cc(C_ZERO), None, ALU.min))
            self.ve(p, lambda e, mr=mr, bm=bm: e.tensor_max(mr, mr, bm))
        self.ve(p, lambda e, negmn=negmn, mr=mr: e.tensor_scalar(
            negmn, mr, self.cc(C_NEG1), None, ALU.mult))
        p.wait("scalar", self.dv.at())
        pvm = self.pv_mark.get(par)
        if pvm is not None:
            p.wait("scalar", pvm)
        if not first:
            self.a(p, lambda e, alpha=alpha, dold=dold:
                   e.activation(alpha, dold, AF.Exp))
        self.a(p, lambda e, s0b=s0b, negmn=negmn, lb=lb, Pv=Pv: e.activation(
            Pv[:, 0:512], self.fbank(s0b)[:], AF.Exp,
            bias=negmn, accum_out=lb[:, 0:1]))
        self.a(p, lambda e, s1b=s1b, negmn=negmn, lb=lb, Pv=Pv: e.activation(
            Pv[:, 512:1024], self.fbank(s1b)[:], AF.Exp,
            bias=negmn, accum_out=lb[:, 1:2]), inc=True)
        self.exp_mark[par] = self.ac.at()

    def flash_pv(self, p, i, c, first, jp, s0b, s1b, mkh, qk_pe):
        par = i % 2
        alpha = self.tsc[:, 4 + par:5 + par]
        lbs = self.tsc[:, 6 + par:7 + par]
        lb = self.tsc[:, 16 + 2 * par:18 + 2 * par]
        lc = self.lbuf[:, i:i + 1]
        oc = self.O[:, i * D:(i + 1) * D]
        Pv = self.Psb[:, par * KC:(par + 1) * KC]
        exp_m = self.exp_mark[par]
        p.wait("vector", exp_m)
        self.v(p, lambda e, lbs=lbs, lb=lb: e.reduce_sum(lbs, lb, axis=AX.X))
        if first:
            self.ve(p, lambda e, lc=lc, lbs=lbs: e.tensor_copy(lc, lbs))
        else:
            self.ve(p, lambda e, lc=lc, alpha=alpha, lbs=lbs:
                    e.scalar_tensor_tensor(lc, lc, alpha, lbs,
                                           ALU.mult, ALU.add))
            self.v(p, lambda e, oc=oc, alpha=alpha: e.tensor_scalar(
                oc[:, 0:512], oc[:, 0:512], alpha, None, ALU.mult))
            self.v(p, lambda e, oc=oc, alpha=alpha: e.tensor_scalar(
                oc[:, 512:1024], oc[:, 512:1024], alpha, None, ALU.mult),
                   inc=True)

        copy_marks = [None] * 8
        tsl = [None] * 8

        def emit_t(k):
            jt, q = self.tslot.take(p, "tensor")
            tsl[k] = jt
            sub = self.pt2[q][:, 0:128]
            p.wait("tensor", exp_m)
            self.t(p, lambda e, k=k, sub=sub, Pv=Pv:
                   e.transpose(sub, Pv[:, k * T:(k + 1) * T],
                               self.idn[:]), inc=True)
            p.wait("scalar", self.pe.at())
            self.a(p, lambda e, k=k, sub=sub, Pv=Pv:
                   e.copy(Pv[:, k * T:(k + 1) * T], sub), inc=True)
            self.tslot.release(tsl[k], self.ac)
            copy_marks[k] = self.ac.at()

        emit_t(0)
        emit_t(1)
        for k in range(8):
            if k + 2 < 8:
                emit_t(k + 2)
            p.wait("tensor", copy_marks[k])
            self.t(p, lambda e, k=k, s0b=s0b, Pv=Pv:
                   e.matmul(self.fbank(s0b)[:], Pv[:, k * T:(k + 1) * T],
                            self.Vsb[:, k * D:k * D + 512],
                            start=(k == 0), stop=(k == 7)))
            self.t(p, lambda e, k=k, s1b=s1b, Pv=Pv:
                   e.matmul(self.fbank(s1b)[:], Pv[:, k * T:(k + 1) * T],
                            self.Vsb[:, k * D + 512:k * D + 1024],
                            start=(k == 0), stop=(k == 7)),
                   inc=(k == 7))
        self.pv_mark[par] = self.pe.at()
        p.wait("vector", self.pe.at())
        if first:
            self.v(p, lambda e, s0b=s0b, oc=oc:
                   e.tensor_copy(oc[:, 0:512], self.fbank(s0b)[:]))
            self.v(p, lambda e, s1b=s1b, oc=oc:
                   e.tensor_copy(oc[:, 512:1024], self.fbank(s1b)[:]),
                   inc=True)
        else:
            self.v(p, lambda e, s0b=s0b, oc=oc:
                   e.tensor_add(oc[:, 0:512], oc[:, 0:512],
                                self.fbank(s0b)[:]))
            self.v(p, lambda e, s1b=s1b, oc=oc:
                   e.tensor_add(oc[:, 512:1024], oc[:, 512:1024],
                                self.fbank(s1b)[:]), inc=True)
        self.fpair.release(jp, self.dv)

    def p3(self):
        p = Ph("p3")
        self.ve(p, lambda e: e.tensor_scalar(self.lbuf[:], self.lbuf[:],
                                             self.cc(C_TINY), None, ALU.max))
        self.ve(p, lambda e: e.reciprocal(self.lbuf[:], self.lbuf[:]))
        if self.dbg:
            p.wait("sync", self.dv.at())
            self.dma(p, "sync", self.so, self.d_l, self.lbuf[:])
            self.dma(p, "sync", self.so, self.d_mr, self.mrun[:])
        for i in range(self.NT0):
            j, xo, xsem = self.xo2r.next(p, "sync")
            self.dma(p, "sync", xsem, xo[:], self.x_own[i * T:(i + 1) * T, :])
            p.wait("vector", xsem.at())
            self.v(p, lambda e, xo=xo, i=i: e.tensor_scalar(
                self.xsc[:], xo[:], self.r_own[:, i:i + 1],
                self.nm_own[:, i:i + 1], ALU.mult, ALU.add))
            self.v(p, lambda e, i=i: e.scalar_tensor_tensor(
                self.s0[:, i * D:(i + 1) * D], self.O[:, i * D:(i + 1) * D],
                self.lbuf[:, i:i + 1], self.xsc[:], ALU.mult, ALU.add),
                   inc=True)
            self.xo2r.release(j, self.dv)
        if self.dbg:
            p.wait("sync", self.dv.at())
            self.dma(p, "sync", self.so, self.d_s0, self.s0[:])
        p.emit(self.nc)

    def p4_main(self, p):
        R = self.rws
        c1 = lambda col: self.cst[0:1, col:col + 1]
        self.ve(p, lambda e: e.tensor_scalar(self.rtmp[:], self.rsum[:],
                                             self.cc(C_8192), None, ALU.mult))
        self.ve(p, lambda e: e.tensor_copy(self.meanbf[:], self.rtmp[:]))
        p.wait("tensor", self.dv.at())
        jpr, prb_r = self.fpair.take(p, "tensor")
        rps = self.fbank(prb_r[0])
        for ch in range(DCH):
            self.t(p, lambda e, ch=ch, rps=rps: e.matmul(
                rps[0:1, 0:E], self.meanbf[:, ch:ch + 1],
                self.wrs[:, ch * E:(ch + 1) * E],
                start=(ch == 0), stop=False))
        self.t(p, lambda e, rps=rps: e.matmul(rps[0:1, 0:E],
                                              self.on1[0:1, 0:1],
                                              self.brs[0:1, :],
                                              start=False, stop=True),
               inc=True)
        p.wait("vector", self.pe.at())
        self.ve(p, lambda e, rps=rps: e.reduce_max(R[:, 24:25],
                                                   rps[0:1, 0:E],
                                                   axis=AX.X))
        self.ve(p, lambda e: e.tensor_scalar(R[:, 25:26], R[:, 24:25],
                                             c1(C_NEG1), None, ALU.mult))
        p.wait("scalar", self.dv.at())
        self.a(p, lambda e, rps=rps: e.activation(self.rw[:, 0:E],
                                                  rps[0:1, 0:E],
                                                  AF.Exp, bias=R[:, 25:26],
                                                  accum_out=R[:, 26:27]),
               inc=True)
        self.fpair.release(jpr, self.ac)
        p.wait("vector", self.ac.at())
        self.ve(p, lambda e: e.reciprocal(R[:, 27:28], R[:, 26:27]))
        self.ve(p, lambda e: e.tensor_scalar(self.rw[:, 0:E], self.rw[:, 0:E],
                                             R[:, 27:28], None, ALU.mult))
        self.ve(p, lambda e: e.reduce_max(R[:, 24:25], self.rw[:, 0:E],
                                          axis=AX.X))
        self.ve(p, lambda e: e.tensor_scalar(R[:, 0:E], self.rw[:, 0:E],
                                             R[:, 24:25], None, ALU.is_ge))
        self.ve(p, lambda e: e.scalar_tensor_tensor(R[:, 8:16], R[:, 0:E],
                                                    c1(C_NEGBIG),
                                                    self.rw[:, 0:E],
                                                    ALU.mult, ALU.add))
        self.ve(p, lambda e: e.reduce_max(R[:, 25:26], R[:, 8:16], axis=AX.X))
        self.ve(p, lambda e: e.tensor_scalar(R[:, 16:24], R[:, 8:16],
                                             R[:, 25:26], None, ALU.is_ge))
        self.ve(p, lambda e: e.tensor_sub(R[:, 26:27], R[:, 25:26],
                                          R[:, 24:25]))
        p.wait("scalar", self.dv.at())
        self.a(p, lambda e: e.activation(R[:, 27:28], R[:, 26:27], AF.Exp),
               inc=True)
        p.wait("vector", self.ac.at())
        self.ve(p, lambda e: e.tensor_scalar(R[:, 28:29], R[:, 27:28],
                                             c1(C_ONE), None, ALU.add))
        self.ve(p, lambda e: e.reciprocal(R[:, 29:30], R[:, 28:29]))
        self.ve(p, lambda e: e.tensor_mul(R[:, 30:31], R[:, 27:28],
                                          R[:, 29:30]))
        self.ve(p, lambda e: e.tensor_scalar(self.rw[:, 8:16], R[:, 0:E],
                                             R[:, 29:30], None, ALU.mult))
        self.ve(p, lambda e: e.scalar_tensor_tensor(self.rw[:, 8:16],
                                                    R[:, 16:24], R[:, 30:31],
                                                    self.rw[:, 8:16],
                                                    ALU.mult, ALU.add))
        R2 = self.rws
        c1r = lambda col: self.cst[0:1, col:col + 1]
        EO = self.eoff
        self.ve(p, lambda e: e.tensor_copy(R2[:, 26:27], R2[:, 29:30]))
        self.ve(p, lambda e: e.tensor_copy(R2[:, 27:28], R2[:, 30:31]))
        self.ve(p, lambda e: e.tensor_mul(R2[:, 0:E], R2[:, 0:E],
                                          self.io8[0:1, 0:E]))
        self.ve(p, lambda e: e.reduce_sum(R2[:, 24:25], R2[:, 0:E],
                                          axis=AX.X))
        self.ve(p, lambda e: e.tensor_mul(R2[:, 16:24], R2[:, 16:24],
                                          self.io8[0:1, 0:E]))
        self.ve(p, lambda e: e.reduce_sum(R2[:, 25:26], R2[:, 16:24],
                                          axis=AX.X))
        self.ve(p, lambda e: e.tensor_scalar(EO[:, 0:2], R2[:, 24:26],
                                             c1r(C_32), None, ALU.mult))
        self.ve(p, lambda e: e.tensor_scalar(R2[:, 12:14], R2[:, 24:26],
                                             c1r(C_DFF), None, ALU.mult))
        self.ve(p, lambda e: e.tensor_scalar(EO[:, 2:10], self.io8[0:1, 0:E],
                                             c1r(C_512), None, ALU.mult))
        self.ve(p, lambda e: e.tensor_scalar(EO[:, 10:18], EO[:, 2:10],
                                             R2[:, 13:14], None, ALU.add))
        self.ve(p, lambda e: e.tensor_scalar(EO[:, 2:10], EO[:, 2:10],
                                             R2[:, 12:13], None, ALU.add))
        self.ve(p, lambda e: e.tensor_scalar(EO[:, 18:26], self.io8[0:1, 0:E],
                                             c1r(C_W2S), None, ALU.mult))
        self.ve(p, lambda e: e.tensor_scalar(EO[:, 26:34], EO[:, 18:26],
                                             R2[:, 13:14], None, ALU.add))
        self.ve(p, lambda e: e.tensor_scalar(EO[:, 18:26], EO[:, 18:26],
                                             R2[:, 12:13], None, ALU.add))
        self.ve(p, lambda e: e.tensor_copy(self.eidx[0:1, :], EO[0:1, :]))
        self.eidx_mark = self.dv.at()

        p.wait("gpsimd", self.dv.at())
        self.dma(p, "gpsimd", self.s_sc, self.scr2, R2[:, 26:28])
        p.wait("gpsimd", self.s_sc.at())
        self.dma(p, "gpsimd", self.s_sc, self.cwb2[:],
                 self.scr2.broadcast_to([T, 2]))
        self.dma(p, "gpsimd", self.s_sc, self.scr8, self.rw[:, 8:16])
        p.wait("gpsimd", self.s_sc.at())
        self.dma(p, "gpsimd", self.s_sc, self.cwb[:],
                 self.scr8.broadcast_to([T, E]))
        if self.dbg:
            self.dma(p, "sync", self.so, self.d_rw, self.rw[:])
            p.wait("gpsimd", self.s_sc.at())
            self.dma(p, "sync", self.so, self.d_cw, self.cwb[:])
        self.dma(p, "gpsimd", self.s_cw, self.cwcol[:],
                 self.scr8.rearrange("o e -> e o"), guard=self.s_sc.at())

    def p4(self):
        p = Ph("p4t")
        if self.B2S_LATE:
            self.dma(p, "gpsimd", self.s_cw, self.b2s[:], self.b2)
        p.wait("tensor", self.s_sc.at(), self.s_cw.at())
        jb0, b0 = self.bank.take(p, "tensor")
        jb1, b1 = self.bank.take(p, "tensor")
        self.t(p, lambda e, b0=b0: e.matmul(self.psb[b0][0:1, :],
                                            self.cwcol[:], self.b2s[:, 0:512],
                                            start=True, stop=True))
        self.t(p, lambda e, b1=b1: e.matmul(self.psb[b1][0:1, :],
                                            self.cwcol[:],
                                            self.b2s[:, 512:1024],
                                            start=True, stop=True), inc=True)
        p.wait("scalar", self.pe.at())
        self.a(p, lambda e, b0=b0: e.copy(self.bsum_b[:, 0:512],
                                          self.psb[b0][0:1, :]))
        self.a(p, lambda e, b1=b1: e.copy(self.bsum_b[:, 512:1024],
                                          self.psb[b1][0:1, :]), inc=True)
        self.bank.release(jb0, self.ac)
        self.bank.release(jb1, self.ac)
        p.wait("tensor", self.ac.at())
        jc0, cb0 = self.bank.take(p, "tensor")
        jc1, cb1 = self.bank.take(p, "tensor")
        self.t(p, lambda e, cb0=cb0: e.matmul(self.psb[cb0][:],
                                              self.on1[0:1, :],
                                              self.bsum_b[:, 0:512],
                                              start=True, stop=True))
        self.t(p, lambda e, cb1=cb1: e.matmul(self.psb[cb1][:],
                                              self.on1[0:1, :],
                                              self.bsum_b[:, 512:1024],
                                              start=True, stop=True), inc=True)
        p.wait("vector", self.pe.at())
        self.v(p, lambda e, cb0=cb0: e.tensor_copy(self.bsb[:, 0:512],
                                                   self.psb[cb0][:]))
        self.v(p, lambda e, cb1=cb1: e.tensor_copy(self.bsb[:, 512:1024],
                                                   self.psb[cb1][:]),
               inc=True)
        self.bank.release(jc0, self.dv)
        self.bank.release(jc1, self.dv)
        p.emit(self.nc)

    def p5(self):
        p = Ph("p5")
        p.wait("gpsimd", self.eidx_mark)
        w1v = self.w1.rearrange("e (c p) f -> e p c f", p=T, c=DCH)[0]
        w2v = self.w2.rearrange("e (c p) d -> e p c d", p=T, c=FF // T)[0]
        b1v = self.b1r
        for s_i in range(2):
            incf = self.s_ix.inc(16)
            def g_b1(e, s_i=s_i, incf=incf):
                v = e.value_load(self.eidx[0:1, s_i:s_i + 1])
                srcb = dataclasses.replace(b1v[:, 0:32], offset=v)
                incf(e.dma_start(out=self.b1s[:, s_i * 32:(s_i + 1) * 32],
                                 in_=srcb))
                e.free_register(v.val)
            p.op("gpsimd", g_b1)
        p.wait("scalar", self.s_ix.at())
        hpool = Pool(list(range(len(self.hsb))))
        for e_i in range(2):
            for w in range(NWIN):
                firstw = (e_i == 0 and w == 0)
                j1, w1s, w1sem = self.w1r.next(p, "gpsimd")
                incf1 = w1sem.inc(16)
                def g_w1(e, e_i=e_i, w=w, w1s=w1s, incf1=incf1):
                    v = e.value_load(
                        self.eidx[0:1, 2 + e_i * 8 + w:3 + e_i * 8 + w])
                    src1 = dataclasses.replace(w1v[:, :, 0:512], offset=v)
                    incf1(e.dma_start(
                        out=w1s[:].rearrange("p (c f) -> p c f", c=DCH),
                        in_=src1))
                    e.free_register(v.val)
                p.op("gpsimd", g_w1)
                j2, w2s, w2sem = self.w2r.next(p, "gpsimd")
                incf2 = w2sem.inc(16)
                def g_w2(e, e_i=e_i, w=w, w2s=w2s, incf2=incf2):
                    v = e.value_load(
                        self.eidx[0:1, 18 + e_i * 8 + w:19 + e_i * 8 + w])
                    src2 = dataclasses.replace(w2v[:, 0:4, :], offset=v)
                    incf2(e.dma_start(
                        out=w2s[:].rearrange("p (c d) -> p c d", c=4),
                        in_=src2))
                    e.free_register(v.val)
                p.op("gpsimd", g_w2)
                hslots = []
                p.wait("tensor", w1sem.at(), w2sem.at())
                twins = [(g, min(512, self.NTT - g))
                         for g in range(0, self.NTT, 512)]
                for fc in range(4):
                    jh, hi = hpool.take(p, "scalar")
                    hslots.append(hi)
                    hs = self.hsb[hi]
                    for g0, gn in twins:
                        jb, bi = self.bank.take(p, "tensor")
                        bk = self.psb[bi]
                        for dn in range(DCH):
                            self.t(p, lambda e, dn=dn, fc=fc, g0=g0, gn=gn,
                                   w1s=w1s, bk=bk:
                                   e.matmul(bk[:, 0:gn],
                                            w1s[:,
                                                dn * 512 + fc * T:
                                                dn * 512 + fc * T + T],
                                            self.xnfm[:,
                                                      dn * self.NTT + g0:
                                                      dn * self.NTT + g0 + gn],
                                            start=(dn == 0),
                                            stop=(dn == DCH - 1)),
                                   inc=(dn == DCH - 1))
                        p.wait("scalar", self.pe.at())
                        bcol = self.b1s[:, e_i * 32 + w * 4 + fc:
                                        e_i * 32 + w * 4 + fc + 1]
                        self.a(p, lambda e, bk=bk, hs=hs, g0=g0, gn=gn,
                               bcol=bcol:
                               e.activation(hs[:, g0:g0 + gn],
                                            bk[:, 0:gn], AF.Gelu, bias=bcol),
                               inc=True)
                        self.bank.release(jb, self.ac)
                for i in range(self.NT0):
                    jy0, y0 = self.bank.take(p, "tensor")
                    jy1, y1 = self.bank.take(p, "tensor")
                    p.wait("tensor", self.ac.at())
                    for fc in range(4):
                        hs = self.hsb[hslots[fc]]
                        self.t(p, lambda e, fc=fc, i=i, w2s=w2s, hs=hs, y0=y0:
                               e.matmul(self.psb[y0][:],
                                        hs[:, i * T:(i + 1) * T],
                                        w2s[:, fc * D:fc * D + 512],
                                        start=(fc == 0), stop=(fc == 3)))
                        self.t(p, lambda e, fc=fc, i=i, w2s=w2s, hs=hs, y1=y1:
                               e.matmul(self.psb[y1][:],
                                        hs[:, i * T:(i + 1) * T],
                                        w2s[:, fc * D + 512:
                                            fc * D + 1024],
                                        start=(fc == 0), stop=(fc == 3)),
                               inc=(fc == 3))
                    p.wait("vector", self.pe.at())
                    oc = self.O[:, i * D:(i + 1) * D]
                    cwc = self.cwb2[:, e_i:e_i + 1]
                    if firstw:
                        self.v(p, lambda e, y0=y0, oc=oc, cwc=cwc:
                               e.tensor_scalar(oc[:, 0:512], self.psb[y0][:],
                                               cwc, None, ALU.mult))
                        self.v(p, lambda e, y1=y1, oc=oc, cwc=cwc:
                               e.tensor_scalar(oc[:, 512:1024],
                                               self.psb[y1][:],
                                               cwc, None, ALU.mult), inc=True)
                    else:
                        self.v(p, lambda e, y0=y0, oc=oc, cwc=cwc:
                               e.scalar_tensor_tensor(oc[:, 0:512],
                                                      self.psb[y0][:], cwc,
                                                      oc[:, 0:512],
                                                      ALU.mult, ALU.add))
                        self.v(p, lambda e, y1=y1, oc=oc, cwc=cwc:
                               e.scalar_tensor_tensor(oc[:, 512:1024],
                                                      self.psb[y1][:], cwc,
                                                      oc[:, 512:1024],
                                                      ALU.mult, ALU.add),
                               inc=True)
                    self.bank.release(jy0, self.dv)
                    self.bank.release(jy1, self.dv)
                for hi in hslots:
                    hpool.release(hi, self.pe)
                self.w1r.release(j1, self.pe)
                self.w2r.release(j2, self.pe)
        if self.dbg:
            p.wait("sync", self.dv.at())
            self.dma(p, "sync", self.so, self.d_y[:, 0:NOWN * D // 2],
                     self.O[:, 0:NOWN * D // 2])
            self.dma(p, "sync", self.so, self.d_y[:, NOWN * D // 2:],
                     self.O[:, NOWN * D // 2:])
            p.wait("sync", self.so.at())
        p.emit(self.nc)

    def p6(self):
        p = Ph("p6")
        self.dma(p, "sync", self.s_gb, self.gm[:], self.gmb)
        self.dma(p, "sync", self.s_gb, self.bt[:], self.btb)
        p.wait("vector", self.s_gb.at())
        for i in range(self.NT0):
            j, xo, xsem = self.xo2r.next(p, "sync")
            self.dma(p, "sync", xsem, xo[:],
                     self.x_own[i * T:(i + 1) * T, :])
            p.wait("vector", xsem.at())
            oc = self.O[:, i * D:(i + 1) * D]
            self.v(p, lambda e, xo=xo, i=i: e.tensor_scalar(
                self.sqscr6[:], xo[:], self.r_own[:, i:i + 1],
                self.nm_own[:, i:i + 1], ALU.mult, ALU.add))
            self.v(p, lambda e: e.tensor_add(self.sqscr6[:], self.sqscr6[:],
                                             self.bsb[:]))
            self.v(p, lambda e, oc=oc: e.tensor_add(oc, oc, self.sqscr6[:]),
                   inc=True)
            self.xo2r.release(j, self.dv)
            self.v(p, lambda e, oc=oc, i=i: e.reduce_sum(
                self.sxs[:, i:i + 1], oc, axis=AX.X))
            self.v(p, lambda e, oc=oc: e.tensor_mul(self.sqscr6[:], oc, oc))
            self.v(p, lambda e, i=i: e.reduce_sum(self.sqs[:, i:i + 1],
                                                  self.sqscr6[:], axis=AX.X),
                   inc=True)
        for lo in range(0, self.NT0, 8):
            self.ln_chain(p, lo, min(8, self.NT0 - lo),
                          self.r_own, self.nm_own)
        for i in range(self.NT0):
            oc = self.O[:, i * D:(i + 1) * D]
            sc = self.s0[:, i * D:(i + 1) * D]
            self.v(p, lambda e, oc=oc, i=i: e.tensor_scalar(
                self.gscr[:], oc, self.r_own[:, i:i + 1],
                self.nm_own[:, i:i + 1], ALU.mult, ALU.add))
            self.v(p, lambda e: e.tensor_mul(self.gscr[:], self.gscr[:],
                                             self.gm[:]))
            self.v(p, lambda e: e.tensor_add(self.gscr[:], self.gscr[:],
                                             self.bt[:]), inc=True)
            p.wait("scalar", self.dv.at())
            self.a(p, lambda e: e.activation(self.sqscr6[:], self.gscr[:],
                                             AF.Silu), inc=True)
            p.wait("vector", self.ac.at())
            self.v(p, lambda e, sc=sc: e.tensor_mul(sc, self.sqscr6[:], sc),
                   inc=True)
            j, xo, xsem = self.xo2r.next(p, "sync")
            p.wait("sync", self.so.at())
            self.dma(p, "sync", xsem, xo[:],
                     self.x_own[i * T:(i + 1) * T, :])
            p.wait("vector", xsem.at())
            self.v(p, lambda e, xo=xo, sc=sc: e.tensor_add(xo[:], xo[:], sc),
                   inc=True)
            p.wait("sync", self.dv.at())
            self.dma(p, "sync", self.so, self.x1o[i * T:(i + 1) * T, :], xo[:])
        p.wait("sync", self.so.at())
        p.emit(self.nc)


def build_l0(dbg=False, upto=6):
    k = L0(dbg=dbg)
    nc = k.build(upto=upto)
    k.ctx.__exit__(None, None, None)
    return nc


def host_common(inputs):
    f32 = lambda a: np.ascontiguousarray(np.asarray(a, np.float32))
    d = {}
    d["wt"] = f32(np.asarray(inputs["L0_attn_W"], np.float32).T)
    d["w1"] = f32(inputs["L0_ffn_W1"])
    b1 = np.asarray(inputs["L0_ffn_b1"], np.float32)
    d["b1r"] = f32(b1.reshape(E, FF // T, T).transpose(2, 0, 1).reshape(T, -1))
    d["w2"] = f32(inputs["L0_ffn_W2"])
    d["b2"] = f32(inputs["L0_ffn_b2"])
    d["wr"] = f32(inputs["L0_ffn_Wr"])
    d["br"] = f32(np.asarray(inputs["L0_ffn_br"], np.float32).reshape(1, E))
    d["gmb"] = f32(np.tile(np.asarray(inputs["L0_ffn_gamma"],
                                      np.float32)[None, :], (T, 1)))
    d["btb"] = f32(np.tile(np.asarray(inputs["L0_ffn_beta"],
                                      np.float32)[None, :], (T, 1)))
    d["cst_in"] = f32(np.tile(np.array([CONST_VALS], np.float32), (T, 1)))
    d["iota8"] = np.arange(16, dtype=np.float32).reshape(1, 16)
    d["idn"] = np.eye(T, dtype=NPBF)
    d["on1"] = np.ones((1, T), dtype=NPBF)
    return d


def host_l0_inmaps(inputs):
    com = host_common(inputs)
    x0 = np.ascontiguousarray(np.asarray(inputs["L0_attn_x"], np.float32))
    qi = np.arange(T)
    ki = np.arange(KC)
    ci = np.arange(NCHUNK)
    ti = np.arange(NOWN)
    in_maps = []
    for core in range(8):
        b, k = divmod(core, 4)
        qg = (ti[:, None, None, None] + 16 * k) * T + qi[None, None, :, None]
        kg = ci[None, :, None, None] * KC + ki[None, None, None, :]
        cm = np.where(kg <= qg, np.float32(0), np.float32(-1e9)).astype(NPBF)
        m = dict(com)
        m["x_seq"] = x0[b]
        m["x_own"] = np.ascontiguousarray(x0[b, 2048 * k:2048 * (k + 1)])
        m["cmask"] = cm
        in_maps.append(m)
    return in_maps


NOH = 18
NOHT = NOH * T


class L1(L0):

    def __init__(self, dbg=False):
        self.dbg = dbg
        nc = bass.Bass("TRN2", target_bir_lowering=False)
        self.nc = nc
        self.ctx = contextlib.ExitStack()
        self.ctx.__enter__()

        inp = lambda n, s, d=F32: nc.dram_tensor(n, s, d, kind="ExternalInput").ap()
        outp = lambda n, s, d=F32: nc.dram_tensor(n, s, d, kind="ExternalOutput").ap()
        self.x1f = inp("x1f", [NOHT, D])
        self.bmsk = inp("bmsk", [NOWN, 3, T, T], BF16)
        self.wqkv = inp("wqkv", [D, 3 * D])
        self.bqkvr = inp("bqkvr", [T, 24])
        self.bqv = inp("bqv", [1, D])
        self.lnwb = inp("lnwb", [T, D])
        self.lnbb = inp("lnbb", [T, D])
        self.wg = inp("wg", [D, E])
        self.w1 = inp("w1", [E, D, FF])
        self.b1r = inp("b1r", [T, E * (FF // T)])
        self.w2 = inp("w2", [E, FF, D])
        self.b2 = inp("b2", [E, D])
        self.cst_in = inp("cst_in", [T, NCONST])
        self.idn_in = inp("idn", [T, T], BF16)
        self.on1_in = inp("on1", [1, T], BF16)
        self.x2o = outp("x2o", [NOT, D])
        if dbg:
            self.d_xnfm = outp("d_xnfm", [T, DCH * NOHT], BF16)
            self.d_gm = outp("d_gm", [T, NOWN * E])
            self.d_y = outp("d_y", [T, NOWN * D])
            self.d_s1 = outp("d_s1", [T, NOWN * D])
            self.d_att = outp("d_att", [T, NOWN * D])

        c = self.ctx
        sb = lambda n, s, d=F32: c.enter_context(nc.sbuf_tensor(n, s, d))
        pt = lambda n, s, d=F32: c.enter_context(nc.psum_tensor(n, s, d))
        self.cst = sb("cst", [T, NCONST])
        self.idn = sb("idn_s", [T, T], BF16)
        self.on1 = sb("on1_s", [1, T], BF16)
        self.b2row = [sb(f"b2row{i}", [1, D], BF16) for i in range(2)]
        self.wgs = sb("wgs", [T, DCH * E], BF16)
        self.bqvb = sb("bqvb", [1, D], BF16)
        self.bqr = sb("bqr", [T, 24])
        self.xnfm = sb("xnfm", [T, DCH * NOHT], BF16)
        self.r_oh = sb("r_oh", [T, NOH])
        self.nm_oh = sb("nm_oh", [T, NOH])
        self.gmask = sb("gmask", [T, NOWN * E])
        self.tsc = sb("tsc", [T, 20])
        self.tscl = sb("tscl", [T, 48])
        self.sxs = sb("sxs", [T, NOH])
        self.sqs = sb("sqs", [T, NOH])

        self.psb = [pt(f"psf{i}", [T, 512]) for i in range(5)]
        self.pt2 = [pt(f"pt2_{i}", [T, 1024], BF16) for i in range(2)]
        self.ps_r = pt("ps_r", [T, 512])

        self.s_c0 = Sem(nc, c, "s_c0")
        self.s_g0 = Sem(nc, c, "s_g0")
        self.s_b1 = Sem(nc, c, "s_b1")
        self.s_gb = Sem(nc, c, "s_gb")
        self.pe = Sem(nc, c, "pe")
        self.dv = Sem(nc, c, "dv")
        self.ac = Sem(nc, c, "ac")
        self.so = Sem(nc, c, "so")

        self.bank = Pool(list(range(5)))
        self.tslot = Pool(list(range(2)))
        self.b2r = Ring(nc, c, "rb2", self.b2row)

    def build(self, upto=9, q4upto=9):
        nc = self.nc
        self.q4upto = q4upto
        self.q0()
        with contextlib.ExitStack() as sA:
            sbA = lambda n, s, d=F32: self.sbt(sA, n, s, d)
            self.sqscr = sbA("sqscr", [T, D])
            self.xntm = sbA("xntm", [T, D], BF16)
            self.xs = [sbA(f"xs{i}", [T, D]) for i in range(2)]
            self.xsr = Ring(nc, self.ctx, "rxs", self.xs)
            self.g1 = sbA("g1", [T, NOWN * D], BF16)
            with contextlib.ExitStack() as sO:
                sbO = lambda n, s, d=F32: self.sbt(sO, n, s, d)
                self.O = sbO("O", [T, NOWN * D])
                self.q1()
                if upto < 2:
                    return nc
                with contextlib.ExitStack() as s5:
                    sb5 = lambda n, s, d=F32: self.sbt(s5, n, s, d)
                    self.b1s = sb5("b1s", [T, E * (FF // T)])
                    self.b2bb = sb5("b2bb", [T, D])
                    self.w1w = [sb5(f"w1w{i}", [T, DCH * 512], BF16)
                                for i in range(2)]
                    self.w2w = [sb5(f"w2w{i}", [T, 4 * D], BF16)
                                for i in range(2)]
                    self.w1r = Ring(nc, self.ctx, "rw1", self.w1w)
                    self.w2r = Ring(nc, self.ctx, "rw2", self.w2w)
                    self.hsb = [sb5(f"hsb{i}", [T, NOT], BF16)
                                for i in range(4)]
                    self.q2()
                if upto < 3:
                    return nc
                self.q3()
            if upto < 4:
                return nc
            with contextlib.ExitStack() as s4:
                sb4 = lambda n, s, d=F32: self.sbt(s4, n, s, d)
                self.kfm = sb4("kfm", [T, DCH * NOHT], BF16)
                self.vtm = sb4("vtm", [T, NOH * D], BF16)
                self.wqs = sb4("wqs", [T, DCH * D], BF16)
                self.bmr_sl = [sb4(f"bmsl{i}", [T, 3 * T], BF16)
                               for i in range(2)]
                self.bmr = Ring(nc, self.ctx, "rbm", self.bmr_sl)
                self.lnw = sb4("lnw", [T, D])
                self.lnb = sb4("lnb", [T, D])
                self.qf = sb4("qf", [T, 2 * DCH * T], BF16)
                self.sTs = sb4("sTs", [T, 3 * T], BF16)
                self.gscr = self.sqscr
                self.x1s = [sb4(f"x1s{i}", [T, D]) for i in range(2)]
                self.x1r = Ring(nc, self.ctx, "rx1", self.x1s)
                self.q4(q4upto=self.q4upto)
        return nc

    def q0(self):
        p = Ph("q0")
        for dst, srca in [(self.cst, self.cst_in), (self.idn, self.idn_in),
                          (self.on1, self.on1_in), (self.bqr, self.bqkvr)]:
            self.dma(p, "sync", self.s_c0, dst[:], srca)
        self.dma(p, "gpsimd", self.s_g0,
                 self.wgs[:].rearrange("p (c e) -> p c e", c=DCH),
                 self.wg.rearrange("(c p) e -> p c e", p=T, c=DCH))
        self.dma(p, "gpsimd", self.s_g0, self.bqvb[:], self.bqv)
        p.wait("vector", self.s_c0.at(), self.s_g0.at())
        p.wait("scalar", self.s_c0.at())
        p.wait("tensor", self.s_c0.at(), self.s_g0.at())
        p.emit(self.nc)

    def q1(self):
        p = Ph("q1")
        p.wait("sync", self.so.at())
        for i in range(NOH):
            j, xsl, xsem = self.xsr.next(p, "sync")
            self.dma(p, "sync", xsem, xsl[:],
                     self.x1f[i * T:(i + 1) * T, :])
            p.wait("vector", xsem.at())
            self.stats_tile(p, xsl[:], i)
            self.xsr.release(j, self.dv)
        self.ln_chain(p, 0, NOH, self.r_oh, self.nm_oh)
        for i in range(NOH):
            j, xsl, xsem = self.xsr.next(p, "sync")
            self.dma(p, "sync", xsem, xsl[:],
                     self.x1f[i * T:(i + 1) * T, :])
            p.wait("vector", xsem.at())
            self.fm_tile(p, xsl[:],
                         self.r_oh[:, i:i + 1], self.nm_oh[:, i:i + 1],
                         self.xnfm, lambda dch, i=i: dch * NOHT + i * T)
            self.xsr.release(j, self.dv)
        for i in range(NOWN):
            oi = i + 2
            p.wait("tensor", self.ac.at())
            jg, gbi = self.bank.take(p, "tensor")
            gb = self.psb[gbi]
            for dn in range(DCH):
                self.t(p, lambda e, dn=dn, oi=oi, gb=gb: e.matmul(
                    gb[:, 0:E],
                    self.xnfm[:, dn * NOHT + oi * T:dn * NOHT + oi * T + T],
                    self.wgs[:, dn * E:(dn + 1) * E],
                    start=(dn == 0), stop=(dn == DCH - 1)),
                    inc=(dn == DCH - 1))
            p.wait("vector", self.pe.at())
            gm = self.gmask[:, i * E:(i + 1) * E]
            self.v(p, lambda e, gb=gb: e.reduce_max(self.tsc[:, 0:1],
                                                    gb[:, 0:E], axis=AX.X))
            self.v(p, lambda e, gm=gm, gb=gb: e.tensor_scalar(
                gm, gb[:, 0:E], self.tsc[:, 0:1], None, ALU.is_ge))
            self.bank.release(jg, self.dv)
        if self.dbg:
            p.wait("sync", self.ac.at(), self.dv.at())
            self.dma(p, "sync", self.so, self.d_xnfm, self.xnfm[:])
            self.dma(p, "sync", self.so, self.d_gm, self.gmask[:])
            p.wait("sync", self.so.at())
        p.emit(self.nc)

    def q2(self):
        p = Ph("q2")
        self.dma(p, "sync", self.s_b1, self.b1s[:], self.b1r)
        b2pend = self.b2r.next(p, "gpsimd")
        self.dma(p, "gpsimd", b2pend[2], b2pend[1][:], self.b2[0:1, :])
        p.wait("scalar", self.s_b1.at())
        hpool = Pool(list(range(4)))
        for e_i in range(E):
            jb2, b2sl, b2sem = b2pend
            if e_i + 1 < E:
                b2pend = self.b2r.next(p, "gpsimd")
                self.dma(p, "gpsimd", b2pend[2], b2pend[1][:],
                         self.b2[e_i + 1:e_i + 2, :])
            jbb, bb = self.bank.take(p, "tensor")
            jbb1, bb1 = self.bank.take(p, "tensor")
            p.wait("tensor", b2sem.at())
            self.t(p, lambda e, bb=bb, b2sl=b2sl: e.matmul(
                self.psb[bb][:], self.on1[0:1, :],
                b2sl[0:1, 0:512], start=True, stop=True))
            self.t(p, lambda e, bb1=bb1, b2sl=b2sl: e.matmul(
                self.psb[bb1][:], self.on1[0:1, :],
                b2sl[0:1, 512:1024], start=True, stop=True), inc=True)
            self.b2r.release(jb2, self.pe)
            p.wait("vector", self.pe.at())
            self.v(p, lambda e, bb=bb: e.tensor_copy(self.b2bb[:, 0:512],
                                                     self.psb[bb][:]))
            self.v(p, lambda e, bb1=bb1: e.tensor_copy(self.b2bb[:, 512:1024],
                                                       self.psb[bb1][:]))
            self.bank.release(jbb, self.dv)
            self.bank.release(jbb1, self.dv)
            for w in range(NWIN):
                firstw = (e_i == 0 and w == 0)
                j1, w1s, w1sem = self.w1r.next(p, "gpsimd")
                self.dma(p, "gpsimd", w1sem,
                         w1s[:].rearrange("p (c f) -> p c f", c=DCH),
                         self.w1[e_i, :, w * 512:(w + 1) * 512].rearrange(
                             "(c p) f -> p c f", p=T, c=DCH))
                j2, w2s, w2sem = self.w2r.next(p, "gpsimd")
                self.dma(p, "gpsimd", w2sem,
                         w2s[:].rearrange("p (c d) -> p c d", c=4),
                         self.w2[e_i, w * 512:(w + 1) * 512, :].rearrange(
                             "(c p) d -> p c d", p=T, c=4))
                hslots = []
                p.wait("tensor", w1sem.at(), w2sem.at())
                for fc in range(4):
                    jh, hi = hpool.take(p, "scalar")
                    hslots.append(hi)
                    hs = self.hsb[hi]
                    for tg in range(4):
                        jb, bi = self.bank.take(p, "tensor")
                        bk = self.psb[bi]
                        for dn in range(DCH):
                            self.t(p, lambda e, dn=dn, fc=fc, tg=tg, w1s=w1s,
                                   bk=bk:
                                   e.matmul(bk[:],
                                            w1s[:, dn * 512 + fc * T:
                                                dn * 512 + fc * T + T],
                                            self.xnfm[:,
                                                      dn * NOHT + 2 * T +
                                                      tg * 512:
                                                      dn * NOHT + 2 * T +
                                                      tg * 512 + 512],
                                            start=(dn == 0),
                                            stop=(dn == DCH - 1)),
                                   inc=(dn == DCH - 1))
                        p.wait("scalar", self.pe.at())
                        bcol = self.b1s[:, e_i * 32 + w * 4 + fc:
                                        e_i * 32 + w * 4 + fc + 1]
                        self.a(p, lambda e, bk=bk, hs=hs, tg=tg, bcol=bcol:
                               e.activation(hs[:, tg * 512:(tg + 1) * 512],
                                            bk[:], AF.Gelu, bias=bcol),
                               inc=True)
                        self.bank.release(jb, self.ac)
                for i in range(NOWN):
                    jy0, y0 = self.bank.take(p, "tensor")
                    jy1, y1 = self.bank.take(p, "tensor")
                    p.wait("tensor", self.ac.at())
                    for fc in range(4):
                        hs = self.hsb[hslots[fc]]
                        self.t(p, lambda e, fc=fc, i=i, w2s=w2s, hs=hs, y0=y0:
                               e.matmul(self.psb[y0][:],
                                        hs[:, i * T:(i + 1) * T],
                                        w2s[:, fc * D:fc * D + 512],
                                        start=(fc == 0), stop=(fc == 3)))
                        self.t(p, lambda e, fc=fc, i=i, w2s=w2s, hs=hs, y1=y1:
                               e.matmul(self.psb[y1][:],
                                        hs[:, i * T:(i + 1) * T],
                                        w2s[:, fc * D + 512:fc * D + 1024],
                                        start=(fc == 0), stop=(fc == 3)),
                               inc=(fc == 3))
                    p.wait("vector", self.pe.at())
                    oc = self.O[:, i * D:(i + 1) * D]
                    gc = self.gmask[:, i * E + e_i:i * E + e_i + 1]
                    if firstw:
                        self.v(p, lambda e, y0=y0, oc=oc, gc=gc:
                               e.tensor_scalar(oc[:, 0:512], self.psb[y0][:],
                                               gc, None, ALU.mult))
                        self.v(p, lambda e, y1=y1, oc=oc, gc=gc:
                               e.tensor_scalar(oc[:, 512:1024],
                                               self.psb[y1][:],
                                               gc, None, ALU.mult))
                    else:
                        self.v(p, lambda e, y0=y0, oc=oc, gc=gc:
                               e.scalar_tensor_tensor(oc[:, 0:512],
                                                      self.psb[y0][:], gc,
                                                      oc[:, 0:512],
                                                      ALU.mult, ALU.add))
                        self.v(p, lambda e, y1=y1, oc=oc, gc=gc:
                               e.scalar_tensor_tensor(oc[:, 512:1024],
                                                      self.psb[y1][:], gc,
                                                      oc[:, 512:1024],
                                                      ALU.mult, ALU.add))
                    self.bank.release(jy0, self.dv)
                    self.bank.release(jy1, self.dv)
                for hi in hslots:
                    hpool.release(hi, self.pe)
                self.w1r.release(j1, self.pe)
                self.w2r.release(j2, self.pe)
            for i in range(NOWN):
                oc = self.O[:, i * D:(i + 1) * D]
                gc = self.gmask[:, i * E + e_i:i * E + e_i + 1]
                self.v(p, lambda e, oc=oc, gc=gc: e.scalar_tensor_tensor(
                    oc, self.b2bb[:], gc, oc, ALU.mult, ALU.add))
        if self.dbg:
            p.wait("sync", self.dv.at())
            self.dma(p, "sync", self.so, self.d_y[:, 0:NOWN * D // 2],
                     self.O[:, 0:NOWN * D // 2])
            self.dma(p, "sync", self.so, self.d_y[:, NOWN * D // 2:],
                     self.O[:, NOWN * D // 2:])
            p.wait("sync", self.so.at())
        p.emit(self.nc)

    def q3(self):
        p = Ph("q3")
        p.wait("scalar", self.dv.at())
        for i in range(NOWN):
            self.a(p, lambda e, i=i: e.activation(
                self.g1[:, i * D:(i + 1) * D], self.O[:, i * D:(i + 1) * D],
                AF.Silu))
        p.emit(self.nc)

    def q4(self, q4upto=9):
        nc = self.nc
        p = Ph("q4")
        self.dma(p, "sync", self.s_gb, self.lnw[:], self.lnwb)
        self.dma(p, "sync", self.s_gb, self.lnb[:], self.lnbb)
        p.wait("vector", self.s_gb.at())
        qp = Pool(list(range(6)))
        qp.marks = [self.bank.marks[j] for j in range(5)] + [self.dv.at()]
        qb = lambda j: self.psb[j] if j < 5 else self.ps_r
        self.dma(p, "gpsimd", self.s_g0,
                 self.wqs[:].rearrange("p (c n) -> p c n", c=DCH),
                 self.wqkv[:, D:2 * D].rearrange("(c p) n -> p c n", p=T,
                                                 c=DCH))
        p.wait("tensor", self.s_g0.at(), self.ac.at())
        wins = [(0, 512), (512, 512), (1024, 512), (1536, 512), (2048, 256)]
        for dk in range(DCH):
            for w0, wn in wins:
                jb, bi = qp.take(p, "tensor")
                bk = qb(bi)
                for dn in range(DCH):
                    self.t(p, lambda e, dn=dn, dk=dk, w0=w0, wn=wn, bk=bk:
                           e.matmul(bk[:, 0:wn],
                                    self.wqs[:, dn * D + dk * T:
                                             dn * D + dk * T + T],
                                    self.xnfm[:, dn * NOHT + w0:
                                              dn * NOHT + w0 + wn],
                                    start=(dn == 0), stop=(dn == DCH - 1)),
                           inc=(dn == DCH - 1))
                p.wait("vector", self.pe.at())
                bcol = self.bqr[:, 8 + dk:9 + dk]
                self.v(p, lambda e, bk=bk, dk=dk, w0=w0, wn=wn, bcol=bcol:
                       e.tensor_scalar(self.kfm[:, dk * NOHT + w0:
                                                dk * NOHT + w0 + wn],
                                       bk[:, 0:wn], bcol, None, ALU.add))
                qp.release(jb, self.dv)
        if q4upto < 2:
            p.emit(nc)
            return
        self.dma(p, "gpsimd", self.s_g0,
                 self.wqs[:].rearrange("p (c n) -> p c n", c=DCH),
                 self.wqkv[:, 2 * D:3 * D].rearrange("(c p) n -> p c n",
                                                     p=T, c=DCH),
                 guard=self.dv.at())
        p.wait("tensor", self.s_g0.at())
        for oi in range(NOH):
            jb0, b0i = qp.take(p, "tensor")
            jb1, b1i = qp.take(p, "tensor")
            b0, b1 = qb(b0i), qb(b1i)
            for dn in range(DCH):
                self.t(p, lambda e, dn=dn, oi=oi, b0=b0: e.matmul(
                    b0[:],
                    self.xnfm[:, dn * NOHT + oi * T:dn * NOHT + oi * T + T],
                    self.wqs[:, dn * D:dn * D + 512],
                    start=(dn == 0), stop=False))
                self.t(p, lambda e, dn=dn, oi=oi, b1=b1: e.matmul(
                    b1[:],
                    self.xnfm[:, dn * NOHT + oi * T:dn * NOHT + oi * T + T],
                    self.wqs[:, dn * D + 512:dn * D + 1024],
                    start=(dn == 0), stop=False))
            self.t(p, lambda e, b0=b0: e.matmul(
                b0[:], self.on1[0:1, :], self.bqvb[:, 0:512],
                start=False, stop=True))
            self.t(p, lambda e, b1=b1: e.matmul(
                b1[:], self.on1[0:1, :], self.bqvb[:, 512:1024],
                start=False, stop=True), inc=True)
            p.wait("vector", self.pe.at())
            self.v(p, lambda e, b0=b0, oi=oi: e.tensor_copy(
                self.vtm[:, oi * D:oi * D + 512], b0[:]))
            self.v(p, lambda e, b1=b1, oi=oi: e.tensor_copy(
                self.vtm[:, oi * D + 512:oi * D + 1024], b1[:]))
            qp.release(jb0, self.dv)
            qp.release(jb1, self.dv)
        if q4upto < 3:
            p.emit(nc)
            return
        self.dma(p, "gpsimd", self.s_g0,
                 self.wqs[:].rearrange("p (c n) -> p c n", c=DCH),
                 self.wqkv[:, 0:D].rearrange("(c p) n -> p c n", p=T, c=DCH),
                 guard=self.dv.at())
        p.wait("tensor", self.s_g0.at())

        def stage_a(i):
            oi = i + 2
            par = i % 2
            jbm, bmsl, bmsem = self.bmr.next(p, "sync")
            self.dma(p, "sync", bmsem,
                     bmsl[:].rearrange("p (o q) -> p o q", o=3),
                     self.bmsk[i].rearrange("o k q -> k o q"))
            qfp = self.qf[:, par * D:(par + 1) * D]
            for half in range(2):
                jb, bi = qp.take(p, "tensor")
                bk = qb(bi)
                for dq in range(4):
                    dqg = half * 4 + dq
                    for dn in range(DCH):
                        self.t(p, lambda e, dn=dn, dqg=dqg, dq=dq, oi=oi,
                               bk=bk:
                               e.matmul(bk[:, dq * T:(dq + 1) * T],
                                        self.wqs[:, dn * D + dqg * T:
                                                 dn * D + dqg * T + T],
                                        self.xnfm[:, dn * NOHT + oi * T:
                                                  dn * NOHT + oi * T + T],
                                        start=(dn == 0),
                                        stop=(dn == DCH - 1)),
                               inc=(dn == DCH - 1))
                p.wait("vector", self.pe.at())
                for dq in range(4):
                    dqg = half * 4 + dq
                    bcol = self.bqr[:, dqg:dqg + 1]
                    self.v(p, lambda e, bk=bk, dq=dq, dqg=dqg, bcol=bcol,
                           qfp=qfp:
                           e.tensor_scalar(qfp[:, dqg * T:(dqg + 1) * T],
                                           bk[:, dq * T:(dq + 1) * T],
                                           bcol, None, ALU.add))
                qp.release(jb, self.dv)
            return (i, jbm, bmsl, bmsem, self.dv.at(), qfp)

        def stage_bc(st):
            i, jbm, bmsl, bmsem, evA, qfp = st
            oi = i + 2
            for o in range(3):
                kt = oi - (2 - o)
                jb, bi = qp.take(p, "tensor")
                bk = qb(bi)
                if o == 0:
                    p.wait("tensor", evA)
                for dn in range(DCH):
                    self.t(p, lambda e, dn=dn, kt=kt, bk=bk, qfp=qfp:
                           e.matmul(
                        bk[:, 0:T],
                        self.kfm[:, dn * NOHT + kt * T:dn * NOHT + kt * T + T],
                        qfp[:, dn * T:(dn + 1) * T],
                        start=(dn == 0), stop=(dn == DCH - 1)),
                        inc=(dn == DCH - 1))
                p.wait("vector", self.pe.at(), bmsem.at())
                bmc = bmsl[:, o * T:o * T + T]
                self.v(p, lambda e, bk=bk, o=o, bmc=bmc: e.tensor_tensor(
                    self.sTs[:, o * T:(o + 1) * T], bk[:, 0:T], bmc,
                    ALU.mult))
                qp.release(jb, self.dv)
            self.bmr.release(jbm, self.dv)
            mm_mark = self.dv.at()
            ja0, a0i = qp.take(p, "tensor")
            ja1, a1i = qp.take(p, "tensor")
            a0b, a1b = qb(a0i), qb(a1i)
            p.wait("tensor", mm_mark)
            for o in range(3):
                kt = oi - (2 - o)
                self.t(p, lambda e, o=o, kt=kt, a0b=a0b: e.matmul(
                    a0b[:], self.sTs[:, o * T:(o + 1) * T],
                    self.vtm[:, kt * D:kt * D + 512],
                    start=(o == 0), stop=(o == 2)))
                self.t(p, lambda e, o=o, kt=kt, a1b=a1b: e.matmul(
                    a1b[:], self.sTs[:, o * T:(o + 1) * T],
                    self.vtm[:, kt * D + 512:kt * D + 1024],
                    start=(o == 0), stop=(o == 2)), inc=(o == 2))
            p.wait("vector", self.pe.at())
            sx = self.tsc[:, 0:1]
            sq = self.tsc[:, 1:2]
            t0 = self.tsc[:, 2:3]
            t1 = self.tsc[:, 3:4]
            self.v(p, lambda e, a0b=a0b: e.reduce_sum(t0, a0b[:], axis=AX.X))
            self.v(p, lambda e, a1b=a1b: e.reduce_sum(t1, a1b[:], axis=AX.X))
            self.ve(p, lambda e: e.tensor_add(sx, t0, t1))
            p.wait("scalar", self.dv.at())
            self.a(p, lambda e, a0b=a0b: e.activation(
                self.sqscr[:, 0:512], a0b[:], AF.Square))
            self.a(p, lambda e, a1b=a1b: e.activation(
                self.sqscr[:, 512:1024], a1b[:], AF.Square))
            p.wait("vector", self.ac.at())
            self.v(p, lambda e: e.reduce_sum(sq, self.sqscr[:], axis=AX.X))
            mu = self.tsc[:, 4:5]
            var = self.tsc[:, 5:6]
            rr = self.tsc[:, 6:7]
            nm = self.tsc[:, 7:8]
            self.ve(p, lambda e: e.tensor_scalar(mu, sx, self.cc(C_1024),
                                                 None, ALU.mult))
            self.ve(p, lambda e: e.tensor_scalar(var, sq, self.cc(C_1024),
                                                 self.cc(C_EPS), ALU.mult,
                                                 ALU.add))
            self.ve(p, lambda e: e.tensor_mul(t0, mu, mu))
            self.ve(p, lambda e: e.tensor_sub(var, var, t0))
            p.wait("scalar", self.dv.at())
            self.a(p, lambda e: e.activation(rr, var, AF.Sqrt))
            p.wait("vector", self.ac.at())
            self.ve(p, lambda e: e.reciprocal(rr, rr))
            self.ve(p, lambda e: e.tensor_mul(nm, mu, rr))
            self.ve(p, lambda e: e.tensor_scalar(nm, nm, self.cc(C_NEG1),
                                                 None, ALU.mult))
            self.v(p, lambda e, a0b=a0b: e.tensor_scalar(
                self.gscr[:, 0:512], a0b[:], rr, nm, ALU.mult, ALU.add))
            self.v(p, lambda e, a1b=a1b: e.tensor_scalar(
                self.gscr[:, 512:1024], a1b[:], rr, nm, ALU.mult, ALU.add))
            qp.release(ja0, self.dv)
            qp.release(ja1, self.dv)
            self.v(p, lambda e: e.tensor_mul(self.gscr[:], self.gscr[:],
                                             self.lnw[:]))
            self.v(p, lambda e: e.tensor_add(self.gscr[:], self.gscr[:],
                                             self.lnb[:]))
            jx, x1t, x1sem = self.x1r.next(p, "sync")
            p.wait("sync", self.so.at())
            self.dma(p, "sync", x1sem, x1t[:],
                     self.x1f[(i + 2) * T:(i + 3) * T, :])
            self.v(p, lambda e, i=i: e.tensor_mul(
                self.gscr[:], self.gscr[:], self.g1[:, i * D:(i + 1) * D]))
            p.wait("vector", x1sem.at())
            self.v(p, lambda e, x1t=x1t: e.tensor_add(x1t[:], x1t[:],
                                                      self.gscr[:]))
            p.wait("sync", self.dv.at())
            self.dma(p, "sync", self.so, self.x2o[i * T:(i + 1) * T, :],
                     x1t[:])

        st = stage_a(0)
        for i in range(NOWN):
            nxt = stage_a(i + 1) if i + 1 < NOWN else None
            stage_bc(st)
            st = nxt
        p.wait("sync", self.so.at())
        p.emit(nc)


def build_l1(dbg=False, upto=9, q4upto=9):
    k = L1(dbg=dbg)
    nc = k.build(upto=upto, q4upto=q4upto)
    k.ctx.__exit__(None, None, None)
    return nc


def host_l1_common(inputs):
    f32 = lambda a: np.ascontiguousarray(np.asarray(a, np.float32))
    d = {}
    d["wqkv"] = f32(inputs["L1_attn_W_qkv"])
    bq = np.asarray(inputs["L1_attn_b_qkv"], np.float32)
    d["bqkvr"] = f32(bq.reshape(24, T).T)
    d["bqv"] = f32(bq[2 * D:3 * D].reshape(1, D))
    d["lnwb"] = f32(np.tile(np.asarray(inputs["L1_attn_ln_w"],
                                       np.float32)[None, :], (T, 1)))
    d["lnbb"] = f32(np.tile(np.asarray(inputs["L1_attn_ln_b"],
                                       np.float32)[None, :], (T, 1)))
    d["wg"] = f32(np.asarray(inputs["L1_ffn_Wg"], np.float32).T)
    d["w1"] = f32(inputs["L1_ffn_W1"])
    b1 = np.asarray(inputs["L1_ffn_b1"], np.float32)
    d["b1r"] = f32(b1.reshape(E, FF // T, T).transpose(2, 0, 1).reshape(T, -1))
    d["w2"] = f32(inputs["L1_ffn_W2"])
    d["b2"] = f32(inputs["L1_ffn_b2"])
    d["cst_in"] = f32(np.tile(np.array([CONST_VALS], np.float32), (T, 1)))
    d["idn"] = np.eye(T, dtype=NPBF)
    d["on1"] = np.ones((1, T), dtype=NPBF)
    return d


def host_l1_inmaps(inputs, x1_full):
    com = host_l1_common(inputs)
    kk = np.arange(T)
    qq = np.arange(T)
    in_maps = []
    for core in range(8):
        b, k = divmod(core, 4)
        lo = 2048 * k - 256
        if lo < 0:
            halo = np.zeros((256, D), np.float32)
        else:
            halo = x1_full[b, lo:2048 * k]
        x1f = np.ascontiguousarray(
            np.concatenate([halo, x1_full[b, 2048 * k:2048 * (k + 1)]],
                           axis=0).astype(np.float32))
        bm = np.zeros((NOWN, 3, T, T), np.float32)
        for i in range(NOWN):
            gt = 16 * k + i
            for o in range(3):
                gk = gt - (2 - o)
                if gk < 0:
                    continue
                qg = gt * T + qq[None, :]
                kg = gk * T + kk[:, None]
                bm[i, o] = np.where((qg - 256 < kg) & (kg <= qg),
                                   np.float32(1.0 / 512.0), np.float32(0))
        m = dict(com)
        m["x1f"] = x1f
        m["bmsk"] = bm.astype(NPBF)
        in_maps.append(m)
    return in_maps



class LF(L1):

    B2S_LATE = True

    def __init__(self):
        self.dbg = False
        self.NT0 = NOH
        self.NTT = NOH * T
        nc = bass.Bass("TRN2", target_bir_lowering=False)
        self.nc = nc
        self.ctx = contextlib.ExitStack()
        self.ctx.__enter__()
        inp = lambda n, s, d=F32: nc.dram_tensor(n, s, d, kind="ExternalInput").ap()
        outp = lambda n, s, d=F32: nc.dram_tensor(n, s, d, kind="ExternalOutput").ap()
        self.x_seq = inp("x_seq", [S, D])
        self.x_oh = inp("x_oh", [NOH * T, D])
        self.cmask = inp("cmask", [NOH, NCHUNK, T, KC], BF16)
        self.wt = inp("wt", [D, 2 * D])
        self.w1a = inp("w1a", [E, D, FF])
        self.b1ra = inp("b1ra", [T, E * (FF // T)])
        self.w2a = inp("w2a", [E, FF, D])
        self.b2a = inp("b2a", [E, D])
        self.wr = inp("wr", [D, E])
        self.br = inp("br", [1, E])
        self.gmb = inp("gmb", [T, D])
        self.btb = inp("btb", [T, D])
        self.bmsk = inp("bmsk", [NOWN, 3, T, T], BF16)
        self.wqkv = inp("wqkv", [D, 3 * D])
        self.bqkvr = inp("bqkvr", [T, 24])
        self.bqv = inp("bqv", [1, D])
        self.lnwb = inp("lnwb", [T, D])
        self.lnbb = inp("lnbb", [T, D])
        self.wg = inp("wg", [D, E])
        self.w1b = inp("w1b", [E, D, FF])
        self.b1rb = inp("b1rb", [T, E * (FF // T)])
        self.w2b = inp("w2b", [E, FF, D])
        self.b2b = inp("b2b", [E, D])
        self.iota8 = inp("iota8", [1, 16])
        self.cst_in = inp("cst_in", [T, NCONST])
        self.idn_in = inp("idn", [T, T], BF16)
        self.on1_in = inp("on1", [1, T], BF16)
        self.x2o = outp("x2o", [NOT, D])
        self.x1scr = nc.dram_tensor("x1scr", [NOH * T, D], F32,
                                    kind="Internal").ap()
        self.scr8 = nc.dram_tensor("scr8", [1, E], F32, kind="Internal").ap()

        c = self.ctx
        sb = lambda n, s, d=F32: c.enter_context(nc.sbuf_tensor(n, s, d))
        pt = lambda n, s, d=F32: c.enter_context(nc.psum_tensor(n, s, d))
        self.cst = sb("cst", [T, NCONST])
        self.idn = sb("idn_s", [T, T], BF16)
        self.on1 = sb("on1_s", [1, T], BF16)
        self.tsc = sb("tsc", [T, 20])
        self.tscl = sb("tscl", [T, 48])
        self.sxs = sb("sxs", [T, 18])
        self.sqs = sb("sqs", [T, 18])
        self.psb = [pt(f"psf{i}", [T, 512]) for i in range(5)]
        self.pt2 = [pt(f"pt2_{i}", [T, 1024], BF16) for i in range(2)]
        self.ps_r = pt("ps_r", [T, 512])
        self.s_c0 = Sem(nc, c, "s_c0")
        self.s_g0 = Sem(nc, c, "s_g0")
        self.s_sc = Sem(nc, c, "s_sc")
        self.s_cw = Sem(nc, c, "s_cw")
        self.s_b1 = Sem(nc, c, "s_b1")
        self.s_gb = Sem(nc, c, "s_gb")
        self.s_ix = Sem(nc, c, "s_ix")
        self.pe = Sem(nc, c, "pe")
        self.dv = Sem(nc, c, "dv")
        self.ac = Sem(nc, c, "ac")
        self.so = Sem(nc, c, "so")
        self.bank = Pool(list(range(5)))
        self.tslot = Pool(list(range(2)))
        self.io8 = sb("io8", [1, 16])
        self.eidx = sb("eidx", [1, 34], mybir.dt.int32)
        self.eoff = sb("eoff", [1, 34])
        self.cwb2 = sb("cwb2", [T, 2])
        self.scr2 = nc.dram_tensor("scr2", [1, 2], F32, kind="Internal").ap()

    def build_fused(self):
        nc = self.nc
        self.x_own = self.x_oh
        self.x1o = self.x1scr
        self.w1, self.b1r = self.w1a, self.b1ra
        self.w2, self.b2 = self.w2a, self.b2a
        with contextlib.ExitStack() as lb:
            sbL = lambda n, s, d=F32: self.sbt(lb, n, s, d)
            self.wrs = sbL("wrs", [T, DCH * E], BF16)
            self.brs = sbL("brs", [1, E], BF16)
            self.xnfm = sbL("xnfm0", [T, DCH * self.NTT], BF16)
            self.r_own = sbL("r_own", [T, self.NT0])
            self.nm_own = sbL("nm_own", [T, self.NT0])
            self.r_kv = sbL("r_kv", [T, 16])
            self.nm_kv = sbL("nm_kv", [T, 16])
            self.O = sbL("O0", [T, self.NT0 * D])
            self.lbuf = sbL("lbuf", [T, self.NT0])
            self.mrun = sbL("mrun", [T, self.NT0])
            self.rsum = sbL("rsum", [T, DCH])
            self.rtmp = sbL("rtmp", [T, DCH])
            self.rw = sbL("rw", [1, 2 * E])
            self.rws = sbL("rws", [1, 32])
            self.meanbf = sbL("meanbf2", [T, DCH], BF16)
            self.cwb = sbL("cwb", [T, E])
            self.cwcol = sbL("cwcol", [E, 1], BF16)
            self.p0()
            with contextlib.ExitStack() as s12:
                sb2 = lambda n, s, d=F32: self.sbt(s12, n, s, d)
                self.sqscr = sb2("sqscr", [T, D])
                self.wkv = sb2("wkv", [T, DCH * 2 * D], BF16)
                self.Ksb = sb2("Ksb", [T, DCH * KC], BF16)
                self.Vsb = sb2("Vsb", [T, 8 * D], BF16)
                self.kvfm = sb2("kvfm", [T, DCH * 512], BF16)
                self.xs = [sb2(f"xs{i}", [T, D]) for i in range(2)]
                self.xntm = sb2("xntm", [T, D], BF16)
                self.mks = [sb2(f"mk{i}", [T, KC], BF16) for i in range(3)]
                self.Psb = sb2("Psb", [T, 2 * KC], BF16)
                self.xsr = Ring(nc, self.ctx, "rxs", self.xs)
                self.mkr = Ring(nc, self.ctx, "rmk", self.mks)
                self.p1()
                self.p2()
            with contextlib.ExitStack() as s36:
                sb3 = lambda n, s, d=F32: self.sbt(s36, n, s, d)
                self.s0 = sb3("s0", [T, self.NT0 * D], BF16)
                self.xo2 = [sb3(f"xo2{i}", [T, D]) for i in range(1)]
                self.xo2r = Ring(nc, self.ctx, "rxo", self.xo2)
                self.bsum_b = sb3("bsum_b", [1, D], BF16)
                self.bsb = sb3("bsb", [T, D])
                with contextlib.ExitStack() as s34:
                    sb34 = lambda n, s, d=F32: self.sbt(s34, n, s, d)
                    self.xsc = sb34("xsc", [T, D])
                    self.b2s = sb34("b2s", [E, D], BF16)
                    self.p3()
                    self.p4()
                with contextlib.ExitStack() as s5:
                    sb5 = lambda n, s, d=F32: self.sbt(s5, n, s, d)
                    self.b1s = sb5("b1s", [T, 2 * (FF // T)])
                    self.w1w = [sb5(f"w1w{i}", [T, DCH * 512], BF16)
                                for i in range(2)]
                    self.w2w = [sb5(f"w2w{i}", [T, 4 * D], BF16)
                                for i in range(2)]
                    self.w1r = Ring(nc, self.ctx, "rw1", self.w1w)
                    self.w2r = Ring(nc, self.ctx, "rw2", self.w2w)
                    self.hsb = [sb5(f"hsb{i}", [T, self.NTT], BF16)
                                for i in range(4)]
                    self.p5()
                with contextlib.ExitStack() as s6:
                    sb6 = lambda n, s, d=F32: self.sbt(s6, n, s, d)
                    self.sqscr6 = sb6("sqscr6", [T, D])
                    self.gscr = sb6("gscr", [T, D])
                    self.gm = sb6("gm", [T, D])
                    self.bt = sb6("bt", [T, D])
                    self.p6()
        self.NMPfx = "q_"
        self.x1f = self.x1scr
        self.w1, self.b1r = self.w1b, self.b1rb
        self.w2, self.b2 = self.w2b, self.b2b
        with contextlib.ExitStack() as l1b:
            sbM = lambda n, s, d=F32: self.sbt(l1b, n, s, d)
            self.b2row = [sbM(f"b2row{i}", [1, D], BF16) for i in range(2)]
            self.b2r = Ring(nc, self.ctx, "rb2", self.b2row)
            self.wgs = sbM("wgs", [T, DCH * E], BF16)
            self.bqvb = sbM("bqvb", [1, D], BF16)
            self.bqr = sbM("bqr", [T, 24])
            self.xnfm = sbM("xnfm1", [T, DCH * NOHT], BF16)
            self.r_oh = sbM("r_oh", [T, NOH])
            self.nm_oh = sbM("nm_oh", [T, NOH])
            self.gmask = sbM("gmask", [T, NOWN * E])
            self.q0()
            with contextlib.ExitStack() as sA:
                sbA = lambda n, s, d=F32: self.sbt(sA, n, s, d)
                self.sqscr = sbA("sqscr", [T, D])
                self.xntm = sbA("xntm", [T, D], BF16)
                self.xs = [sbA(f"xs{i}", [T, D]) for i in range(2)]
                self.xsr = Ring(nc, self.ctx, "rxs", self.xs)
                self.g1 = sbA("g1", [T, NOWN * D], BF16)
                with contextlib.ExitStack() as sO:
                    sbO = lambda n, s, d=F32: self.sbt(sO, n, s, d)
                    self.O = sbO("O", [T, NOWN * D])
                    self.q1()
                    with contextlib.ExitStack() as s5:
                        sb5 = lambda n, s, d=F32: self.sbt(s5, n, s, d)
                        self.b1s = sb5("b1s", [T, E * (FF // T)])
                        self.b2bb = sb5("b2bb", [T, D])
                        self.w1w = [sb5(f"w1w{i}", [T, DCH * 512], BF16)
                                    for i in range(2)]
                        self.w2w = [sb5(f"w2w{i}", [T, 4 * D], BF16)
                                    for i in range(2)]
                        self.w1r = Ring(nc, self.ctx, "rw1", self.w1w)
                        self.w2r = Ring(nc, self.ctx, "rw2", self.w2w)
                        self.hsb = [sb5(f"hsb{i}", [T, NOT], BF16)
                                    for i in range(4)]
                        self.q2()
                    self.q3()
                with contextlib.ExitStack() as s4:
                    sb4 = lambda n, s, d=F32: self.sbt(s4, n, s, d)
                    self.kfm = sb4("kfm", [T, DCH * NOHT], BF16)
                    self.vtm = sb4("vtm", [T, NOH * D], BF16)
                    self.wqs = sb4("wqs", [T, DCH * D], BF16)
                    self.bmr_sl = [sb4(f"bmsl{i}", [T, 3 * T], BF16)
                                   for i in range(2)]
                    self.bmr = Ring(nc, self.ctx, "rbm", self.bmr_sl)
                    self.lnw = sb4("lnw", [T, D])
                    self.lnb = sb4("lnb", [T, D])
                    self.qf = sb4("qf", [T, 2 * DCH * T], BF16)
                    self.sTs = sb4("sTs", [T, 3 * T], BF16)
                    self.gscr = self.sqscr
                    self.x1s = [sb4(f"x1s{i}", [T, D]) for i in range(2)]
                    self.x1r = Ring(nc, self.ctx, "rx1", self.x1s)
                    self.q4(q4upto=9)
        return nc


def build_fused():
    k = LF()
    nc = k.build_fused()
    k.ctx.__exit__(None, None, None)
    return nc


def host_fused_inmaps(inputs):
    com0 = host_common(inputs)
    com1 = host_l1_common(inputs)
    x0 = np.ascontiguousarray(np.asarray(inputs["L0_attn_x"], np.float32))
    qi = np.arange(T)
    ki = np.arange(KC)
    ci = np.arange(NCHUNK)
    ti = np.arange(NOH)
    kk = np.arange(T)
    qq = np.arange(T)
    in_maps = []
    for core in range(8):
        b, k = divmod(core, 4)
        lo = 2048 * k - 256
        halo = (np.zeros((256, D), np.float32) if lo < 0
                else x0[b, lo:2048 * k])
        x_oh = np.ascontiguousarray(np.concatenate(
            [halo, x0[b, 2048 * k:2048 * (k + 1)]], axis=0))
        qg = (ti[:, None, None, None] + 16 * k - 2) * T + qi[None, None, :, None]
        kg = ci[None, :, None, None] * KC + ki[None, None, None, :]
        cm = np.where((kg <= qg), np.float32(0), np.float32(-1e9)).astype(NPBF)
        bm = np.zeros((NOWN, 3, T, T), np.float32)
        for i in range(NOWN):
            gt = 16 * k + i
            for o in range(3):
                gk = gt - (2 - o)
                if gk < 0:
                    continue
                qg2 = gt * T + qq[None, :]
                kg2 = gk * T + kk[:, None]
                bm[i, o] = np.where((qg2 - 256 < kg2) & (kg2 <= qg2),
                                    np.float32(1.0 / 512.0), np.float32(0))
        m = {}
        m["x_seq"] = x0[b]
        m["x_oh"] = x_oh
        m["cmask"] = cm
        m["wt"] = com0["wt"]
        m["w1a"] = com0["w1"]
        m["b1ra"] = com0["b1r"]
        m["w2a"] = com0["w2"]
        m["b2a"] = com0["b2"]
        m["wr"] = com0["wr"]
        m["br"] = com0["br"]
        m["gmb"] = com0["gmb"]
        m["btb"] = com0["btb"]
        m["bmsk"] = bm.astype(NPBF)
        for key in ("wqkv", "bqkvr", "bqv", "lnwb", "lnbb", "wg"):
            m[key] = com1[key]
        m["w1b"] = com1["w1"]
        m["b1rb"] = com1["b1r"]
        m["w2b"] = com1["w2"]
        m["b2b"] = com1["b2"]
        m["cst_in"] = com0["cst_in"]
        m["iota8"] = com0["iota8"]
        m["idn"] = com0["idn"]
        m["on1"] = com0["on1"]
        in_maps.append(m)
    return in_maps


from concourse.bass_utils import run_bass_kernel_spmd

_NC_CACHE = {}


def _get_nc():
    if "f" not in _NC_CACHE:
        _NC_CACHE["f"] = build_fused()
    return _NC_CACHE["f"]


def kernel(**inputs):
    nc = _get_nc()
    im = host_fused_inmaps(inputs)
    res = run_bass_kernel_spmd(nc, im, core_ids=list(range(8)))
    out = np.empty((B, S, D), np.float32)
    for core in range(8):
        b, k = divmod(core, 4)
        out[b, 2048 * k:2048 * (k + 1)] = res.results[core]["x2o"]
    return out


# version 12
# speedup vs baseline: 1.0353x; 1.0353x over previous
import contextlib
import dataclasses
import numpy as np
import ml_dtypes
import concourse.bass as bass
import concourse.mybir as mybir

F32 = mybir.dt.float32
BF16 = mybir.dt.bfloat16
AF = mybir.ActivationFunctionType
ALU = mybir.AluOpType
AX = mybir.AxisListType
NPBF = ml_dtypes.bfloat16

B, S, D, FF, E = 2, 8192, 1024, 4096, 8
T = 128
DCH = D // T
NOWN = 16
NOT = NOWN * T
NCHUNK = 8
KC = 1024
NWIN = FF // 512

C_1024, C_EPS, C_NEG1, C_TINY, C_8192, C_ONE, C_NEGBIG, C_ZERO, C_32, C_512, C_DFF, C_W2S, C_256, C_W2H = range(14)
CONST_VALS = [1.0 / 1024, 1e-5, -1.0, 1e-30, 1.0 / 8192, 1.0, -1e9, 0.0,
              32.0, 512.0, 4194304.0, 524288.0, 256.0, 262144.0]
NCONST = len(CONST_VALS)

ENGS = ("sync", "gpsimd", "tensor", "vector", "scalar")


class Sem:
    def __init__(self, nc, ctx, name):
        self.h = ctx.enter_context(nc.semaphore(name))
        self.n = 0

    def inc(self, k=1):
        self.n += k
        return lambda inst, h=self.h, k=k: inst.then_inc(h, k)

    def at(self):
        return (self.h, self.n)


class Ph:
    def __init__(self, name):
        self.name = name
        self.ops = {e: [] for e in ENGS}

    def op(self, eng, fn):
        self.ops[eng].append(fn)

    def wait(self, eng, *marks):
        for m in marks:
            if m is None:
                continue
            h, v = m
            if v > 0:
                self.ops[eng].append(lambda e, h=h, v=v: e.wait_ge(h, v))

    def emit(self, nc):
        with nc.Block(self.name) as blk:
            for eng in ENGS:
                fns = self.ops[eng]
                if fns:
                    getattr(blk, eng)(lambda e, fns=fns: [f(e) for f in fns])


class Pool:
    def __init__(self, items):
        self.items = list(items)
        self.marks = [None] * len(self.items)
        self.i = 0

    def take(self, p, eng):
        j = self.i % len(self.items)
        self.i += 1
        p.wait(eng, self.marks[j])
        return j, self.items[j]

    def release(self, j, sem):
        self.marks[j] = sem.at()


_RING_N = [0]


class Ring:

    def __init__(self, nc, ctx, name, slots):
        _RING_N[0] += 1
        name = f"{name}_{_RING_N[0]}"
        self.slots = slots
        self.sems = [Sem(nc, ctx, f"{name}{i}") for i in range(len(slots))]
        self.marks = [None] * len(slots)
        self.i = 0

    def next(self, p, eng):
        j = self.i % len(self.slots)
        self.i += 1
        p.wait(eng, self.marks[j])
        return j, self.slots[j], self.sems[j]

    def release(self, j, sem):
        self.marks[j] = sem.at()


class L0:

    def __init__(self, dbg=False, nt0=NOWN):
        self.dbg = dbg
        self.NT0 = nt0
        self.NTT = nt0 * T
        nc = bass.Bass("TRN2", target_bir_lowering=False)
        self.nc = nc
        self.ctx = contextlib.ExitStack()
        self.ctx.__enter__()

        inp = lambda n, s, d=F32: nc.dram_tensor(n, s, d, kind="ExternalInput").ap()
        outp = lambda n, s, d=F32: nc.dram_tensor(n, s, d, kind="ExternalOutput").ap()
        self.x_seq = inp("x_seq", [S, D])
        self.x_own = inp("x_own", [self.NTT, D])
        self.cmask = inp("cmask", [self.NT0, NCHUNK, T, KC], BF16)
        self.wt = inp("wt", [D, 2 * D])
        self.w1 = inp("w1", [E, D, FF])
        self.b1r = inp("b1r", [T, E * (FF // T)])
        self.w2 = inp("w2", [E, FF, D])
        self.b2 = inp("b2", [E, D])
        self.wr = inp("wr", [D, E])
        self.br = inp("br", [1, E])
        self.gmb = inp("gmb", [T, D])
        self.btb = inp("btb", [T, D])
        self.iota8 = inp("iota8", [1, 16])
        self.cst_in = inp("cst_in", [T, NCONST])
        self.idn_in = inp("idn", [T, T], BF16)
        self.on1_in = inp("on1", [1, T], BF16)
        self.x1o = outp("x1o", [NOT, D])
        if dbg:
            self.d_l = outp("d_l", [T, NOWN])
            self.d_mr = outp("d_mr", [T, NOWN])
            self.d_s0 = outp("d_s0", [T, NOWN * D], BF16)
            self.d_cw = outp("d_cw", [T, E])
            self.d_rw = outp("d_rw", [1, 2 * E])
            self.d_y = outp("d_y", [T, NOWN * D])
            self.d_xnfm = outp("d_xnfm", [T, DCH * NOT], BF16)
        self.scr8 = nc.dram_tensor("scr8", [1, E], F32, kind="Internal").ap()

        c = self.ctx
        sb = lambda n, s, d=F32: c.enter_context(nc.sbuf_tensor(n, s, d))
        pt = lambda n, s, d=F32: c.enter_context(nc.psum_tensor(n, s, d))
        self.cst = sb("cst", [T, NCONST])
        self.idn = sb("idn_s", [T, T], BF16)
        self.on1 = sb("on1_s", [1, T], BF16)
        self.b2s = sb("b2s", [E, D], BF16)
        self.wrs = sb("wrs", [T, DCH * E], BF16)
        self.brs = sb("brs", [1, E], BF16)
        self.meanbf = sb("meanbf", [T, DCH], BF16)
        self.xnfm = sb("xnfm", [T, DCH * self.NTT], BF16)
        self.r_own = sb("r_own", [T, self.NT0])
        self.nm_own = sb("nm_own", [T, self.NT0])
        self.r_kv = sb("r_kv", [T, 16])
        self.nm_kv = sb("nm_kv", [T, 16])
        self.O = sb("O", [T, self.NT0 * D])
        self.lbuf = sb("lbuf", [T, self.NT0])
        self.mrun = sb("mrun", [T, self.NT0])
        self.rsum = sb("rsum", [T, DCH])
        self.rtmp = sb("rtmp", [T, DCH])
        self.tsc = sb("tsc", [T, 20])
        self.tscl = sb("tscl", [T, 48])
        self.sxs = sb("sxs", [T, 18])
        self.sqs = sb("sqs", [T, 18])
        self.cwb = sb("cwb", [T, E])
        self.rw = sb("rw", [1, 2 * E])
        self.rws = sb("rws", [1, 32])
        self.bsum_b = sb("bsum_b", [1, D], BF16)
        self.bsb = sb("bsb", [T, D])
        self.cw_bf = sb("cw_bf", [1, E], BF16)
        self.cwcol = sb("cwcol", [E, 1], BF16)
        self.io8 = sb("io8", [1, 16])
        self.eidx = sb("eidx", [1, 34], mybir.dt.int32)
        self.eoff = sb("eoff", [1, 34])
        self.cwb2 = sb("cwb2", [T, 2])
        self.scr2 = nc.dram_tensor("scr2", [1, 2], F32, kind="Internal").ap()

        self.psb = [pt(f"psf{i}", [T, 512]) for i in range(5)]
        self.pt2 = [pt(f"pt2_{i}", [T, 1024], BF16) for i in range(2)]
        self.ps_r = pt("ps_r", [T, 512])

        self.s_c0 = Sem(nc, c, "s_c0")
        self.s_g0 = Sem(nc, c, "s_g0")
        self.s_sc = Sem(nc, c, "s_sc")
        self.s_b1 = Sem(nc, c, "s_b1")
        self.s_gb = Sem(nc, c, "s_gb")
        self.s_cw = Sem(nc, c, "s_cw")
        self.s_ix = Sem(nc, c, "s_ix")
        self.pe = Sem(nc, c, "pe")
        self.dv = Sem(nc, c, "dv")
        self.ac = Sem(nc, c, "ac")
        self.so = Sem(nc, c, "so")

        self.bank = Pool(list(range(5)))
        self.tslot = Pool(list(range(2)))

    NMPfx = ""
    B2S_LATE = False

    def fbank(self, j):
        return self.psb[j] if j < 5 else self.ps_r

    def sbt(self, stack, n, s, d=F32):
        return stack.enter_context(
            self.nc.sbuf_tensor(self.NMPfx + n, s, d))

    def cc(self, col):
        return self.cst[:, col:col + 1]

    def dma(self, p, eng, sem, dst, src, guard=None, **kw):
        if guard is not None:
            p.wait(eng, guard)
        incf = sem.inc(16)
        p.op(eng, lambda e, dst=dst, src=src, incf=incf, kw=kw:
             incf(e.dma_start(out=dst, in_=src, **kw)))

    STRICT_EDGES = True

    def v(self, p, fn, inc=False):
        incf = self.dv.inc(1)
        p.op("vector", lambda e, fn=fn, incf=incf: incf(fn(e)))
        if self.STRICT_EDGES:
            h, vv = self.dv.at()
            p.op("vector", lambda e, h=h, vv=vv: e.wait_ge(h, vv))

    def vb(self, p, fn, inc=False):
        incf = self.dv.inc(1)
        p.op("vector", lambda e, fn=fn, incf=incf: incf(fn(e)))

    def ve(self, p, fn):
        incf = self.dv.inc(1)
        p.op("vector", lambda e, fn=fn, incf=incf: incf(fn(e)))
        h, vv = self.dv.at()
        p.op("vector", lambda e, h=h, vv=vv: e.wait_ge(h, vv))

    def a(self, p, fn, inc=False):
        incf = self.ac.inc(1)
        p.op("scalar", lambda e, fn=fn, incf=incf: incf(fn(e)))
        if self.STRICT_EDGES:
            h, vv = self.ac.at()
            p.op("scalar", lambda e, h=h, vv=vv: e.wait_ge(h, vv))

    def t(self, p, fn, inc=False):
        if inc:
            incf = self.pe.inc(1)
            p.op("tensor", lambda e, fn=fn, incf=incf: incf(fn(e)))
        else:
            p.op("tensor", fn)

    def ln_chain(self, p, lo, n, r_dst, nm_dst):
        sx = self.sxs[:, lo:lo + n]
        sq = self.sqs[:, lo:lo + n]
        mu = self.tscl[:, 0:n]
        var = self.tscl[:, 24:24 + n]
        r = r_dst[:, lo:lo + n]
        nm = nm_dst[:, lo:lo + n]
        self.ve(p, lambda e: e.tensor_scalar(mu, sx, self.cc(C_1024), None,
                                             ALU.mult))
        self.ve(p, lambda e: e.tensor_scalar(var, sq, self.cc(C_1024),
                                             self.cc(C_EPS), ALU.mult, ALU.add))
        self.ve(p, lambda e: e.tensor_mul(sq, mu, mu))
        self.ve(p, lambda e: e.tensor_sub(var, var, sq))
        p.wait("scalar", self.dv.at())
        self.a(p, lambda e: e.activation(r, var, AF.Sqrt), inc=True)
        p.wait("vector", self.ac.at())
        self.ve(p, lambda e: e.reciprocal(r, r))
        self.ve(p, lambda e: e.tensor_mul(nm, mu, r))
        self.ve(p, lambda e: e.tensor_scalar(nm, nm, self.cc(C_NEG1), None,
                                             ALU.mult))

    def stats_tile(self, p, xo, col):
        self.vb(p, lambda e: e.reduce_sum(self.sxs[:, col:col + 1], xo,
                                          axis=AX.X))
        self.vb(p, lambda e: e.tensor_mul(self.sqscr[:], xo, xo))
        self.v(p, lambda e: e.reduce_sum(self.sqs[:, col:col + 1],
                                         self.sqscr[:], axis=AX.X), inc=True)

    def fm_tile(self, p, xo, r_col, nm_col, fm_dst, col_fn):
        p.wait("vector", self.pe.at())
        self.v(p, lambda e, xo=xo, r_col=r_col, nm_col=nm_col:
               e.tensor_scalar(self.xntm[:], xo, r_col, nm_col,
                               ALU.mult, ALU.add), inc=True)
        for dch in range(DCH):
            jt, q = self.tslot.take(p, "tensor")
            sub = self.pt2[q][:, 0:128]
            p.wait("tensor", self.dv.at())
            self.t(p, lambda e, dch=dch, sub=sub:
                   e.transpose(sub, self.xntm[:, dch * T:(dch + 1) * T],
                               self.idn[:]), inc=True)
            p.wait("scalar", self.pe.at())
            co = col_fn(dch)
            self.a(p, lambda e, sub=sub, fm_dst=fm_dst, co=co:
                   e.copy(fm_dst[:, co:co + T], sub), inc=True)
            self.tslot.release(jt, self.ac)

    def build(self, upto=6):
        nc = self.nc
        self.upto = upto
        self.p0()
        with contextlib.ExitStack() as s12:
            sb2 = lambda n, s, d=F32: self.sbt(s12, n, s, d)
            self.sqscr = sb2("sqscr", [T, D])
            self.wkv = sb2("wkv", [T, DCH * 2 * D], BF16)
            self.Ksb = sb2("Ksb", [T, DCH * KC], BF16)
            self.Vsb = sb2("Vsb", [T, 8 * D], BF16)
            self.kvfm = sb2("kvfm", [T, DCH * 512], BF16)
            self.xs = [sb2(f"xs{i}", [T, D]) for i in range(2)]
            self.xntm = sb2("xntm", [T, D], BF16)
            self.mks = [sb2(f"mk{i}", [T, KC], BF16) for i in range(3)]
            self.Psb = sb2("Psb", [T, 2 * KC], BF16)
            self.xsr = Ring(nc, self.ctx, "rxs", self.xs)
            self.mkr = Ring(nc, self.ctx, "rmk", self.mks)
            self.p1()
            if upto >= 2:
                self.p2()
        if upto < 2:
            return nc
        with contextlib.ExitStack() as s36:
            sb3 = lambda n, s, d=F32: self.sbt(s36, n, s, d)
            self.s0 = sb3("s0", [T, self.NT0 * D], BF16)
            self.xo2 = [sb3(f"xo2{i}", [T, D]) for i in range(2)]
            self.xo2r = Ring(nc, self.ctx, "rxo", self.xo2)
            self.xsc = sb3("xsc", [T, D])
            if upto >= 3:
                self.p3()
            if upto >= 4:
                pm = Ph("p4m")
                self.p4_main(pm)
                pm.emit(nc)
                self.p4()
            if upto < 5:
                return nc
            with contextlib.ExitStack() as s5:
                sb5 = lambda n, s, d=F32: self.sbt(s5, n, s, d)
                self.b1s = sb5("b1s", [T, E * (FF // T)])
                self.w1w = [sb5(f"w1w{i}", [T, DCH * 512], BF16) for i in range(2)]
                self.w2w = [sb5(f"w2w{i}", [T, 4 * D], BF16) for i in range(2)]
                self.w1r = Ring(nc, self.ctx, "rw1", self.w1w)
                self.w2r = Ring(nc, self.ctx, "rw2", self.w2w)
                self.hsb = [sb5(f"hsb{i}", [T, self.NTT], BF16)
                            for i in range(5)]
                self.p5()
            if upto < 6:
                return nc
            with contextlib.ExitStack() as s6:
                sb6 = lambda n, s, d=F32: self.sbt(s6, n, s, d)
                self.sqscr6 = sb6("sqscr6", [T, D])
                self.gscr = sb6("gscr", [T, D])
                self.gm = sb6("gm", [T, D])
                self.bt = sb6("bt", [T, D])
                self.p6()
        return nc

    def p0(self):
        p = Ph("p0")
        for dst, srca in [(self.cst, self.cst_in), (self.idn, self.idn_in),
                          (self.on1, self.on1_in), (self.io8, self.iota8)]:
            self.dma(p, "sync", self.s_c0, dst[:], srca)
        if not self.B2S_LATE:
            self.dma(p, "gpsimd", self.s_g0, self.b2s[:], self.b2)
        self.dma(p, "gpsimd", self.s_g0,
                 self.wrs[:].rearrange("p (c e) -> p c e", c=DCH),
                 self.wr.rearrange("(c p) e -> p c e", p=T, c=DCH))
        self.dma(p, "gpsimd", self.s_g0, self.brs[:], self.br)
        p.wait("vector", self.s_c0.at(), self.s_g0.at())
        p.wait("scalar", self.s_c0.at())
        p.wait("tensor", self.s_c0.at(), self.s_g0.at())
        p.emit(self.nc)

    def p1(self):
        p = Ph("p1")
        for i in range(self.NT0):
            j, xsl, xsem = self.xsr.next(p, "sync")
            self.dma(p, "sync", xsem, xsl[:],
                     self.x_own[i * T:(i + 1) * T, :])
            p.wait("vector", xsem.at())
            self.stats_tile(p, xsl[:], i)
            self.xsr.release(j, self.dv)
        self.ln_chain(p, 0, self.NT0, self.r_own, self.nm_own)
        for i in range(self.NT0):
            j, xsl, xsem = self.xsr.next(p, "sync")
            self.dma(p, "sync", xsem, xsl[:],
                     self.x_own[i * T:(i + 1) * T, :])
            p.wait("vector", xsem.at())
            self.fm_tile(p, xsl[:],
                         self.r_own[:, i:i + 1], self.nm_own[:, i:i + 1],
                         self.xnfm,
                         lambda dch, i=i: dch * self.NTT + i * T)
            self.xsr.release(j, self.dv)
        if self.dbg:
            p.wait("sync", self.ac.at())
            self.dma(p, "sync", self.so, self.d_xnfm, self.xnfm[:])
            p.wait("sync", self.so.at())
        p.emit(self.nc)

    def p2(self):
        nc = self.nc
        p = Ph("p2w")
        self.dma(p, "gpsimd", self.s_g0,
                 self.wkv[:].rearrange("p (c n) -> p c n", c=DCH),
                 self.wt.rearrange("(c p) n -> p c n", p=T, c=DCH))
        p.emit(nc)

        self.fpair = Pool([(0, 1), (2, 3), (4, 5)])
        self.fm_pre = []
        self.pv_mark = {}
        self.exp_mark = {}
        chunks = list(reversed(range(NCHUNK)))
        p = Ph("p2s0")
        for pr in range(4):
            self.pro_loadstats_part(p, chunks[0], 0, pr)
        p.emit(nc)
        for ci, c in enumerate(chunks):
            first = ci == 0
            par = ci % 2
            p = Ph(f"p2c{c}")
            mkq = {}

            def issue_mask(ii, p=p, c=c, mkq=mkq):
                jm, mkb, msem = self.mkr.next(p, "sync")
                self.dma(p, "sync", msem, mkb[:], self.cmask[ii, c, :, :])
                mkq[ii] = (jm, mkb, msem)
            issue_mask(0)
            if self.NT0 > 1:
                issue_mask(1)
            if first:
                p.wait("tensor", self.s_g0.at(), self.ac.at())
            self.pro_kv(p, c, par)
            if ci == NCHUNK - 1:
                kvm = (self.dv.at(), self.ac.at())
                self.p4_main(p)
            else:
                kvm = (self.dv.at(), self.ac.at())
            smq = None
            pvq = None
            for i in range(self.NT0):
                cur = self.flash_qk(p, i, c, first, mkq.pop(i), kvm)
                if smq is not None:
                    self.flash_sm(p, *smq)
                if i + 2 < self.NT0:
                    issue_mask(i + 2)
                if smq is not None:
                    if pvq is not None:
                        self.flash_pv(p, *pvq)
                    pvq = smq
                smq = cur
                if ci + 1 < NCHUNK:
                    if 4 <= i < 8:
                        self.pro_loadstats_part(p, chunks[ci + 1],
                                                (ci + 1) % 2, i - 4)
                    elif 10 <= i < 12:
                        self.pro_fm_load(p, chunks[ci + 1], i - 10)
            self.flash_sm(p, *smq)
            self.flash_pv(p, *pvq)
            self.flash_pv(p, *smq)
            p.emit(nc)
        for j in range(5):
            self.bank.marks[j] = self.dv.at()

    def pro_loadstats_part(self, p, c, par, pr):
        for t2 in range(2):
            tt = pr * 2 + t2
            j, xsl, xsem = self.xsr.next(p, "sync")
            r0 = c * KC + tt * T
            self.dma(p, "sync", xsem, xsl[:], self.x_seq[r0:r0 + T, :])
            p.wait("vector", xsem.at())
            self.stats_tile(p, xsl[:], par * 8 + tt)
            self.xsr.release(j, self.dv)

    def pro_fm_load(self, p, c, tt):
        j, xsl, xsem = self.xsr.next(p, "sync")
        r0 = c * KC + tt * T
        self.dma(p, "sync", xsem, xsl[:], self.x_seq[r0:r0 + T, :])
        self.fm_pre.append((j, xsl, xsem))

    def pro_kv(self, p, c, par):
        self.ln_chain(p, par * 8, 8, self.r_kv, self.nm_kv)
        for pr in range(4):
            sub = pr // 2
            tin = pr % 2
            for t2 in range(2):
                tt = pr * 2 + t2
                if self.fm_pre:
                    j, xsl, xsem = self.fm_pre.pop(0)
                else:
                    j, xsl, xsem = self.xsr.next(p, "sync")
                    r0 = c * KC + tt * T
                    self.dma(p, "sync", xsem, xsl[:],
                             self.x_seq[r0:r0 + T, :])
                p.wait("vector", xsem.at())
                self.fm_tile(p, xsl[:],
                             self.r_kv[:, par * 8 + tt:par * 8 + tt + 1],
                             self.nm_kv[:, par * 8 + tt:par * 8 + tt + 1],
                             self.kvfm,
                             lambda dch, tin=tin, t2=t2:
                             dch * 512 + (tin * 2 + t2) * T)
                self.xsr.release(j, self.dv)
            if pr % 2 == 1:
                self.kv_proj(p, c, sub)

    def kv_proj(self, p, c, sub):
        p.wait("vector", self.ac.at())
        for dch in range(DCH):
            self.v(p, lambda e, dch=dch: e.reduce_sum(
                self.rtmp[:, dch:dch + 1],
                self.kvfm[:, dch * 512:(dch + 1) * 512], axis=AX.X))
        if c == NCHUNK - 1 and sub == 0:
            self.ve(p, lambda e: e.tensor_copy(self.rsum[:], self.rtmp[:]))
        else:
            self.ve(p, lambda e: e.tensor_add(self.rsum[:], self.rsum[:],
                                              self.rtmp[:]))
        p.wait("tensor", self.ac.at())
        jp, pr_b = self.fpair.take(p, "tensor")
        A, B = self.fbank(pr_b[0]), self.fbank(pr_b[1])
        mAB = [None, None]
        for dk in range(DCH):
            bk = A if dk % 2 == 0 else B
            if mAB[dk % 2] is not None:
                p.wait("tensor", mAB[dk % 2])
            for dn in range(DCH):
                self.t(p, lambda e, dn=dn, dk=dk, bk=bk:
                       e.matmul(bk[:],
                                self.wkv[:, dn * 2048 + dk * T:
                                          dn * 2048 + dk * T + T],
                                self.kvfm[:, dn * 512:(dn + 1) * 512],
                                start=(dn == 0), stop=(dn == DCH - 1)),
                       inc=(dn == DCH - 1))
            p.wait("vector", self.pe.at())
            self.v(p, lambda e, bk=bk, dk=dk:
                   e.tensor_copy(self.Ksb[:, dk * KC + sub * 512:
                                          dk * KC + sub * 512 + 512], bk[:]),
                   inc=True)
            mAB[dk % 2] = self.dv.at()
        for t4 in range(4):
            ks = sub * 4 + t4
            p.wait("tensor", mAB[0], mAB[1])
            for dn in range(DCH):
                self.t(p, lambda e, dn=dn, t4=t4, A=A:
                       e.matmul(A[:],
                                self.kvfm[:, dn * 512 + t4 * T:
                                          dn * 512 + t4 * T + T],
                                self.wkv[:, dn * 2048 + 1024:dn * 2048 + 1536],
                                start=(dn == 0), stop=(dn == DCH - 1)))
                self.t(p, lambda e, dn=dn, t4=t4, B=B:
                       e.matmul(B[:],
                                self.kvfm[:, dn * 512 + t4 * T:
                                          dn * 512 + t4 * T + T],
                                self.wkv[:, dn * 2048 + 1536:dn * 2048 + 2048],
                                start=(dn == 0), stop=(dn == DCH - 1)),
                       inc=(dn == DCH - 1))
            p.wait("vector", self.pe.at())
            self.v(p, lambda e, A=A, ks=ks:
                   e.tensor_copy(self.Vsb[:, ks * D:ks * D + 512],
                                 A[:]))
            self.v(p, lambda e, B=B, ks=ks:
                   e.tensor_copy(self.Vsb[:, ks * D + 512:ks * D + 1024],
                                 B[:]), inc=True)
            mAB[0] = mAB[1] = self.dv.at()
        self.fpair.release(jp, self.dv)

    def flash_qk(self, p, i, c, first, mkh, kvm):
        jp, pr_b = self.fpair.take(p, "tensor")
        s0b, s1b = pr_b
        if i == 0:
            p.wait("tensor", *kvm)
        for s2, bi in ((0, s0b), (1, s1b)):
            for dn in range(DCH):
                self.t(p, lambda e, dn=dn, s2=s2, bi=bi, i=i:
                       e.matmul(self.fbank(bi)[:],
                                self.xnfm[:, dn * self.NTT + i * T:
                                          dn * self.NTT + i * T + T],
                                self.Ksb[:, dn * KC + s2 * 512:
                                         dn * KC + s2 * 512 + 512],
                                start=(dn == 0), stop=(dn == DCH - 1)),
                       inc=(dn == DCH - 1))
        qk_pe = self.pe.at()
        return (i, c, first, jp, s0b, s1b, mkh, qk_pe)

    def flash_sm(self, p, i, c, first, jp, s0b, s1b, mkh, qk_pe):
        par = i % 2
        negmn = self.tsc[:, 0 + par:1 + par]
        dold = self.tsc[:, 2 + par:3 + par]
        alpha = self.tsc[:, 4 + par:5 + par]
        lbs = self.tsc[:, 6 + par:7 + par]
        bm = self.tsc[:, 8 + par:9 + par]
        bm2 = self.tsc[:, 10 + 2 * par:12 + 2 * par]
        lb = self.tsc[:, 16 + 2 * par:18 + 2 * par]
        mr = self.mrun[:, i:i + 1]
        lc = self.lbuf[:, i:i + 1]
        oc = self.O[:, i * D:(i + 1) * D]
        Pv = self.Psb[:, par * KC:(par + 1) * KC]
        jm, mk, msem = mkh

        p.wait("vector", msem.at(), qk_pe)
        self.vb(p, lambda e, s0b=s0b, mk=mk: e.tensor_add(
            self.fbank(s0b)[:], self.fbank(s0b)[:], mk[:, 0:512]))
        self.vb(p, lambda e, s1b=s1b, mk=mk: e.tensor_add(
            self.fbank(s1b)[:], self.fbank(s1b)[:], mk[:, 512:1024]),
                inc=True)
        self.mkr.release(jm, self.dv)
        self.vb(p, lambda e, s0b=s0b, bm2=bm2: e.reduce_max(
            bm2[:, 0:1], self.fbank(s0b)[:], axis=AX.X))
        self.vb(p, lambda e, s1b=s1b, bm2=bm2: e.reduce_max(
            bm2[:, 1:2], self.fbank(s1b)[:], axis=AX.X))
        self.v(p, lambda e, bm=bm, bm2=bm2: e.reduce_max(bm, bm2, axis=AX.X))
        if first:
            self.ve(p, lambda e, mr=mr, bm=bm: e.tensor_copy(mr, bm))
        else:
            self.ve(p, lambda e, dold=dold, mr=mr, bm=bm:
                    e.tensor_sub(dold, mr, bm))
            self.ve(p, lambda e, dold=dold: e.tensor_scalar(
                dold, dold, self.cc(C_ZERO), None, ALU.min))
            self.ve(p, lambda e, mr=mr, bm=bm: e.tensor_max(mr, mr, bm))
        self.ve(p, lambda e, negmn=negmn, mr=mr: e.tensor_scalar(
            negmn, mr, self.cc(C_NEG1), None, ALU.mult))
        p.wait("scalar", self.dv.at())
        pvm = self.pv_mark.get(par)
        if pvm is not None:
            p.wait("scalar", pvm)
        if not first:
            self.a(p, lambda e, alpha=alpha, dold=dold:
                   e.activation(alpha, dold, AF.Exp))
        self.a(p, lambda e, s0b=s0b, negmn=negmn, lb=lb, Pv=Pv: e.activation(
            Pv[:, 0:512], self.fbank(s0b)[:], AF.Exp,
            bias=negmn, accum_out=lb[:, 0:1]))
        self.a(p, lambda e, s1b=s1b, negmn=negmn, lb=lb, Pv=Pv: e.activation(
            Pv[:, 512:1024], self.fbank(s1b)[:], AF.Exp,
            bias=negmn, accum_out=lb[:, 1:2]), inc=True)
        self.exp_mark[par] = self.ac.at()

    def flash_pv(self, p, i, c, first, jp, s0b, s1b, mkh, qk_pe):
        par = i % 2
        alpha = self.tsc[:, 4 + par:5 + par]
        lbs = self.tsc[:, 6 + par:7 + par]
        lb = self.tsc[:, 16 + 2 * par:18 + 2 * par]
        lc = self.lbuf[:, i:i + 1]
        oc = self.O[:, i * D:(i + 1) * D]
        Pv = self.Psb[:, par * KC:(par + 1) * KC]
        exp_m = self.exp_mark[par]
        p.wait("vector", exp_m)
        self.v(p, lambda e, lbs=lbs, lb=lb: e.reduce_sum(lbs, lb, axis=AX.X))
        if first:
            self.ve(p, lambda e, lc=lc, lbs=lbs: e.tensor_copy(lc, lbs))
        else:
            self.ve(p, lambda e, lc=lc, alpha=alpha, lbs=lbs:
                    e.scalar_tensor_tensor(lc, lc, alpha, lbs,
                                           ALU.mult, ALU.add))
            self.vb(p, lambda e, oc=oc, alpha=alpha: e.tensor_scalar(
                oc[:, 0:512], oc[:, 0:512], alpha, None, ALU.mult))
            self.vb(p, lambda e, oc=oc, alpha=alpha: e.tensor_scalar(
                oc[:, 512:1024], oc[:, 512:1024], alpha, None, ALU.mult),
                    inc=True)

        copy_marks = [None] * 8
        tsl = [None] * 8

        def emit_t(k):
            jt, q = self.tslot.take(p, "tensor")
            tsl[k] = jt
            sub = self.pt2[q][:, 0:128]
            p.wait("tensor", exp_m)
            self.t(p, lambda e, k=k, sub=sub, Pv=Pv:
                   e.transpose(sub, Pv[:, k * T:(k + 1) * T],
                               self.idn[:]), inc=True)
            p.wait("scalar", self.pe.at())
            self.a(p, lambda e, k=k, sub=sub, Pv=Pv:
                   e.copy(Pv[:, k * T:(k + 1) * T], sub), inc=True)
            self.tslot.release(tsl[k], self.ac)
            copy_marks[k] = self.ac.at()

        emit_t(0)
        emit_t(1)
        for k in range(8):
            if k + 2 < 8:
                emit_t(k + 2)
            p.wait("tensor", copy_marks[k])
            self.t(p, lambda e, k=k, s0b=s0b, Pv=Pv:
                   e.matmul(self.fbank(s0b)[:], Pv[:, k * T:(k + 1) * T],
                            self.Vsb[:, k * D:k * D + 512],
                            start=(k == 0), stop=(k == 7)))
            self.t(p, lambda e, k=k, s1b=s1b, Pv=Pv:
                   e.matmul(self.fbank(s1b)[:], Pv[:, k * T:(k + 1) * T],
                            self.Vsb[:, k * D + 512:k * D + 1024],
                            start=(k == 0), stop=(k == 7)),
                   inc=(k == 7))
        self.pv_mark[par] = self.pe.at()
        p.wait("vector", self.pe.at())
        if first:
            self.vb(p, lambda e, s0b=s0b, oc=oc:
                    e.tensor_copy(oc[:, 0:512], self.fbank(s0b)[:]))
            self.vb(p, lambda e, s1b=s1b, oc=oc:
                    e.tensor_copy(oc[:, 512:1024], self.fbank(s1b)[:]),
                    inc=True)
        else:
            self.vb(p, lambda e, s0b=s0b, oc=oc:
                    e.tensor_add(oc[:, 0:512], oc[:, 0:512],
                                 self.fbank(s0b)[:]))
            self.vb(p, lambda e, s1b=s1b, oc=oc:
                    e.tensor_add(oc[:, 512:1024], oc[:, 512:1024],
                                 self.fbank(s1b)[:]), inc=True)
        self.fpair.release(jp, self.dv)

    def p3(self):
        p = Ph("p3")
        self.ve(p, lambda e: e.tensor_scalar(self.lbuf[:], self.lbuf[:],
                                             self.cc(C_TINY), None, ALU.max))
        self.ve(p, lambda e: e.reciprocal(self.lbuf[:], self.lbuf[:]))
        if self.dbg:
            p.wait("sync", self.dv.at())
            self.dma(p, "sync", self.so, self.d_l, self.lbuf[:])
            self.dma(p, "sync", self.so, self.d_mr, self.mrun[:])
        for i in range(self.NT0):
            j, xo, xsem = self.xo2r.next(p, "sync")
            self.dma(p, "sync", xsem, xo[:], self.x_own[i * T:(i + 1) * T, :])
            p.wait("vector", xsem.at())
            self.v(p, lambda e, xo=xo, i=i: e.tensor_scalar(
                self.xsc[:], xo[:], self.r_own[:, i:i + 1],
                self.nm_own[:, i:i + 1], ALU.mult, ALU.add))
            self.v(p, lambda e, i=i: e.scalar_tensor_tensor(
                self.s0[:, i * D:(i + 1) * D], self.O[:, i * D:(i + 1) * D],
                self.lbuf[:, i:i + 1], self.xsc[:], ALU.mult, ALU.add),
                   inc=True)
            self.xo2r.release(j, self.dv)
        if self.dbg:
            p.wait("sync", self.dv.at())
            self.dma(p, "sync", self.so, self.d_s0, self.s0[:])
        p.emit(self.nc)

    def p4_main(self, p):
        R = self.rws
        c1 = lambda col: self.cst[0:1, col:col + 1]
        self.ve(p, lambda e: e.tensor_scalar(self.rtmp[:], self.rsum[:],
                                             self.cc(C_8192), None, ALU.mult))
        self.ve(p, lambda e: e.tensor_copy(self.meanbf[:], self.rtmp[:]))
        p.wait("tensor", self.dv.at())
        jpr, prb_r = self.fpair.take(p, "tensor")
        rps = self.fbank(prb_r[0])
        for ch in range(DCH):
            self.t(p, lambda e, ch=ch, rps=rps: e.matmul(
                rps[0:1, 0:E], self.meanbf[:, ch:ch + 1],
                self.wrs[:, ch * E:(ch + 1) * E],
                start=(ch == 0), stop=False))
        self.t(p, lambda e, rps=rps: e.matmul(rps[0:1, 0:E],
                                              self.on1[0:1, 0:1],
                                              self.brs[0:1, :],
                                              start=False, stop=True),
               inc=True)
        p.wait("vector", self.pe.at())
        self.ve(p, lambda e, rps=rps: e.reduce_max(R[:, 24:25],
                                                   rps[0:1, 0:E],
                                                   axis=AX.X))
        self.ve(p, lambda e: e.tensor_scalar(R[:, 25:26], R[:, 24:25],
                                             c1(C_NEG1), None, ALU.mult))
        p.wait("scalar", self.dv.at())
        self.a(p, lambda e, rps=rps: e.activation(self.rw[:, 0:E],
                                                  rps[0:1, 0:E],
                                                  AF.Exp, bias=R[:, 25:26],
                                                  accum_out=R[:, 26:27]),
               inc=True)
        self.fpair.release(jpr, self.ac)
        p.wait("vector", self.ac.at())
        self.ve(p, lambda e: e.reciprocal(R[:, 27:28], R[:, 26:27]))
        self.ve(p, lambda e: e.tensor_scalar(self.rw[:, 0:E], self.rw[:, 0:E],
                                             R[:, 27:28], None, ALU.mult))
        self.ve(p, lambda e: e.reduce_max(R[:, 24:25], self.rw[:, 0:E],
                                          axis=AX.X))
        self.ve(p, lambda e: e.tensor_scalar(R[:, 0:E], self.rw[:, 0:E],
                                             R[:, 24:25], None, ALU.is_ge))
        self.ve(p, lambda e: e.scalar_tensor_tensor(R[:, 8:16], R[:, 0:E],
                                                    c1(C_NEGBIG),
                                                    self.rw[:, 0:E],
                                                    ALU.mult, ALU.add))
        self.ve(p, lambda e: e.reduce_max(R[:, 25:26], R[:, 8:16], axis=AX.X))
        self.ve(p, lambda e: e.tensor_scalar(R[:, 16:24], R[:, 8:16],
                                             R[:, 25:26], None, ALU.is_ge))
        self.ve(p, lambda e: e.tensor_sub(R[:, 26:27], R[:, 25:26],
                                          R[:, 24:25]))
        p.wait("scalar", self.dv.at())
        self.a(p, lambda e: e.activation(R[:, 27:28], R[:, 26:27], AF.Exp),
               inc=True)
        p.wait("vector", self.ac.at())
        self.ve(p, lambda e: e.tensor_scalar(R[:, 28:29], R[:, 27:28],
                                             c1(C_ONE), None, ALU.add))
        self.ve(p, lambda e: e.reciprocal(R[:, 29:30], R[:, 28:29]))
        self.ve(p, lambda e: e.tensor_mul(R[:, 30:31], R[:, 27:28],
                                          R[:, 29:30]))
        self.ve(p, lambda e: e.tensor_scalar(self.rw[:, 8:16], R[:, 0:E],
                                             R[:, 29:30], None, ALU.mult))
        self.ve(p, lambda e: e.scalar_tensor_tensor(self.rw[:, 8:16],
                                                    R[:, 16:24], R[:, 30:31],
                                                    self.rw[:, 8:16],
                                                    ALU.mult, ALU.add))
        R2 = self.rws
        c1r = lambda col: self.cst[0:1, col:col + 1]
        EO = self.eoff
        self.ve(p, lambda e: e.tensor_copy(R2[:, 26:27], R2[:, 29:30]))
        self.ve(p, lambda e: e.tensor_copy(R2[:, 27:28], R2[:, 30:31]))
        self.ve(p, lambda e: e.tensor_mul(R2[:, 0:E], R2[:, 0:E],
                                          self.io8[0:1, 0:E]))
        self.ve(p, lambda e: e.reduce_sum(R2[:, 24:25], R2[:, 0:E],
                                          axis=AX.X))
        self.ve(p, lambda e: e.tensor_mul(R2[:, 16:24], R2[:, 16:24],
                                          self.io8[0:1, 0:E]))
        self.ve(p, lambda e: e.reduce_sum(R2[:, 25:26], R2[:, 16:24],
                                          axis=AX.X))
        self.ve(p, lambda e: e.tensor_scalar(EO[:, 0:2], R2[:, 24:26],
                                             c1r(C_32), None, ALU.mult))
        self.ve(p, lambda e: e.tensor_scalar(R2[:, 12:14], R2[:, 24:26],
                                             c1r(C_DFF), None, ALU.mult))
        self.ve(p, lambda e: e.tensor_scalar(EO[:, 2:10], self.io8[0:1, 0:E],
                                             c1r(C_512), None, ALU.mult))
        self.ve(p, lambda e: e.tensor_scalar(EO[:, 10:18], EO[:, 2:10],
                                             R2[:, 13:14], None, ALU.add))
        self.ve(p, lambda e: e.tensor_scalar(EO[:, 2:10], EO[:, 2:10],
                                             R2[:, 12:13], None, ALU.add))
        self.ve(p, lambda e: e.tensor_scalar(EO[:, 18:26], self.io8[0:1, 0:E],
                                             c1r(C_W2S), None, ALU.mult))
        self.ve(p, lambda e: e.tensor_scalar(EO[:, 26:34], EO[:, 18:26],
                                             R2[:, 13:14], None, ALU.add))
        self.ve(p, lambda e: e.tensor_scalar(EO[:, 18:26], EO[:, 18:26],
                                             R2[:, 12:13], None, ALU.add))
        self.ve(p, lambda e: e.tensor_copy(self.eidx[0:1, :], EO[0:1, :]))
        self.eidx_mark = self.dv.at()

        p.wait("gpsimd", self.dv.at())
        self.dma(p, "gpsimd", self.s_sc, self.scr2, R2[:, 26:28])
        p.wait("gpsimd", self.s_sc.at())
        self.dma(p, "gpsimd", self.s_sc, self.cwb2[:],
                 self.scr2.broadcast_to([T, 2]))
        self.dma(p, "gpsimd", self.s_sc, self.scr8, self.rw[:, 8:16])
        p.wait("gpsimd", self.s_sc.at())
        self.dma(p, "gpsimd", self.s_sc, self.cwb[:],
                 self.scr8.broadcast_to([T, E]))
        if self.dbg:
            self.dma(p, "sync", self.so, self.d_rw, self.rw[:])
            p.wait("gpsimd", self.s_sc.at())
            self.dma(p, "sync", self.so, self.d_cw, self.cwb[:])
        self.dma(p, "gpsimd", self.s_cw, self.cwcol[:],
                 self.scr8.rearrange("o e -> e o"), guard=self.s_sc.at())

    def p4(self):
        p = Ph("p4t")
        if self.B2S_LATE:
            self.dma(p, "gpsimd", self.s_cw, self.b2s[:], self.b2)
        p.wait("tensor", self.s_sc.at(), self.s_cw.at())
        jb0, b0 = self.bank.take(p, "tensor")
        jb1, b1 = self.bank.take(p, "tensor")
        self.t(p, lambda e, b0=b0: e.matmul(self.psb[b0][0:1, :],
                                            self.cwcol[:], self.b2s[:, 0:512],
                                            start=True, stop=True))
        self.t(p, lambda e, b1=b1: e.matmul(self.psb[b1][0:1, :],
                                            self.cwcol[:],
                                            self.b2s[:, 512:1024],
                                            start=True, stop=True), inc=True)
        p.wait("scalar", self.pe.at())
        self.a(p, lambda e, b0=b0: e.copy(self.bsum_b[:, 0:512],
                                          self.psb[b0][0:1, :]))
        self.a(p, lambda e, b1=b1: e.copy(self.bsum_b[:, 512:1024],
                                          self.psb[b1][0:1, :]), inc=True)
        self.bank.release(jb0, self.ac)
        self.bank.release(jb1, self.ac)
        p.wait("tensor", self.ac.at())
        jc0, cb0 = self.bank.take(p, "tensor")
        jc1, cb1 = self.bank.take(p, "tensor")
        self.t(p, lambda e, cb0=cb0: e.matmul(self.psb[cb0][:],
                                              self.on1[0:1, :],
                                              self.bsum_b[:, 0:512],
                                              start=True, stop=True))
        self.t(p, lambda e, cb1=cb1: e.matmul(self.psb[cb1][:],
                                              self.on1[0:1, :],
                                              self.bsum_b[:, 512:1024],
                                              start=True, stop=True), inc=True)
        p.wait("vector", self.pe.at())
        self.v(p, lambda e, cb0=cb0: e.tensor_copy(self.bsb[:, 0:512],
                                                   self.psb[cb0][:]))
        self.v(p, lambda e, cb1=cb1: e.tensor_copy(self.bsb[:, 512:1024],
                                                   self.psb[cb1][:]),
               inc=True)
        self.bank.release(jc0, self.dv)
        self.bank.release(jc1, self.dv)
        p.emit(self.nc)

    def p5(self):
        p = Ph("p5")
        p.wait("gpsimd", self.eidx_mark)
        w1v = self.w1.rearrange("e (c p) f -> e p c f", p=T, c=DCH)[0]
        w2v = self.w2.rearrange("e (c p) d -> e p c d", p=T, c=FF // T)[0]
        b1v = self.b1r
        for s_i in range(2):
            incf = self.s_ix.inc(16)
            def g_b1(e, s_i=s_i, incf=incf):
                v = e.value_load(self.eidx[0:1, s_i:s_i + 1])
                srcb = dataclasses.replace(b1v[:, 0:32], offset=v)
                incf(e.dma_start(out=self.b1s[:, s_i * 32:(s_i + 1) * 32],
                                 in_=srcb))
                e.free_register(v.val)
            p.op("gpsimd", g_b1)
        p.wait("scalar", self.s_ix.at())
        hpool = Pool(list(range(len(self.hsb))))
        for e_i in range(2):
            for w in range(NWIN):
                firstw = (e_i == 0 and w == 0)
                j1, w1s, w1sem = self.w1r.next(p, "gpsimd")
                incf1 = w1sem.inc(16)
                def g_w1(e, e_i=e_i, w=w, w1s=w1s, incf1=incf1):
                    v = e.value_load(
                        self.eidx[0:1, 2 + e_i * 8 + w:3 + e_i * 8 + w])
                    src1 = dataclasses.replace(w1v[:, :, 0:512], offset=v)
                    incf1(e.dma_start(
                        out=w1s[:].rearrange("p (c f) -> p c f", c=DCH),
                        in_=src1))
                    e.free_register(v.val)
                p.op("gpsimd", g_w1)
                j2, w2s, w2sem = self.w2r.next(p, "gpsimd")
                incf2 = w2sem.inc(16)
                def g_w2(e, e_i=e_i, w=w, w2s=w2s, incf2=incf2):
                    v = e.value_load(
                        self.eidx[0:1, 18 + e_i * 8 + w:19 + e_i * 8 + w])
                    src2 = dataclasses.replace(w2v[:, 0:4, :], offset=v)
                    incf2(e.dma_start(
                        out=w2s[:].rearrange("p (c d) -> p c d", c=4),
                        in_=src2))
                    e.free_register(v.val)
                p.op("gpsimd", g_w2)
                hslots = []
                p.wait("tensor", w1sem.at(), w2sem.at())
                twins = [(g, min(512, self.NTT - g))
                         for g in range(0, self.NTT, 512)]
                for fc in range(4):
                    jh, hi = hpool.take(p, "scalar")
                    hslots.append(hi)
                    hs = self.hsb[hi]
                    for g0, gn in twins:
                        jb, bi = self.bank.take(p, "tensor")
                        bk = self.psb[bi]
                        for dn in range(DCH):
                            self.t(p, lambda e, dn=dn, fc=fc, g0=g0, gn=gn,
                                   w1s=w1s, bk=bk:
                                   e.matmul(bk[:, 0:gn],
                                            w1s[:,
                                                dn * 512 + fc * T:
                                                dn * 512 + fc * T + T],
                                            self.xnfm[:,
                                                      dn * self.NTT + g0:
                                                      dn * self.NTT + g0 + gn],
                                            start=(dn == 0),
                                            stop=(dn == DCH - 1)),
                                   inc=(dn == DCH - 1))
                        p.wait("scalar", self.pe.at())
                        bcol = self.b1s[:, e_i * 32 + w * 4 + fc:
                                        e_i * 32 + w * 4 + fc + 1]
                        self.a(p, lambda e, bk=bk, hs=hs, g0=g0, gn=gn,
                               bcol=bcol:
                               e.activation(hs[:, g0:g0 + gn],
                                            bk[:, 0:gn], AF.Gelu, bias=bcol),
                               inc=True)
                        self.bank.release(jb, self.ac)
                for i in range(self.NT0):
                    jy0, y0 = self.bank.take(p, "tensor")
                    jy1, y1 = self.bank.take(p, "tensor")
                    p.wait("tensor", self.ac.at())
                    for fc in range(4):
                        hs = self.hsb[hslots[fc]]
                        self.t(p, lambda e, fc=fc, i=i, w2s=w2s, hs=hs, y0=y0:
                               e.matmul(self.psb[y0][:],
                                        hs[:, i * T:(i + 1) * T],
                                        w2s[:, fc * D:fc * D + 512],
                                        start=(fc == 0), stop=(fc == 3)))
                        self.t(p, lambda e, fc=fc, i=i, w2s=w2s, hs=hs, y1=y1:
                               e.matmul(self.psb[y1][:],
                                        hs[:, i * T:(i + 1) * T],
                                        w2s[:, fc * D + 512:
                                            fc * D + 1024],
                                        start=(fc == 0), stop=(fc == 3)),
                               inc=(fc == 3))
                    p.wait("vector", self.pe.at())
                    oc = self.O[:, i * D:(i + 1) * D]
                    cwc = self.cwb2[:, e_i:e_i + 1]
                    if firstw:
                        self.v(p, lambda e, y0=y0, oc=oc, cwc=cwc:
                               e.tensor_scalar(oc[:, 0:512], self.psb[y0][:],
                                               cwc, None, ALU.mult))
                        self.v(p, lambda e, y1=y1, oc=oc, cwc=cwc:
                               e.tensor_scalar(oc[:, 512:1024],
                                               self.psb[y1][:],
                                               cwc, None, ALU.mult), inc=True)
                    else:
                        self.v(p, lambda e, y0=y0, oc=oc, cwc=cwc:
                               e.scalar_tensor_tensor(oc[:, 0:512],
                                                      self.psb[y0][:], cwc,
                                                      oc[:, 0:512],
                                                      ALU.mult, ALU.add))
                        self.v(p, lambda e, y1=y1, oc=oc, cwc=cwc:
                               e.scalar_tensor_tensor(oc[:, 512:1024],
                                                      self.psb[y1][:], cwc,
                                                      oc[:, 512:1024],
                                                      ALU.mult, ALU.add),
                               inc=True)
                    self.bank.release(jy0, self.dv)
                    self.bank.release(jy1, self.dv)
                for hi in hslots:
                    hpool.release(hi, self.pe)
                self.w1r.release(j1, self.pe)
                self.w2r.release(j2, self.pe)
        if self.dbg:
            p.wait("sync", self.dv.at())
            self.dma(p, "sync", self.so, self.d_y[:, 0:NOWN * D // 2],
                     self.O[:, 0:NOWN * D // 2])
            self.dma(p, "sync", self.so, self.d_y[:, NOWN * D // 2:],
                     self.O[:, NOWN * D // 2:])
            p.wait("sync", self.so.at())
        p.emit(self.nc)

    def p6(self):
        p = Ph("p6")
        self.dma(p, "sync", self.s_gb, self.gm[:], self.gmb)
        self.dma(p, "sync", self.s_gb, self.bt[:], self.btb)
        p.wait("vector", self.s_gb.at())
        for i in range(self.NT0):
            j, xo, xsem = self.xo2r.next(p, "sync")
            self.dma(p, "sync", xsem, xo[:],
                     self.x_own[i * T:(i + 1) * T, :])
            p.wait("vector", xsem.at())
            oc = self.O[:, i * D:(i + 1) * D]
            self.v(p, lambda e, xo=xo, i=i: e.tensor_scalar(
                self.sqscr6[:], xo[:], self.r_own[:, i:i + 1],
                self.nm_own[:, i:i + 1], ALU.mult, ALU.add))
            self.v(p, lambda e: e.tensor_add(self.sqscr6[:], self.sqscr6[:],
                                             self.bsb[:]))
            self.v(p, lambda e, oc=oc: e.tensor_add(oc, oc, self.sqscr6[:]),
                   inc=True)
            self.xo2r.release(j, self.dv)
            self.v(p, lambda e, oc=oc, i=i: e.reduce_sum(
                self.sxs[:, i:i + 1], oc, axis=AX.X))
            self.v(p, lambda e, oc=oc: e.tensor_mul(self.sqscr6[:], oc, oc))
            self.v(p, lambda e, i=i: e.reduce_sum(self.sqs[:, i:i + 1],
                                                  self.sqscr6[:], axis=AX.X),
                   inc=True)
        for lo in range(0, self.NT0, 8):
            self.ln_chain(p, lo, min(8, self.NT0 - lo),
                          self.r_own, self.nm_own)
        for i in range(self.NT0):
            oc = self.O[:, i * D:(i + 1) * D]
            sc = self.s0[:, i * D:(i + 1) * D]
            self.v(p, lambda e, oc=oc, i=i: e.tensor_scalar(
                self.gscr[:], oc, self.r_own[:, i:i + 1],
                self.nm_own[:, i:i + 1], ALU.mult, ALU.add))
            self.v(p, lambda e: e.tensor_mul(self.gscr[:], self.gscr[:],
                                             self.gm[:]))
            self.v(p, lambda e: e.tensor_add(self.gscr[:], self.gscr[:],
                                             self.bt[:]), inc=True)
            p.wait("scalar", self.dv.at())
            self.a(p, lambda e: e.activation(self.sqscr6[:], self.gscr[:],
                                             AF.Silu), inc=True)
            p.wait("vector", self.ac.at())
            self.v(p, lambda e, sc=sc: e.tensor_mul(sc, self.sqscr6[:], sc),
                   inc=True)
            j, xo, xsem = self.xo2r.next(p, "sync")
            p.wait("sync", self.so.at())
            self.dma(p, "sync", xsem, xo[:],
                     self.x_own[i * T:(i + 1) * T, :])
            p.wait("vector", xsem.at())
            self.v(p, lambda e, xo=xo, sc=sc: e.tensor_add(xo[:], xo[:], sc),
                   inc=True)
            p.wait("sync", self.dv.at())
            self.dma(p, "sync", self.so, self.x1o[i * T:(i + 1) * T, :], xo[:])
        p.wait("sync", self.so.at())
        p.emit(self.nc)


def build_l0(dbg=False, upto=6):
    k = L0(dbg=dbg)
    nc = k.build(upto=upto)
    k.ctx.__exit__(None, None, None)
    return nc


def host_common(inputs):
    f32 = lambda a: np.ascontiguousarray(np.asarray(a, np.float32))
    d = {}
    d["wt"] = f32(np.asarray(inputs["L0_attn_W"], np.float32).T)
    d["w1"] = f32(inputs["L0_ffn_W1"])
    b1 = np.asarray(inputs["L0_ffn_b1"], np.float32)
    d["b1r"] = f32(b1.reshape(E, FF // T, T).transpose(2, 0, 1).reshape(T, -1))
    d["w2"] = f32(inputs["L0_ffn_W2"])
    d["b2"] = f32(inputs["L0_ffn_b2"])
    d["wr"] = f32(inputs["L0_ffn_Wr"])
    d["br"] = f32(np.asarray(inputs["L0_ffn_br"], np.float32).reshape(1, E))
    d["gmb"] = f32(np.tile(np.asarray(inputs["L0_ffn_gamma"],
                                      np.float32)[None, :], (T, 1)))
    d["btb"] = f32(np.tile(np.asarray(inputs["L0_ffn_beta"],
                                      np.float32)[None, :], (T, 1)))
    d["cst_in"] = f32(np.tile(np.array([CONST_VALS], np.float32), (T, 1)))
    d["iota8"] = np.arange(16, dtype=np.float32).reshape(1, 16)
    d["idn"] = np.eye(T, dtype=NPBF)
    d["on1"] = np.ones((1, T), dtype=NPBF)
    return d


def host_l0_inmaps(inputs):
    com = host_common(inputs)
    x0 = np.ascontiguousarray(np.asarray(inputs["L0_attn_x"], np.float32))
    qi = np.arange(T)
    ki = np.arange(KC)
    ci = np.arange(NCHUNK)
    ti = np.arange(NOWN)
    in_maps = []
    for core in range(8):
        b, k = divmod(core, 4)
        qg = (ti[:, None, None, None] + 16 * k) * T + qi[None, None, :, None]
        kg = ci[None, :, None, None] * KC + ki[None, None, None, :]
        cm = np.where(kg <= qg, np.float32(0), np.float32(-1e9)).astype(NPBF)
        m = dict(com)
        m["x_seq"] = x0[b]
        m["x_own"] = np.ascontiguousarray(x0[b, 2048 * k:2048 * (k + 1)])
        m["cmask"] = cm
        in_maps.append(m)
    return in_maps


NOH = 18
NOHT = NOH * T


class L1(L0):

    def __init__(self, dbg=False):
        self.dbg = dbg
        nc = bass.Bass("TRN2", target_bir_lowering=False)
        self.nc = nc
        self.ctx = contextlib.ExitStack()
        self.ctx.__enter__()

        inp = lambda n, s, d=F32: nc.dram_tensor(n, s, d, kind="ExternalInput").ap()
        outp = lambda n, s, d=F32: nc.dram_tensor(n, s, d, kind="ExternalOutput").ap()
        self.x1f = inp("x1f", [NOHT, D])
        self.bmsk = inp("bmsk", [NOWN, 3, T, T], BF16)
        self.wqkv = inp("wqkv", [D, 3 * D])
        self.bqkvr = inp("bqkvr", [T, 24])
        self.bqv = inp("bqv", [1, D])
        self.lnwb = inp("lnwb", [T, D])
        self.lnbb = inp("lnbb", [T, D])
        self.wg = inp("wg", [D, E])
        self.w1 = inp("w1", [E, D, FF])
        self.b1r = inp("b1r", [T, E * (FF // T)])
        self.w2 = inp("w2", [E, FF, D])
        self.b2 = inp("b2", [E, D])
        self.cst_in = inp("cst_in", [T, NCONST])
        self.idn_in = inp("idn", [T, T], BF16)
        self.on1_in = inp("on1", [1, T], BF16)
        self.x2o = outp("x2o", [NOT, D])
        if dbg:
            self.d_xnfm = outp("d_xnfm", [T, DCH * NOHT], BF16)
            self.d_gm = outp("d_gm", [T, NOWN * E])
            self.d_y = outp("d_y", [T, NOWN * D])
            self.d_s1 = outp("d_s1", [T, NOWN * D])
            self.d_att = outp("d_att", [T, NOWN * D])

        c = self.ctx
        sb = lambda n, s, d=F32: c.enter_context(nc.sbuf_tensor(n, s, d))
        pt = lambda n, s, d=F32: c.enter_context(nc.psum_tensor(n, s, d))
        self.cst = sb("cst", [T, NCONST])
        self.idn = sb("idn_s", [T, T], BF16)
        self.on1 = sb("on1_s", [1, T], BF16)
        self.b2row = [sb(f"b2row{i}", [1, D], BF16) for i in range(2)]
        self.wgs = sb("wgs", [T, DCH * E], BF16)
        self.bqvb = sb("bqvb", [1, D], BF16)
        self.bqr = sb("bqr", [T, 24])
        self.xnfm = sb("xnfm", [T, DCH * NOHT], BF16)
        self.r_oh = sb("r_oh", [T, NOH])
        self.nm_oh = sb("nm_oh", [T, NOH])
        self.gmask = sb("gmask", [T, NOWN * E])
        self.tsc = sb("tsc", [T, 20])
        self.tscl = sb("tscl", [T, 48])
        self.sxs = sb("sxs", [T, NOH])
        self.sqs = sb("sqs", [T, NOH])

        self.psb = [pt(f"psf{i}", [T, 512]) for i in range(5)]
        self.pt2 = [pt(f"pt2_{i}", [T, 1024], BF16) for i in range(2)]
        self.ps_r = pt("ps_r", [T, 512])

        self.s_c0 = Sem(nc, c, "s_c0")
        self.s_g0 = Sem(nc, c, "s_g0")
        self.s_b1 = Sem(nc, c, "s_b1")
        self.s_gb = Sem(nc, c, "s_gb")
        self.pe = Sem(nc, c, "pe")
        self.dv = Sem(nc, c, "dv")
        self.ac = Sem(nc, c, "ac")
        self.so = Sem(nc, c, "so")

        self.bank = Pool(list(range(5)))
        self.tslot = Pool(list(range(2)))
        self.b2r = Ring(nc, c, "rb2", self.b2row)

    def build(self, upto=9, q4upto=9):
        nc = self.nc
        self.q4upto = q4upto
        self.q0()
        with contextlib.ExitStack() as sA:
            sbA = lambda n, s, d=F32: self.sbt(sA, n, s, d)
            self.sqscr = sbA("sqscr", [T, D])
            self.xntm = sbA("xntm", [T, D], BF16)
            self.xs = [sbA(f"xs{i}", [T, D]) for i in range(2)]
            self.xsr = Ring(nc, self.ctx, "rxs", self.xs)
            self.g1 = sbA("g1", [T, NOWN * D], BF16)
            with contextlib.ExitStack() as sO:
                sbO = lambda n, s, d=F32: self.sbt(sO, n, s, d)
                self.O = sbO("O", [T, NOWN * D])
                self.q1()
                if upto < 2:
                    return nc
                with contextlib.ExitStack() as s5:
                    sb5 = lambda n, s, d=F32: self.sbt(s5, n, s, d)
                    self.b1s = sb5("b1s", [T, E * (FF // T)])
                    self.b2bb = sb5("b2bb", [T, D])
                    self.w1w = [sb5(f"w1w{i}", [T, DCH * 512], BF16)
                                for i in range(2)]
                    self.w2w = [sb5(f"w2w{i}", [T, 4 * D], BF16)
                                for i in range(2)]
                    self.w1r = Ring(nc, self.ctx, "rw1", self.w1w)
                    self.w2r = Ring(nc, self.ctx, "rw2", self.w2w)
                    self.hsb = [sb5(f"hsb{i}", [T, NOT], BF16)
                                for i in range(4)]
                    self.q2()
                if upto < 3:
                    return nc
                self.q3()
            if upto < 4:
                return nc
            with contextlib.ExitStack() as s4:
                sb4 = lambda n, s, d=F32: self.sbt(s4, n, s, d)
                self.kfm = sb4("kfm", [T, DCH * NOHT], BF16)
                self.vtm = sb4("vtm", [T, NOH * D], BF16)
                self.wqs = sb4("wqs", [T, DCH * D], BF16)
                self.bmr_sl = [sb4(f"bmsl{i}", [T, 3 * T], BF16)
                               for i in range(2)]
                self.bmr = Ring(nc, self.ctx, "rbm", self.bmr_sl)
                self.lnw = sb4("lnw", [T, D])
                self.lnb = sb4("lnb", [T, D])
                self.qf = sb4("qf", [T, 2 * DCH * T], BF16)
                self.sTs = sb4("sTs", [T, 3 * T], BF16)
                self.gscr = self.sqscr
                self.x1s = [sb4(f"x1s{i}", [T, D]) for i in range(2)]
                self.x1r = Ring(nc, self.ctx, "rx1", self.x1s)
                self.q4(q4upto=self.q4upto)
        return nc

    def q0(self):
        p = Ph("q0")
        for dst, srca in [(self.cst, self.cst_in), (self.idn, self.idn_in),
                          (self.on1, self.on1_in), (self.bqr, self.bqkvr)]:
            self.dma(p, "sync", self.s_c0, dst[:], srca)
        self.dma(p, "gpsimd", self.s_g0,
                 self.wgs[:].rearrange("p (c e) -> p c e", c=DCH),
                 self.wg.rearrange("(c p) e -> p c e", p=T, c=DCH))
        self.dma(p, "gpsimd", self.s_g0, self.bqvb[:], self.bqv)
        p.wait("vector", self.s_c0.at(), self.s_g0.at())
        p.wait("scalar", self.s_c0.at())
        p.wait("tensor", self.s_c0.at(), self.s_g0.at())
        p.emit(self.nc)

    def q1(self):
        p = Ph("q1")
        p.wait("sync", self.so.at())
        for i in range(NOH):
            j, xsl, xsem = self.xsr.next(p, "sync")
            self.dma(p, "sync", xsem, xsl[:],
                     self.x1f[i * T:(i + 1) * T, :])
            p.wait("vector", xsem.at())
            self.stats_tile(p, xsl[:], i)
            self.xsr.release(j, self.dv)
        self.ln_chain(p, 0, NOH, self.r_oh, self.nm_oh)
        for i in range(NOH):
            j, xsl, xsem = self.xsr.next(p, "sync")
            self.dma(p, "sync", xsem, xsl[:],
                     self.x1f[i * T:(i + 1) * T, :])
            p.wait("vector", xsem.at())
            self.fm_tile(p, xsl[:],
                         self.r_oh[:, i:i + 1], self.nm_oh[:, i:i + 1],
                         self.xnfm, lambda dch, i=i: dch * NOHT + i * T)
            self.xsr.release(j, self.dv)
        for i in range(NOWN):
            oi = i + 2
            p.wait("tensor", self.ac.at())
            jg, gbi = self.bank.take(p, "tensor")
            gb = self.psb[gbi]
            for dn in range(DCH):
                self.t(p, lambda e, dn=dn, oi=oi, gb=gb: e.matmul(
                    gb[:, 0:E],
                    self.xnfm[:, dn * NOHT + oi * T:dn * NOHT + oi * T + T],
                    self.wgs[:, dn * E:(dn + 1) * E],
                    start=(dn == 0), stop=(dn == DCH - 1)),
                    inc=(dn == DCH - 1))
            p.wait("vector", self.pe.at())
            gm = self.gmask[:, i * E:(i + 1) * E]
            self.v(p, lambda e, gb=gb: e.reduce_max(self.tsc[:, 0:1],
                                                    gb[:, 0:E], axis=AX.X))
            self.v(p, lambda e, gm=gm, gb=gb: e.tensor_scalar(
                gm, gb[:, 0:E], self.tsc[:, 0:1], None, ALU.is_ge))
            self.bank.release(jg, self.dv)
        if self.dbg:
            p.wait("sync", self.ac.at(), self.dv.at())
            self.dma(p, "sync", self.so, self.d_xnfm, self.xnfm[:])
            self.dma(p, "sync", self.so, self.d_gm, self.gmask[:])
            p.wait("sync", self.so.at())
        p.emit(self.nc)

    def q2(self):
        p = Ph("q2")
        self.dma(p, "sync", self.s_b1, self.b1s[:], self.b1r)
        b2pend = self.b2r.next(p, "gpsimd")
        self.dma(p, "gpsimd", b2pend[2], b2pend[1][:], self.b2[0:1, :])
        p.wait("scalar", self.s_b1.at())
        hpool = Pool(list(range(4)))
        for e_i in range(E):
            jb2, b2sl, b2sem = b2pend
            if e_i + 1 < E:
                b2pend = self.b2r.next(p, "gpsimd")
                self.dma(p, "gpsimd", b2pend[2], b2pend[1][:],
                         self.b2[e_i + 1:e_i + 2, :])
            jbb, bb = self.bank.take(p, "tensor")
            jbb1, bb1 = self.bank.take(p, "tensor")
            p.wait("tensor", b2sem.at())
            self.t(p, lambda e, bb=bb, b2sl=b2sl: e.matmul(
                self.psb[bb][:], self.on1[0:1, :],
                b2sl[0:1, 0:512], start=True, stop=True))
            self.t(p, lambda e, bb1=bb1, b2sl=b2sl: e.matmul(
                self.psb[bb1][:], self.on1[0:1, :],
                b2sl[0:1, 512:1024], start=True, stop=True), inc=True)
            self.b2r.release(jb2, self.pe)
            p.wait("vector", self.pe.at())
            self.v(p, lambda e, bb=bb: e.tensor_copy(self.b2bb[:, 0:512],
                                                     self.psb[bb][:]))
            self.v(p, lambda e, bb1=bb1: e.tensor_copy(self.b2bb[:, 512:1024],
                                                       self.psb[bb1][:]))
            self.bank.release(jbb, self.dv)
            self.bank.release(jbb1, self.dv)
            for w in range(NWIN):
                firstw = (e_i == 0 and w == 0)
                j1, w1s, w1sem = self.w1r.next(p, "gpsimd")
                self.dma(p, "gpsimd", w1sem,
                         w1s[:].rearrange("p (c f) -> p c f", c=DCH),
                         self.w1[e_i, :, w * 512:(w + 1) * 512].rearrange(
                             "(c p) f -> p c f", p=T, c=DCH))
                j2, w2s, w2sem = self.w2r.next(p, "gpsimd")
                self.dma(p, "gpsimd", w2sem,
                         w2s[:].rearrange("p (c d) -> p c d", c=4),
                         self.w2[e_i, w * 512:(w + 1) * 512, :].rearrange(
                             "(c p) d -> p c d", p=T, c=4))
                hslots = []
                p.wait("tensor", w1sem.at(), w2sem.at())
                for fc in range(4):
                    jh, hi = hpool.take(p, "scalar")
                    hslots.append(hi)
                    hs = self.hsb[hi]
                    for tg in range(4):
                        jb, bi = self.bank.take(p, "tensor")
                        bk = self.psb[bi]
                        for dn in range(DCH):
                            self.t(p, lambda e, dn=dn, fc=fc, tg=tg, w1s=w1s,
                                   bk=bk:
                                   e.matmul(bk[:],
                                            w1s[:, dn * 512 + fc * T:
                                                dn * 512 + fc * T + T],
                                            self.xnfm[:,
                                                      dn * NOHT + 2 * T +
                                                      tg * 512:
                                                      dn * NOHT + 2 * T +
                                                      tg * 512 + 512],
                                            start=(dn == 0),
                                            stop=(dn == DCH - 1)),
                                   inc=(dn == DCH - 1))
                        p.wait("scalar", self.pe.at())
                        bcol = self.b1s[:, e_i * 32 + w * 4 + fc:
                                        e_i * 32 + w * 4 + fc + 1]
                        self.a(p, lambda e, bk=bk, hs=hs, tg=tg, bcol=bcol:
                               e.activation(hs[:, tg * 512:(tg + 1) * 512],
                                            bk[:], AF.Gelu, bias=bcol),
                               inc=True)
                        self.bank.release(jb, self.ac)
                for i in range(NOWN):
                    jy0, y0 = self.bank.take(p, "tensor")
                    jy1, y1 = self.bank.take(p, "tensor")
                    p.wait("tensor", self.ac.at())
                    for fc in range(4):
                        hs = self.hsb[hslots[fc]]
                        self.t(p, lambda e, fc=fc, i=i, w2s=w2s, hs=hs, y0=y0:
                               e.matmul(self.psb[y0][:],
                                        hs[:, i * T:(i + 1) * T],
                                        w2s[:, fc * D:fc * D + 512],
                                        start=(fc == 0), stop=(fc == 3)))
                        self.t(p, lambda e, fc=fc, i=i, w2s=w2s, hs=hs, y1=y1:
                               e.matmul(self.psb[y1][:],
                                        hs[:, i * T:(i + 1) * T],
                                        w2s[:, fc * D + 512:fc * D + 1024],
                                        start=(fc == 0), stop=(fc == 3)),
                               inc=(fc == 3))
                    p.wait("vector", self.pe.at())
                    oc = self.O[:, i * D:(i + 1) * D]
                    gc = self.gmask[:, i * E + e_i:i * E + e_i + 1]
                    if firstw:
                        self.v(p, lambda e, y0=y0, oc=oc, gc=gc:
                               e.tensor_scalar(oc[:, 0:512], self.psb[y0][:],
                                               gc, None, ALU.mult))
                        self.v(p, lambda e, y1=y1, oc=oc, gc=gc:
                               e.tensor_scalar(oc[:, 512:1024],
                                               self.psb[y1][:],
                                               gc, None, ALU.mult))
                    else:
                        self.v(p, lambda e, y0=y0, oc=oc, gc=gc:
                               e.scalar_tensor_tensor(oc[:, 0:512],
                                                      self.psb[y0][:], gc,
                                                      oc[:, 0:512],
                                                      ALU.mult, ALU.add))
                        self.v(p, lambda e, y1=y1, oc=oc, gc=gc:
                               e.scalar_tensor_tensor(oc[:, 512:1024],
                                                      self.psb[y1][:], gc,
                                                      oc[:, 512:1024],
                                                      ALU.mult, ALU.add))
                    self.bank.release(jy0, self.dv)
                    self.bank.release(jy1, self.dv)
                for hi in hslots:
                    hpool.release(hi, self.pe)
                self.w1r.release(j1, self.pe)
                self.w2r.release(j2, self.pe)
            for i in range(NOWN):
                oc = self.O[:, i * D:(i + 1) * D]
                gc = self.gmask[:, i * E + e_i:i * E + e_i + 1]
                self.v(p, lambda e, oc=oc, gc=gc: e.scalar_tensor_tensor(
                    oc, self.b2bb[:], gc, oc, ALU.mult, ALU.add))
        if self.dbg:
            p.wait("sync", self.dv.at())
            self.dma(p, "sync", self.so, self.d_y[:, 0:NOWN * D // 2],
                     self.O[:, 0:NOWN * D // 2])
            self.dma(p, "sync", self.so, self.d_y[:, NOWN * D // 2:],
                     self.O[:, NOWN * D // 2:])
            p.wait("sync", self.so.at())
        p.emit(self.nc)

    def q3(self):
        p = Ph("q3")
        p.wait("scalar", self.dv.at())
        for i in range(NOWN):
            self.a(p, lambda e, i=i: e.activation(
                self.g1[:, i * D:(i + 1) * D], self.O[:, i * D:(i + 1) * D],
                AF.Silu))
        p.emit(self.nc)

    def q4(self, q4upto=9):
        nc = self.nc
        p = Ph("q4")
        self.dma(p, "sync", self.s_gb, self.lnw[:], self.lnwb)
        self.dma(p, "sync", self.s_gb, self.lnb[:], self.lnbb)
        p.wait("vector", self.s_gb.at())
        qp = Pool(list(range(6)))
        qp.marks = [self.bank.marks[j] for j in range(5)] + [self.dv.at()]
        qb = lambda j: self.psb[j] if j < 5 else self.ps_r
        self.dma(p, "gpsimd", self.s_g0,
                 self.wqs[:].rearrange("p (c n) -> p c n", c=DCH),
                 self.wqkv[:, D:2 * D].rearrange("(c p) n -> p c n", p=T,
                                                 c=DCH))
        p.wait("tensor", self.s_g0.at(), self.ac.at())
        wins = [(0, 512), (512, 512), (1024, 512), (1536, 512), (2048, 256)]
        for dk in range(DCH):
            for w0, wn in wins:
                jb, bi = qp.take(p, "tensor")
                bk = qb(bi)
                for dn in range(DCH):
                    self.t(p, lambda e, dn=dn, dk=dk, w0=w0, wn=wn, bk=bk:
                           e.matmul(bk[:, 0:wn],
                                    self.wqs[:, dn * D + dk * T:
                                             dn * D + dk * T + T],
                                    self.xnfm[:, dn * NOHT + w0:
                                              dn * NOHT + w0 + wn],
                                    start=(dn == 0), stop=(dn == DCH - 1)),
                           inc=(dn == DCH - 1))
                p.wait("vector", self.pe.at())
                bcol = self.bqr[:, 8 + dk:9 + dk]
                self.v(p, lambda e, bk=bk, dk=dk, w0=w0, wn=wn, bcol=bcol:
                       e.tensor_scalar(self.kfm[:, dk * NOHT + w0:
                                                dk * NOHT + w0 + wn],
                                       bk[:, 0:wn], bcol, None, ALU.add))
                qp.release(jb, self.dv)
        if q4upto < 2:
            p.emit(nc)
            return
        self.dma(p, "gpsimd", self.s_g0,
                 self.wqs[:].rearrange("p (c n) -> p c n", c=DCH),
                 self.wqkv[:, 2 * D:3 * D].rearrange("(c p) n -> p c n",
                                                     p=T, c=DCH),
                 guard=self.dv.at())
        p.wait("tensor", self.s_g0.at())
        for oi in range(NOH):
            jb0, b0i = qp.take(p, "tensor")
            jb1, b1i = qp.take(p, "tensor")
            b0, b1 = qb(b0i), qb(b1i)
            for dn in range(DCH):
                self.t(p, lambda e, dn=dn, oi=oi, b0=b0: e.matmul(
                    b0[:],
                    self.xnfm[:, dn * NOHT + oi * T:dn * NOHT + oi * T + T],
                    self.wqs[:, dn * D:dn * D + 512],
                    start=(dn == 0), stop=False))
                self.t(p, lambda e, dn=dn, oi=oi, b1=b1: e.matmul(
                    b1[:],
                    self.xnfm[:, dn * NOHT + oi * T:dn * NOHT + oi * T + T],
                    self.wqs[:, dn * D + 512:dn * D + 1024],
                    start=(dn == 0), stop=False))
            self.t(p, lambda e, b0=b0: e.matmul(
                b0[:], self.on1[0:1, :], self.bqvb[:, 0:512],
                start=False, stop=True))
            self.t(p, lambda e, b1=b1: e.matmul(
                b1[:], self.on1[0:1, :], self.bqvb[:, 512:1024],
                start=False, stop=True), inc=True)
            p.wait("vector", self.pe.at())
            self.v(p, lambda e, b0=b0, oi=oi: e.tensor_copy(
                self.vtm[:, oi * D:oi * D + 512], b0[:]))
            self.v(p, lambda e, b1=b1, oi=oi: e.tensor_copy(
                self.vtm[:, oi * D + 512:oi * D + 1024], b1[:]))
            qp.release(jb0, self.dv)
            qp.release(jb1, self.dv)
        if q4upto < 3:
            p.emit(nc)
            return
        self.dma(p, "gpsimd", self.s_g0,
                 self.wqs[:].rearrange("p (c n) -> p c n", c=DCH),
                 self.wqkv[:, 0:D].rearrange("(c p) n -> p c n", p=T, c=DCH),
                 guard=self.dv.at())
        p.wait("tensor", self.s_g0.at())

        def stage_a(i):
            oi = i + 2
            par = i % 2
            jbm, bmsl, bmsem = self.bmr.next(p, "sync")
            self.dma(p, "sync", bmsem,
                     bmsl[:].rearrange("p (o q) -> p o q", o=3),
                     self.bmsk[i].rearrange("o k q -> k o q"))
            qfp = self.qf[:, par * D:(par + 1) * D]
            for half in range(2):
                jb, bi = qp.take(p, "tensor")
                bk = qb(bi)
                for dq in range(4):
                    dqg = half * 4 + dq
                    for dn in range(DCH):
                        self.t(p, lambda e, dn=dn, dqg=dqg, dq=dq, oi=oi,
                               bk=bk:
                               e.matmul(bk[:, dq * T:(dq + 1) * T],
                                        self.wqs[:, dn * D + dqg * T:
                                                 dn * D + dqg * T + T],
                                        self.xnfm[:, dn * NOHT + oi * T:
                                                  dn * NOHT + oi * T + T],
                                        start=(dn == 0),
                                        stop=(dn == DCH - 1)),
                               inc=(dn == DCH - 1))
                p.wait("vector", self.pe.at())
                for dq in range(4):
                    dqg = half * 4 + dq
                    bcol = self.bqr[:, dqg:dqg + 1]
                    self.v(p, lambda e, bk=bk, dq=dq, dqg=dqg, bcol=bcol,
                           qfp=qfp:
                           e.tensor_scalar(qfp[:, dqg * T:(dqg + 1) * T],
                                           bk[:, dq * T:(dq + 1) * T],
                                           bcol, None, ALU.add))
                qp.release(jb, self.dv)
            return (i, jbm, bmsl, bmsem, self.dv.at(), qfp)

        def stage_bc(st):
            i, jbm, bmsl, bmsem, evA, qfp = st
            oi = i + 2
            for o in range(3):
                kt = oi - (2 - o)
                jb, bi = qp.take(p, "tensor")
                bk = qb(bi)
                if o == 0:
                    p.wait("tensor", evA)
                for dn in range(DCH):
                    self.t(p, lambda e, dn=dn, kt=kt, bk=bk, qfp=qfp:
                           e.matmul(
                        bk[:, 0:T],
                        self.kfm[:, dn * NOHT + kt * T:dn * NOHT + kt * T + T],
                        qfp[:, dn * T:(dn + 1) * T],
                        start=(dn == 0), stop=(dn == DCH - 1)),
                        inc=(dn == DCH - 1))
                p.wait("vector", self.pe.at(), bmsem.at())
                bmc = bmsl[:, o * T:o * T + T]
                self.v(p, lambda e, bk=bk, o=o, bmc=bmc: e.tensor_tensor(
                    self.sTs[:, o * T:(o + 1) * T], bk[:, 0:T], bmc,
                    ALU.mult))
                qp.release(jb, self.dv)
            self.bmr.release(jbm, self.dv)
            mm_mark = self.dv.at()
            ja0, a0i = qp.take(p, "tensor")
            ja1, a1i = qp.take(p, "tensor")
            a0b, a1b = qb(a0i), qb(a1i)
            p.wait("tensor", mm_mark)
            for o in range(3):
                kt = oi - (2 - o)
                self.t(p, lambda e, o=o, kt=kt, a0b=a0b: e.matmul(
                    a0b[:], self.sTs[:, o * T:(o + 1) * T],
                    self.vtm[:, kt * D:kt * D + 512],
                    start=(o == 0), stop=(o == 2)))
                self.t(p, lambda e, o=o, kt=kt, a1b=a1b: e.matmul(
                    a1b[:], self.sTs[:, o * T:(o + 1) * T],
                    self.vtm[:, kt * D + 512:kt * D + 1024],
                    start=(o == 0), stop=(o == 2)), inc=(o == 2))
            p.wait("vector", self.pe.at())
            sx = self.tsc[:, 0:1]
            sq = self.tsc[:, 1:2]
            t0 = self.tsc[:, 2:3]
            t1 = self.tsc[:, 3:4]
            self.v(p, lambda e, a0b=a0b: e.reduce_sum(t0, a0b[:], axis=AX.X))
            self.v(p, lambda e, a1b=a1b: e.reduce_sum(t1, a1b[:], axis=AX.X))
            self.ve(p, lambda e: e.tensor_add(sx, t0, t1))
            p.wait("scalar", self.dv.at())
            self.a(p, lambda e, a0b=a0b: e.activation(
                self.sqscr[:, 0:512], a0b[:], AF.Square))
            self.a(p, lambda e, a1b=a1b: e.activation(
                self.sqscr[:, 512:1024], a1b[:], AF.Square))
            p.wait("vector", self.ac.at())
            self.v(p, lambda e: e.reduce_sum(sq, self.sqscr[:], axis=AX.X))
            mu = self.tsc[:, 4:5]
            var = self.tsc[:, 5:6]
            rr = self.tsc[:, 6:7]
            nm = self.tsc[:, 7:8]
            self.ve(p, lambda e: e.tensor_scalar(mu, sx, self.cc(C_1024),
                                                 None, ALU.mult))
            self.ve(p, lambda e: e.tensor_scalar(var, sq, self.cc(C_1024),
                                                 self.cc(C_EPS), ALU.mult,
                                                 ALU.add))
            self.ve(p, lambda e: e.tensor_mul(t0, mu, mu))
            self.ve(p, lambda e: e.tensor_sub(var, var, t0))
            p.wait("scalar", self.dv.at())
            self.a(p, lambda e: e.activation(rr, var, AF.Sqrt))
            p.wait("vector", self.ac.at())
            self.ve(p, lambda e: e.reciprocal(rr, rr))
            self.ve(p, lambda e: e.tensor_mul(nm, mu, rr))
            self.ve(p, lambda e: e.tensor_scalar(nm, nm, self.cc(C_NEG1),
                                                 None, ALU.mult))
            self.v(p, lambda e, a0b=a0b: e.tensor_scalar(
                self.gscr[:, 0:512], a0b[:], rr, nm, ALU.mult, ALU.add))
            self.v(p, lambda e, a1b=a1b: e.tensor_scalar(
                self.gscr[:, 512:1024], a1b[:], rr, nm, ALU.mult, ALU.add))
            qp.release(ja0, self.dv)
            qp.release(ja1, self.dv)
            self.v(p, lambda e: e.tensor_mul(self.gscr[:], self.gscr[:],
                                             self.lnw[:]))
            self.v(p, lambda e: e.tensor_add(self.gscr[:], self.gscr[:],
                                             self.lnb[:]))
            jx, x1t, x1sem = self.x1r.next(p, "sync")
            p.wait("sync", self.so.at())
            self.dma(p, "sync", x1sem, x1t[:],
                     self.x1f[(i + 2) * T:(i + 3) * T, :])
            self.v(p, lambda e, i=i: e.tensor_mul(
                self.gscr[:], self.gscr[:], self.g1[:, i * D:(i + 1) * D]))
            p.wait("vector", x1sem.at())
            self.v(p, lambda e, x1t=x1t: e.tensor_add(x1t[:], x1t[:],
                                                      self.gscr[:]))
            p.wait("sync", self.dv.at())
            self.dma(p, "sync", self.so, self.x2o[i * T:(i + 1) * T, :],
                     x1t[:])

        st = stage_a(0)
        for i in range(NOWN):
            nxt = stage_a(i + 1) if i + 1 < NOWN else None
            stage_bc(st)
            st = nxt
        p.wait("sync", self.so.at())
        p.emit(nc)


def build_l1(dbg=False, upto=9, q4upto=9):
    k = L1(dbg=dbg)
    nc = k.build(upto=upto, q4upto=q4upto)
    k.ctx.__exit__(None, None, None)
    return nc


def host_l1_common(inputs):
    f32 = lambda a: np.ascontiguousarray(np.asarray(a, np.float32))
    d = {}
    d["wqkv"] = f32(inputs["L1_attn_W_qkv"])
    bq = np.asarray(inputs["L1_attn_b_qkv"], np.float32)
    d["bqkvr"] = f32(bq.reshape(24, T).T)
    d["bqv"] = f32(bq[2 * D:3 * D].reshape(1, D))
    d["lnwb"] = f32(np.tile(np.asarray(inputs["L1_attn_ln_w"],
                                       np.float32)[None, :], (T, 1)))
    d["lnbb"] = f32(np.tile(np.asarray(inputs["L1_attn_ln_b"],
                                       np.float32)[None, :], (T, 1)))
    d["wg"] = f32(np.asarray(inputs["L1_ffn_Wg"], np.float32).T)
    d["w1"] = f32(inputs["L1_ffn_W1"])
    b1 = np.asarray(inputs["L1_ffn_b1"], np.float32)
    d["b1r"] = f32(b1.reshape(E, FF // T, T).transpose(2, 0, 1).reshape(T, -1))
    d["w2"] = f32(inputs["L1_ffn_W2"])
    d["b2"] = f32(inputs["L1_ffn_b2"])
    d["cst_in"] = f32(np.tile(np.array([CONST_VALS], np.float32), (T, 1)))
    d["idn"] = np.eye(T, dtype=NPBF)
    d["on1"] = np.ones((1, T), dtype=NPBF)
    return d


def host_l1_inmaps(inputs, x1_full):
    com = host_l1_common(inputs)
    kk = np.arange(T)
    qq = np.arange(T)
    in_maps = []
    for core in range(8):
        b, k = divmod(core, 4)
        lo = 2048 * k - 256
        if lo < 0:
            halo = np.zeros((256, D), np.float32)
        else:
            halo = x1_full[b, lo:2048 * k]
        x1f = np.ascontiguousarray(
            np.concatenate([halo, x1_full[b, 2048 * k:2048 * (k + 1)]],
                           axis=0).astype(np.float32))
        bm = np.zeros((NOWN, 3, T, T), np.float32)
        for i in range(NOWN):
            gt = 16 * k + i
            for o in range(3):
                gk = gt - (2 - o)
                if gk < 0:
                    continue
                qg = gt * T + qq[None, :]
                kg = gk * T + kk[:, None]
                bm[i, o] = np.where((qg - 256 < kg) & (kg <= qg),
                                   np.float32(1.0 / 512.0), np.float32(0))
        m = dict(com)
        m["x1f"] = x1f
        m["bmsk"] = bm.astype(NPBF)
        in_maps.append(m)
    return in_maps



class LF(L1):

    B2S_LATE = True

    def __init__(self):
        self.dbg = False
        self.NT0 = NOH
        self.NTT = NOH * T
        nc = bass.Bass("TRN2", target_bir_lowering=False)
        self.nc = nc
        self.ctx = contextlib.ExitStack()
        self.ctx.__enter__()
        inp = lambda n, s, d=F32: nc.dram_tensor(n, s, d, kind="ExternalInput").ap()
        outp = lambda n, s, d=F32: nc.dram_tensor(n, s, d, kind="ExternalOutput").ap()
        self.x_seq = inp("x_seq", [S, D])
        self.x_oh = inp("x_oh", [NOH * T, D])
        self.cmask = inp("cmask", [NOH, NCHUNK, T, KC], BF16)
        self.wt = inp("wt", [D, 2 * D])
        self.w1a = inp("w1a", [E, D, FF])
        self.b1ra = inp("b1ra", [T, E * (FF // T)])
        self.w2a = inp("w2a", [E, FF, D])
        self.b2a = inp("b2a", [E, D])
        self.wr = inp("wr", [D, E])
        self.br = inp("br", [1, E])
        self.gmb = inp("gmb", [T, D])
        self.btb = inp("btb", [T, D])
        self.bmsk = inp("bmsk", [NOWN, 3, T, T], BF16)
        self.wqkv = inp("wqkv", [D, 3 * D])
        self.bqkvr = inp("bqkvr", [T, 24])
        self.bqv = inp("bqv", [1, D])
        self.lnwb = inp("lnwb", [T, D])
        self.lnbb = inp("lnbb", [T, D])
        self.wg = inp("wg", [D, E])
        self.w1b = inp("w1b", [E, D, FF])
        self.b1rb = inp("b1rb", [T, E * (FF // T)])
        self.w2b = inp("w2b", [E, FF, D])
        self.b2b = inp("b2b", [E, D])
        self.iota8 = inp("iota8", [1, 16])
        self.cst_in = inp("cst_in", [T, NCONST])
        self.idn_in = inp("idn", [T, T], BF16)
        self.on1_in = inp("on1", [1, T], BF16)
        self.x2o = outp("x2o", [NOT, D])
        self.x1scr = nc.dram_tensor("x1scr", [NOH * T, D], F32,
                                    kind="Internal").ap()
        self.scr8 = nc.dram_tensor("scr8", [1, E], F32, kind="Internal").ap()

        c = self.ctx
        sb = lambda n, s, d=F32: c.enter_context(nc.sbuf_tensor(n, s, d))
        pt = lambda n, s, d=F32: c.enter_context(nc.psum_tensor(n, s, d))
        self.cst = sb("cst", [T, NCONST])
        self.idn = sb("idn_s", [T, T], BF16)
        self.on1 = sb("on1_s", [1, T], BF16)
        self.tsc = sb("tsc", [T, 20])
        self.tscl = sb("tscl", [T, 48])
        self.sxs = sb("sxs", [T, 18])
        self.sqs = sb("sqs", [T, 18])
        self.psb = [pt(f"psf{i}", [T, 512]) for i in range(5)]
        self.pt2 = [pt(f"pt2_{i}", [T, 1024], BF16) for i in range(2)]
        self.ps_r = pt("ps_r", [T, 512])
        self.s_c0 = Sem(nc, c, "s_c0")
        self.s_g0 = Sem(nc, c, "s_g0")
        self.s_sc = Sem(nc, c, "s_sc")
        self.s_cw = Sem(nc, c, "s_cw")
        self.s_b1 = Sem(nc, c, "s_b1")
        self.s_gb = Sem(nc, c, "s_gb")
        self.s_ix = Sem(nc, c, "s_ix")
        self.pe = Sem(nc, c, "pe")
        self.dv = Sem(nc, c, "dv")
        self.ac = Sem(nc, c, "ac")
        self.so = Sem(nc, c, "so")
        self.bank = Pool(list(range(5)))
        self.tslot = Pool(list(range(2)))
        self.io8 = sb("io8", [1, 16])
        self.eidx = sb("eidx", [1, 34], mybir.dt.int32)
        self.eoff = sb("eoff", [1, 34])
        self.cwb2 = sb("cwb2", [T, 2])
        self.scr2 = nc.dram_tensor("scr2", [1, 2], F32, kind="Internal").ap()

    def build_fused(self):
        nc = self.nc
        self.x_own = self.x_oh
        self.x1o = self.x1scr
        self.w1, self.b1r = self.w1a, self.b1ra
        self.w2, self.b2 = self.w2a, self.b2a
        with contextlib.ExitStack() as lb:
            sbL = lambda n, s, d=F32: self.sbt(lb, n, s, d)
            self.wrs = sbL("wrs", [T, DCH * E], BF16)
            self.brs = sbL("brs", [1, E], BF16)
            self.xnfm = sbL("xnfm0", [T, DCH * self.NTT], BF16)
            self.r_own = sbL("r_own", [T, self.NT0])
            self.nm_own = sbL("nm_own", [T, self.NT0])
            self.r_kv = sbL("r_kv", [T, 16])
            self.nm_kv = sbL("nm_kv", [T, 16])
            self.O = sbL("O0", [T, self.NT0 * D])
            self.lbuf = sbL("lbuf", [T, self.NT0])
            self.mrun = sbL("mrun", [T, self.NT0])
            self.rsum = sbL("rsum", [T, DCH])
            self.rtmp = sbL("rtmp", [T, DCH])
            self.rw = sbL("rw", [1, 2 * E])
            self.rws = sbL("rws", [1, 32])
            self.meanbf = sbL("meanbf2", [T, DCH], BF16)
            self.cwb = sbL("cwb", [T, E])
            self.cwcol = sbL("cwcol", [E, 1], BF16)
            self.p0()
            with contextlib.ExitStack() as s12:
                sb2 = lambda n, s, d=F32: self.sbt(s12, n, s, d)
                self.sqscr = sb2("sqscr", [T, D])
                self.wkv = sb2("wkv", [T, DCH * 2 * D], BF16)
                self.Ksb = sb2("Ksb", [T, DCH * KC], BF16)
                self.Vsb = sb2("Vsb", [T, 8 * D], BF16)
                self.kvfm = sb2("kvfm", [T, DCH * 512], BF16)
                self.xs = [sb2(f"xs{i}", [T, D]) for i in range(2)]
                self.xntm = sb2("xntm", [T, D], BF16)
                self.mks = [sb2(f"mk{i}", [T, KC], BF16) for i in range(3)]
                self.Psb = sb2("Psb", [T, 2 * KC], BF16)
                self.xsr = Ring(nc, self.ctx, "rxs", self.xs)
                self.mkr = Ring(nc, self.ctx, "rmk", self.mks)
                self.p1()
                self.p2()
            with contextlib.ExitStack() as s36:
                sb3 = lambda n, s, d=F32: self.sbt(s36, n, s, d)
                self.s0 = sb3("s0", [T, self.NT0 * D], BF16)
                self.xo2 = [sb3(f"xo2{i}", [T, D]) for i in range(1)]
                self.xo2r = Ring(nc, self.ctx, "rxo", self.xo2)
                self.bsum_b = sb3("bsum_b", [1, D], BF16)
                self.bsb = sb3("bsb", [T, D])
                with contextlib.ExitStack() as s34:
                    sb34 = lambda n, s, d=F32: self.sbt(s34, n, s, d)
                    self.xsc = sb34("xsc", [T, D])
                    self.b2s = sb34("b2s", [E, D], BF16)
                    self.p3()
                    self.p4()
                with contextlib.ExitStack() as s5:
                    sb5 = lambda n, s, d=F32: self.sbt(s5, n, s, d)
                    self.b1s = sb5("b1s", [T, 2 * (FF // T)])
                    self.w1w = [sb5(f"w1w{i}", [T, DCH * 512], BF16)
                                for i in range(2)]
                    self.w2w = [sb5(f"w2w{i}", [T, 4 * D], BF16)
                                for i in range(2)]
                    self.w1r = Ring(nc, self.ctx, "rw1", self.w1w)
                    self.w2r = Ring(nc, self.ctx, "rw2", self.w2w)
                    self.hsb = [sb5(f"hsb{i}", [T, self.NTT], BF16)
                                for i in range(4)]
                    self.p5()
                with contextlib.ExitStack() as s6:
                    sb6 = lambda n, s, d=F32: self.sbt(s6, n, s, d)
                    self.sqscr6 = sb6("sqscr6", [T, D])
                    self.gscr = sb6("gscr", [T, D])
                    self.gm = sb6("gm", [T, D])
                    self.bt = sb6("bt", [T, D])
                    self.p6()
        self.NMPfx = "q_"
        self.x1f = self.x1scr
        self.w1, self.b1r = self.w1b, self.b1rb
        self.w2, self.b2 = self.w2b, self.b2b
        with contextlib.ExitStack() as l1b:
            sbM = lambda n, s, d=F32: self.sbt(l1b, n, s, d)
            self.b2row = [sbM(f"b2row{i}", [1, D], BF16) for i in range(2)]
            self.b2r = Ring(nc, self.ctx, "rb2", self.b2row)
            self.wgs = sbM("wgs", [T, DCH * E], BF16)
            self.bqvb = sbM("bqvb", [1, D], BF16)
            self.bqr = sbM("bqr", [T, 24])
            self.xnfm = sbM("xnfm1", [T, DCH * NOHT], BF16)
            self.r_oh = sbM("r_oh", [T, NOH])
            self.nm_oh = sbM("nm_oh", [T, NOH])
            self.gmask = sbM("gmask", [T, NOWN * E])
            self.q0()
            with contextlib.ExitStack() as sA:
                sbA = lambda n, s, d=F32: self.sbt(sA, n, s, d)
                self.sqscr = sbA("sqscr", [T, D])
                self.xntm = sbA("xntm", [T, D], BF16)
                self.xs = [sbA(f"xs{i}", [T, D]) for i in range(2)]
                self.xsr = Ring(nc, self.ctx, "rxs", self.xs)
                self.g1 = sbA("g1", [T, NOWN * D], BF16)
                with contextlib.ExitStack() as sO:
                    sbO = lambda n, s, d=F32: self.sbt(sO, n, s, d)
                    self.O = sbO("O", [T, NOWN * D])
                    self.q1()
                    with contextlib.ExitStack() as s5:
                        sb5 = lambda n, s, d=F32: self.sbt(s5, n, s, d)
                        self.b1s = sb5("b1s", [T, E * (FF // T)])
                        self.b2bb = sb5("b2bb", [T, D])
                        self.w1w = [sb5(f"w1w{i}", [T, DCH * 512], BF16)
                                    for i in range(2)]
                        self.w2w = [sb5(f"w2w{i}", [T, 4 * D], BF16)
                                    for i in range(2)]
                        self.w1r = Ring(nc, self.ctx, "rw1", self.w1w)
                        self.w2r = Ring(nc, self.ctx, "rw2", self.w2w)
                        self.hsb = [sb5(f"hsb{i}", [T, NOT], BF16)
                                    for i in range(4)]
                        self.q2()
                    self.q3()
                with contextlib.ExitStack() as s4:
                    sb4 = lambda n, s, d=F32: self.sbt(s4, n, s, d)
                    self.kfm = sb4("kfm", [T, DCH * NOHT], BF16)
                    self.vtm = sb4("vtm", [T, NOH * D], BF16)
                    self.wqs = sb4("wqs", [T, DCH * D], BF16)
                    self.bmr_sl = [sb4(f"bmsl{i}", [T, 3 * T], BF16)
                                   for i in range(2)]
                    self.bmr = Ring(nc, self.ctx, "rbm", self.bmr_sl)
                    self.lnw = sb4("lnw", [T, D])
                    self.lnb = sb4("lnb", [T, D])
                    self.qf = sb4("qf", [T, 2 * DCH * T], BF16)
                    self.sTs = sb4("sTs", [T, 3 * T], BF16)
                    self.gscr = self.sqscr
                    self.x1s = [sb4(f"x1s{i}", [T, D]) for i in range(2)]
                    self.x1r = Ring(nc, self.ctx, "rx1", self.x1s)
                    self.q4(q4upto=9)
        return nc


def build_fused():
    k = LF()
    nc = k.build_fused()
    k.ctx.__exit__(None, None, None)
    return nc


def host_fused_inmaps(inputs):
    com0 = host_common(inputs)
    com1 = host_l1_common(inputs)
    x0 = np.ascontiguousarray(np.asarray(inputs["L0_attn_x"], np.float32))
    qi = np.arange(T)
    ki = np.arange(KC)
    ci = np.arange(NCHUNK)
    ti = np.arange(NOH)
    kk = np.arange(T)
    qq = np.arange(T)
    in_maps = []
    for core in range(8):
        b, k = divmod(core, 4)
        lo = 2048 * k - 256
        halo = (np.zeros((256, D), np.float32) if lo < 0
                else x0[b, lo:2048 * k])
        x_oh = np.ascontiguousarray(np.concatenate(
            [halo, x0[b, 2048 * k:2048 * (k + 1)]], axis=0))
        qg = (ti[:, None, None, None] + 16 * k - 2) * T + qi[None, None, :, None]
        kg = ci[None, :, None, None] * KC + ki[None, None, None, :]
        cm = np.where((kg <= qg), np.float32(0), np.float32(-1e9)).astype(NPBF)
        bm = np.zeros((NOWN, 3, T, T), np.float32)
        for i in range(NOWN):
            gt = 16 * k + i
            for o in range(3):
                gk = gt - (2 - o)
                if gk < 0:
                    continue
                qg2 = gt * T + qq[None, :]
                kg2 = gk * T + kk[:, None]
                bm[i, o] = np.where((qg2 - 256 < kg2) & (kg2 <= qg2),
                                    np.float32(1.0 / 512.0), np.float32(0))
        m = {}
        m["x_seq"] = x0[b]
        m["x_oh"] = x_oh
        m["cmask"] = cm
        m["wt"] = com0["wt"]
        m["w1a"] = com0["w1"]
        m["b1ra"] = com0["b1r"]
        m["w2a"] = com0["w2"]
        m["b2a"] = com0["b2"]
        m["wr"] = com0["wr"]
        m["br"] = com0["br"]
        m["gmb"] = com0["gmb"]
        m["btb"] = com0["btb"]
        m["bmsk"] = bm.astype(NPBF)
        for key in ("wqkv", "bqkvr", "bqv", "lnwb", "lnbb", "wg"):
            m[key] = com1[key]
        m["w1b"] = com1["w1"]
        m["b1rb"] = com1["b1r"]
        m["w2b"] = com1["w2"]
        m["b2b"] = com1["b2"]
        m["cst_in"] = com0["cst_in"]
        m["iota8"] = com0["iota8"]
        m["idn"] = com0["idn"]
        m["on1"] = com0["on1"]
        in_maps.append(m)
    return in_maps


from concourse.bass_utils import run_bass_kernel_spmd

_NC_CACHE = {}


def _get_nc():
    if "f" not in _NC_CACHE:
        _NC_CACHE["f"] = build_fused()
    return _NC_CACHE["f"]


def kernel(**inputs):
    nc = _get_nc()
    im = host_fused_inmaps(inputs)
    res = run_bass_kernel_spmd(nc, im, core_ids=list(range(8)))
    out = np.empty((B, S, D), np.float32)
    for core in range(8):
        b, k = divmod(core, 4)
        out[b, 2048 * k:2048 * (k + 1)] = res.results[core]["x2o"]
    return out


# version 13
# speedup vs baseline: 1.1016x; 1.0640x over previous
import contextlib
import dataclasses
import numpy as np
import ml_dtypes
import concourse.bass as bass
import concourse.mybir as mybir

F32 = mybir.dt.float32
BF16 = mybir.dt.bfloat16
AF = mybir.ActivationFunctionType
ALU = mybir.AluOpType
AX = mybir.AxisListType
NPBF = ml_dtypes.bfloat16

B, S, D, FF, E = 2, 8192, 1024, 4096, 8
T = 128
DCH = D // T
NOWN = 16
NOT = NOWN * T
NCHUNK = 8
KC = 1024
NWIN = FF // 512

C_1024, C_EPS, C_NEG1, C_TINY, C_8192, C_ONE, C_NEGBIG, C_ZERO, C_32, C_512, C_DFF, C_W2S, C_256, C_W2H = range(14)
CONST_VALS = [1.0 / 1024, 1e-5, -1.0, 1e-30, 1.0 / 8192, 1.0, -1e9, 0.0,
              32.0, 512.0, 4194304.0, 524288.0, 256.0, 262144.0]
NCONST = len(CONST_VALS)

ENGS = ("sync", "gpsimd", "tensor", "vector", "scalar")


class Sem:
    def __init__(self, nc, ctx, name):
        self.h = ctx.enter_context(nc.semaphore(name))
        self.n = 0

    def inc(self, k=1):
        self.n += k
        return lambda inst, h=self.h, k=k: inst.then_inc(h, k)

    def at(self):
        return (self.h, self.n)


class Ph:
    def __init__(self, name):
        self.name = name
        self.ops = {e: [] for e in ENGS}

    def op(self, eng, fn):
        self.ops[eng].append(fn)

    def wait(self, eng, *marks):
        for m in marks:
            if m is None:
                continue
            h, v = m
            if v > 0:
                self.ops[eng].append(lambda e, h=h, v=v: e.wait_ge(h, v))

    def emit(self, nc):
        with nc.Block(self.name) as blk:
            for eng in ENGS:
                fns = self.ops[eng]
                if fns:
                    getattr(blk, eng)(lambda e, fns=fns: [f(e) for f in fns])


class Pool:
    def __init__(self, items):
        self.items = list(items)
        self.marks = [None] * len(self.items)
        self.i = 0

    def take(self, p, eng):
        j = self.i % len(self.items)
        self.i += 1
        p.wait(eng, self.marks[j])
        return j, self.items[j]

    def release(self, j, sem):
        self.marks[j] = sem.at()


_RING_N = [0]


class Ring:

    def __init__(self, nc, ctx, name, slots):
        _RING_N[0] += 1
        name = f"{name}_{_RING_N[0]}"
        self.slots = slots
        self.sems = [Sem(nc, ctx, f"{name}{i}") for i in range(len(slots))]
        self.marks = [None] * len(slots)
        self.i = 0

    def next(self, p, eng):
        j = self.i % len(self.slots)
        self.i += 1
        p.wait(eng, self.marks[j])
        return j, self.slots[j], self.sems[j]

    def release(self, j, sem):
        self.marks[j] = sem.at()


class L0:

    def __init__(self, dbg=False, nt0=NOWN):
        self.dbg = dbg
        self.NT0 = nt0
        self.NTT = nt0 * T
        nc = bass.Bass("TRN2", target_bir_lowering=False)
        self.nc = nc
        self.ctx = contextlib.ExitStack()
        self.ctx.__enter__()

        inp = lambda n, s, d=F32: nc.dram_tensor(n, s, d, kind="ExternalInput").ap()
        outp = lambda n, s, d=F32: nc.dram_tensor(n, s, d, kind="ExternalOutput").ap()
        self.x_seq = inp("x_seq", [S, D])
        self.x_own = inp("x_own", [self.NTT, D])
        self.cmask = inp("cmask", [self.NT0, NCHUNK, T, KC], BF16)
        self.wt = inp("wt", [D, 2 * D])
        self.w1 = inp("w1", [E, D, FF])
        self.b1r = inp("b1r", [T, E * (FF // T)])
        self.w2 = inp("w2", [E, FF, D])
        self.b2 = inp("b2", [E, D])
        self.wr = inp("wr", [D, E])
        self.br = inp("br", [1, E])
        self.gmb = inp("gmb", [T, D])
        self.btb = inp("btb", [T, D])
        self.iota8 = inp("iota8", [1, 16])
        self.cst_in = inp("cst_in", [T, NCONST])
        self.idn_in = inp("idn", [T, T], BF16)
        self.on1_in = inp("on1", [1, T], BF16)
        self.x1o = outp("x1o", [NOT, D])
        if dbg:
            self.d_l = outp("d_l", [T, NOWN])
            self.d_mr = outp("d_mr", [T, NOWN])
            self.d_s0 = outp("d_s0", [T, NOWN * D], BF16)
            self.d_cw = outp("d_cw", [T, E])
            self.d_rw = outp("d_rw", [1, 2 * E])
            self.d_y = outp("d_y", [T, NOWN * D])
            self.d_xnfm = outp("d_xnfm", [T, DCH * NOT], BF16)
        self.scr8 = nc.dram_tensor("scr8", [1, E], F32, kind="Internal").ap()

        c = self.ctx
        sb = lambda n, s, d=F32: c.enter_context(nc.sbuf_tensor(n, s, d))
        pt = lambda n, s, d=F32: c.enter_context(nc.psum_tensor(n, s, d))
        self.cst = sb("cst", [T, NCONST])
        self.idn = sb("idn_s", [T, T], BF16)
        self.on1 = sb("on1_s", [1, T], BF16)
        self.b2s = sb("b2s", [E, D], BF16)
        self.wrs = sb("wrs", [T, DCH * E], BF16)
        self.brs = sb("brs", [1, E], BF16)
        self.meanbf = sb("meanbf", [T, DCH], BF16)
        self.xnfm = sb("xnfm", [T, DCH * self.NTT], BF16)
        self.r_own = sb("r_own", [T, self.NT0])
        self.nm_own = sb("nm_own", [T, self.NT0])
        self.r_kv = sb("r_kv", [T, 16])
        self.nm_kv = sb("nm_kv", [T, 16])
        self.O = sb("O", [T, self.NT0 * D])
        self.lbuf = sb("lbuf", [T, self.NT0])
        self.mrun = sb("mrun", [T, self.NT0])
        self.rsum = sb("rsum", [T, DCH])
        self.rtmp = sb("rtmp", [T, DCH])
        self.tsc = sb("tsc", [T, 20])
        self.tscl = sb("tscl", [T, 48])
        self.sxs = sb("sxs", [T, 18])
        self.sqs = sb("sqs", [T, 18])
        self.cwb = sb("cwb", [T, E])
        self.rw = sb("rw", [1, 2 * E])
        self.rws = sb("rws", [1, 32])
        self.bsum_b = sb("bsum_b", [1, D], BF16)
        self.bsb = sb("bsb", [T, D])
        self.cw_bf = sb("cw_bf", [1, E], BF16)
        self.cwcol = sb("cwcol", [E, 1], BF16)
        self.io8 = sb("io8", [1, 16])
        self.eidx = sb("eidx", [1, 34], mybir.dt.int32)
        self.eoff = sb("eoff", [1, 34])
        self.cwb2 = sb("cwb2", [T, 2])
        self.scr2 = nc.dram_tensor("scr2", [1, 2], F32, kind="Internal").ap()

        self.psb = [pt(f"psf{i}", [T, 512]) for i in range(5)]
        self.pt2 = [pt(f"pt2_{i}", [T, 1024], BF16) for i in range(2)]
        self.ps_r = pt("ps_r", [T, 512])

        self.s_c0 = Sem(nc, c, "s_c0")
        self.s_g0 = Sem(nc, c, "s_g0")
        self.s_sc = Sem(nc, c, "s_sc")
        self.s_b1 = Sem(nc, c, "s_b1")
        self.s_gb = Sem(nc, c, "s_gb")
        self.s_cw = Sem(nc, c, "s_cw")
        self.s_ix = Sem(nc, c, "s_ix")
        self.pe = Sem(nc, c, "pe")
        self.dv = Sem(nc, c, "dv")
        self.ac = Sem(nc, c, "ac")
        self.so = Sem(nc, c, "so")

        self.bank = Pool(list(range(5)))
        self.tslot = Pool(list(range(2)))

    NMPfx = ""
    B2S_LATE = False

    def fbank(self, j):
        return self.psb[j] if j < 5 else self.ps_r

    def sbt(self, stack, n, s, d=F32):
        return stack.enter_context(
            self.nc.sbuf_tensor(self.NMPfx + n, s, d))

    def cc(self, col):
        return self.cst[:, col:col + 1]

    def dma(self, p, eng, sem, dst, src, guard=None, **kw):
        if guard is not None:
            p.wait(eng, guard)
        incf = sem.inc(16)
        p.op(eng, lambda e, dst=dst, src=src, incf=incf, kw=kw:
             incf(e.dma_start(out=dst, in_=src, **kw)))

    STRICT_EDGES = True

    def v(self, p, fn, inc=False):
        incf = self.dv.inc(1)
        p.op("vector", lambda e, fn=fn, incf=incf: incf(fn(e)))
        if self.STRICT_EDGES:
            h, vv = self.dv.at()
            p.op("vector", lambda e, h=h, vv=vv: e.wait_ge(h, vv))

    def vb(self, p, fn, inc=False):
        incf = self.dv.inc(1)
        p.op("vector", lambda e, fn=fn, incf=incf: incf(fn(e)))

    def ve(self, p, fn):
        incf = self.dv.inc(1)
        p.op("vector", lambda e, fn=fn, incf=incf: incf(fn(e)))
        h, vv = self.dv.at()
        p.op("vector", lambda e, h=h, vv=vv: e.wait_ge(h, vv))

    def a(self, p, fn, inc=False):
        incf = self.ac.inc(1)
        p.op("scalar", lambda e, fn=fn, incf=incf: incf(fn(e)))
        if self.STRICT_EDGES:
            h, vv = self.ac.at()
            p.op("scalar", lambda e, h=h, vv=vv: e.wait_ge(h, vv))

    def t(self, p, fn, inc=False):
        if inc:
            incf = self.pe.inc(1)
            p.op("tensor", lambda e, fn=fn, incf=incf: incf(fn(e)))
        else:
            p.op("tensor", fn)

    def ln_chain(self, p, lo, n, r_dst, nm_dst):
        sx = self.sxs[:, lo:lo + n]
        sq = self.sqs[:, lo:lo + n]
        mu = self.tscl[:, 0:n]
        var = self.tscl[:, 24:24 + n]
        r = r_dst[:, lo:lo + n]
        nm = nm_dst[:, lo:lo + n]
        self.ve(p, lambda e: e.tensor_scalar(mu, sx, self.cc(C_1024), None,
                                             ALU.mult))
        self.ve(p, lambda e: e.tensor_scalar(var, sq, self.cc(C_1024),
                                             self.cc(C_EPS), ALU.mult, ALU.add))
        self.ve(p, lambda e: e.tensor_mul(sq, mu, mu))
        self.ve(p, lambda e: e.tensor_sub(var, var, sq))
        p.wait("scalar", self.dv.at())
        self.a(p, lambda e: e.activation(r, var, AF.Sqrt), inc=True)
        p.wait("vector", self.ac.at())
        self.ve(p, lambda e: e.reciprocal(r, r))
        self.ve(p, lambda e: e.tensor_mul(nm, mu, r))
        self.ve(p, lambda e: e.tensor_scalar(nm, nm, self.cc(C_NEG1), None,
                                             ALU.mult))

    def stats_tile(self, p, xo, col):
        self.vb(p, lambda e: e.reduce_sum(self.sxs[:, col:col + 1], xo,
                                          axis=AX.X))
        self.vb(p, lambda e: e.tensor_mul(self.sqscr[:], xo, xo))
        self.v(p, lambda e: e.reduce_sum(self.sqs[:, col:col + 1],
                                         self.sqscr[:], axis=AX.X), inc=True)

    def fm_tile(self, p, xo, r_col, nm_col, fm_dst, col_fn):
        p.wait("vector", self.pe.at())
        self.vb(p, lambda e, xo=xo, r_col=r_col, nm_col=nm_col:
                e.tensor_scalar(self.xntm[:], xo, r_col, nm_col,
                                ALU.mult, ALU.add), inc=True)
        for dch in range(DCH):
            jt, q = self.tslot.take(p, "tensor")
            sub = self.pt2[q][:, 0:128]
            p.wait("tensor", self.dv.at())
            self.t(p, lambda e, dch=dch, sub=sub:
                   e.transpose(sub, self.xntm[:, dch * T:(dch + 1) * T],
                               self.idn[:]), inc=True)
            p.wait("scalar", self.pe.at())
            co = col_fn(dch)
            self.a(p, lambda e, sub=sub, fm_dst=fm_dst, co=co:
                   e.copy(fm_dst[:, co:co + T], sub), inc=True)
            self.tslot.release(jt, self.ac)

    def build(self, upto=6):
        nc = self.nc
        self.upto = upto
        self.p0()
        with contextlib.ExitStack() as s12:
            sb2 = lambda n, s, d=F32: self.sbt(s12, n, s, d)
            self.sqscr = sb2("sqscr", [T, D])
            self.wkv = sb2("wkv", [T, DCH * 2 * D], BF16)
            self.Ksb = sb2("Ksb", [T, DCH * KC], BF16)
            self.Vsb = sb2("Vsb", [T, 8 * D], BF16)
            self.kvfm = sb2("kvfm", [T, DCH * 512], BF16)
            self.xs = [sb2(f"xs{i}", [T, D]) for i in range(2)]
            self.xntm = sb2("xntm", [T, D], BF16)
            self.mks = [sb2(f"mk{i}", [T, KC], BF16) for i in range(3)]
            self.Psb = sb2("Psb", [T, 2 * KC], BF16)
            self.xsr = Ring(nc, self.ctx, "rxs", self.xs)
            self.mkr = Ring(nc, self.ctx, "rmk", self.mks)
            self.p1()
            if upto >= 2:
                self.p2()
        if upto < 2:
            return nc
        with contextlib.ExitStack() as s36:
            sb3 = lambda n, s, d=F32: self.sbt(s36, n, s, d)
            self.s0 = sb3("s0", [T, self.NT0 * D], BF16)
            self.xo2 = [sb3(f"xo2{i}", [T, D]) for i in range(2)]
            self.xo2r = Ring(nc, self.ctx, "rxo", self.xo2)
            self.xsc = sb3("xsc", [T, D])
            if upto >= 3:
                self.p3()
            if upto >= 4:
                pm = Ph("p4m")
                self.p4_main(pm)
                pm.emit(nc)
                self.p4()
            if upto < 5:
                return nc
            with contextlib.ExitStack() as s5:
                sb5 = lambda n, s, d=F32: self.sbt(s5, n, s, d)
                self.b1s = sb5("b1s", [T, E * (FF // T)])
                self.w1w = [sb5(f"w1w{i}", [T, DCH * 512], BF16) for i in range(2)]
                self.w2w = [sb5(f"w2w{i}", [T, 4 * D], BF16) for i in range(2)]
                self.w1r = Ring(nc, self.ctx, "rw1", self.w1w)
                self.w2r = Ring(nc, self.ctx, "rw2", self.w2w)
                self.hsb = [sb5(f"hsb{i}", [T, self.NTT], BF16)
                            for i in range(5)]
                self.p5()
            if upto < 6:
                return nc
            with contextlib.ExitStack() as s6:
                sb6 = lambda n, s, d=F32: self.sbt(s6, n, s, d)
                self.sqscr6 = sb6("sqscr6", [T, D])
                self.gscr = sb6("gscr", [T, D])
                self.gm = sb6("gm", [T, D])
                self.bt = sb6("bt", [T, D])
                self.p6()
        return nc

    def p0(self):
        p = Ph("p0")
        for dst, srca in [(self.cst, self.cst_in), (self.idn, self.idn_in),
                          (self.on1, self.on1_in), (self.io8, self.iota8)]:
            self.dma(p, "sync", self.s_c0, dst[:], srca)
        if not self.B2S_LATE:
            self.dma(p, "gpsimd", self.s_g0, self.b2s[:], self.b2)
        self.dma(p, "gpsimd", self.s_g0,
                 self.wrs[:].rearrange("p (c e) -> p c e", c=DCH),
                 self.wr.rearrange("(c p) e -> p c e", p=T, c=DCH))
        self.dma(p, "gpsimd", self.s_g0, self.brs[:], self.br)
        p.wait("vector", self.s_c0.at(), self.s_g0.at())
        p.wait("scalar", self.s_c0.at())
        p.wait("tensor", self.s_c0.at(), self.s_g0.at())
        p.emit(self.nc)

    def p1(self):
        p = Ph("p1")
        for i in range(self.NT0):
            j, xsl, xsem = self.xsr.next(p, "sync")
            self.dma(p, "sync", xsem, xsl[:],
                     self.x_own[i * T:(i + 1) * T, :])
            p.wait("vector", xsem.at())
            self.stats_tile(p, xsl[:], i)
            self.xsr.release(j, self.dv)
        self.ln_chain(p, 0, self.NT0, self.r_own, self.nm_own)
        for i in range(self.NT0):
            j, xsl, xsem = self.xsr.next(p, "sync")
            self.dma(p, "sync", xsem, xsl[:],
                     self.x_own[i * T:(i + 1) * T, :])
            p.wait("vector", xsem.at())
            self.fm_tile(p, xsl[:],
                         self.r_own[:, i:i + 1], self.nm_own[:, i:i + 1],
                         self.xnfm,
                         lambda dch, i=i: dch * self.NTT + i * T)
            self.xsr.release(j, self.dv)
        if self.dbg:
            p.wait("sync", self.ac.at())
            self.dma(p, "sync", self.so, self.d_xnfm, self.xnfm[:])
            p.wait("sync", self.so.at())
        p.emit(self.nc)

    def p2(self):
        nc = self.nc
        p = Ph("p2w")
        self.dma(p, "gpsimd", self.s_g0,
                 self.wkv[:].rearrange("p (c n) -> p c n", c=DCH),
                 self.wt.rearrange("(c p) n -> p c n", p=T, c=DCH))
        p.emit(nc)

        self.fpair = Pool([(0, 1), (2, 3), (4, 5)])
        self.fm_pre = []
        self.pv_mark = {}
        self.exp_mark = {}
        chunks = list(reversed(range(NCHUNK)))
        p = Ph("p2s0")
        for pr in range(4):
            self.pro_loadstats_part(p, chunks[0], 0, pr)
        p.emit(nc)
        for ci, c in enumerate(chunks):
            first = ci == 0
            par = ci % 2
            p = Ph(f"p2c{c}")
            mkq = {}

            def issue_mask(ii, p=p, c=c, mkq=mkq):
                jm, mkb, msem = self.mkr.next(p, "sync")
                self.dma(p, "sync", msem, mkb[:], self.cmask[ii, c, :, :])
                mkq[ii] = (jm, mkb, msem)
            issue_mask(0)
            if self.NT0 > 1:
                issue_mask(1)
            if first:
                p.wait("tensor", self.s_g0.at(), self.ac.at())
            self.pro_kv(p, c, par)
            if ci == NCHUNK - 1:
                kvm = (self.dv.at(), self.ac.at())
                self.p4_main(p)
            else:
                kvm = (self.dv.at(), self.ac.at())
            smq = None
            pvq = None
            for i in range(self.NT0):
                cur = self.flash_qk(p, i, c, first, mkq.pop(i), kvm)
                if smq is not None:
                    self.flash_sm(p, *smq)
                if i + 2 < self.NT0:
                    issue_mask(i + 2)
                if smq is not None:
                    if pvq is not None:
                        self.flash_pv(p, *pvq)
                    pvq = smq
                smq = cur
                if ci + 1 < NCHUNK:
                    if 4 <= i < 8:
                        self.pro_loadstats_part(p, chunks[ci + 1],
                                                (ci + 1) % 2, i - 4)
                    elif 10 <= i < 12:
                        self.pro_fm_load(p, chunks[ci + 1], i - 10)
            self.flash_sm(p, *smq)
            self.flash_pv(p, *pvq)
            self.flash_pv(p, *smq)
            p.emit(nc)
        for j in range(5):
            self.bank.marks[j] = self.dv.at()

    def pro_loadstats_part(self, p, c, par, pr):
        for t2 in range(2):
            tt = pr * 2 + t2
            j, xsl, xsem = self.xsr.next(p, "sync")
            r0 = c * KC + tt * T
            self.dma(p, "sync", xsem, xsl[:], self.x_seq[r0:r0 + T, :])
            p.wait("vector", xsem.at())
            self.stats_tile(p, xsl[:], par * 8 + tt)
            self.xsr.release(j, self.dv)

    def pro_fm_load(self, p, c, tt):
        j, xsl, xsem = self.xsr.next(p, "sync")
        r0 = c * KC + tt * T
        self.dma(p, "sync", xsem, xsl[:], self.x_seq[r0:r0 + T, :])
        self.fm_pre.append((j, xsl, xsem))

    def pro_kv(self, p, c, par):
        self.ln_chain(p, par * 8, 8, self.r_kv, self.nm_kv)
        for pr in range(4):
            sub = pr // 2
            tin = pr % 2
            for t2 in range(2):
                tt = pr * 2 + t2
                if self.fm_pre:
                    j, xsl, xsem = self.fm_pre.pop(0)
                else:
                    j, xsl, xsem = self.xsr.next(p, "sync")
                    r0 = c * KC + tt * T
                    self.dma(p, "sync", xsem, xsl[:],
                             self.x_seq[r0:r0 + T, :])
                p.wait("vector", xsem.at())
                self.fm_tile(p, xsl[:],
                             self.r_kv[:, par * 8 + tt:par * 8 + tt + 1],
                             self.nm_kv[:, par * 8 + tt:par * 8 + tt + 1],
                             self.kvfm,
                             lambda dch, tin=tin, t2=t2:
                             dch * 512 + (tin * 2 + t2) * T)
                self.xsr.release(j, self.dv)
            if pr % 2 == 1:
                self.kv_proj(p, c, sub)

    def kv_proj(self, p, c, sub):
        p.wait("vector", self.ac.at())
        for dch in range(DCH):
            self.v(p, lambda e, dch=dch: e.reduce_sum(
                self.rtmp[:, dch:dch + 1],
                self.kvfm[:, dch * 512:(dch + 1) * 512], axis=AX.X))
        if c == NCHUNK - 1 and sub == 0:
            self.ve(p, lambda e: e.tensor_copy(self.rsum[:], self.rtmp[:]))
        else:
            self.ve(p, lambda e: e.tensor_add(self.rsum[:], self.rsum[:],
                                              self.rtmp[:]))
        p.wait("tensor", self.ac.at())
        jp, pr_b = self.fpair.take(p, "tensor")
        A, B = self.fbank(pr_b[0]), self.fbank(pr_b[1])
        mAB = [None, None]
        for dk in range(DCH):
            bk = A if dk % 2 == 0 else B
            if mAB[dk % 2] is not None:
                p.wait("tensor", mAB[dk % 2])
            for dn in range(DCH):
                self.t(p, lambda e, dn=dn, dk=dk, bk=bk:
                       e.matmul(bk[:],
                                self.wkv[:, dn * 2048 + dk * T:
                                          dn * 2048 + dk * T + T],
                                self.kvfm[:, dn * 512:(dn + 1) * 512],
                                start=(dn == 0), stop=(dn == DCH - 1)),
                       inc=(dn == DCH - 1))
            p.wait("vector", self.pe.at())
            self.vb(p, lambda e, bk=bk, dk=dk:
                    e.tensor_copy(self.Ksb[:, dk * KC + sub * 512:
                                           dk * KC + sub * 512 + 512],
                                  bk[:]),
                    inc=True)
            mAB[dk % 2] = self.dv.at()
        for t4 in range(4):
            ks = sub * 4 + t4
            p.wait("tensor", mAB[0], mAB[1])
            for dn in range(DCH):
                self.t(p, lambda e, dn=dn, t4=t4, A=A:
                       e.matmul(A[:],
                                self.kvfm[:, dn * 512 + t4 * T:
                                          dn * 512 + t4 * T + T],
                                self.wkv[:, dn * 2048 + 1024:dn * 2048 + 1536],
                                start=(dn == 0), stop=(dn == DCH - 1)))
                self.t(p, lambda e, dn=dn, t4=t4, B=B:
                       e.matmul(B[:],
                                self.kvfm[:, dn * 512 + t4 * T:
                                          dn * 512 + t4 * T + T],
                                self.wkv[:, dn * 2048 + 1536:dn * 2048 + 2048],
                                start=(dn == 0), stop=(dn == DCH - 1)),
                       inc=(dn == DCH - 1))
            p.wait("vector", self.pe.at())
            self.vb(p, lambda e, A=A, ks=ks:
                    e.tensor_copy(self.Vsb[:, ks * D:ks * D + 512],
                                  A[:]))
            self.vb(p, lambda e, B=B, ks=ks:
                    e.tensor_copy(self.Vsb[:, ks * D + 512:ks * D + 1024],
                                  B[:]), inc=True)
            mAB[0] = mAB[1] = self.dv.at()
        self.fpair.release(jp, self.dv)

    def flash_qk(self, p, i, c, first, mkh, kvm):
        jp, pr_b = self.fpair.take(p, "tensor")
        s0b, s1b = pr_b
        if i == 0:
            p.wait("tensor", *kvm)
        for s2, bi in ((0, s0b), (1, s1b)):
            for dn in range(DCH):
                self.t(p, lambda e, dn=dn, s2=s2, bi=bi, i=i:
                       e.matmul(self.fbank(bi)[:],
                                self.xnfm[:, dn * self.NTT + i * T:
                                          dn * self.NTT + i * T + T],
                                self.Ksb[:, dn * KC + s2 * 512:
                                         dn * KC + s2 * 512 + 512],
                                start=(dn == 0), stop=(dn == DCH - 1)),
                       inc=(dn == DCH - 1))
        qk_pe = self.pe.at()
        return (i, c, first, jp, s0b, s1b, mkh, qk_pe)

    def flash_sm(self, p, i, c, first, jp, s0b, s1b, mkh, qk_pe):
        par = i % 2
        negmn = self.tsc[:, 0 + par:1 + par]
        dold = self.tsc[:, 2 + par:3 + par]
        alpha = self.tsc[:, 4 + par:5 + par]
        lbs = self.tsc[:, 6 + par:7 + par]
        bm = self.tsc[:, 8 + par:9 + par]
        bm2 = self.tsc[:, 10 + 2 * par:12 + 2 * par]
        lb = self.tsc[:, 16 + 2 * par:18 + 2 * par]
        mr = self.mrun[:, i:i + 1]
        lc = self.lbuf[:, i:i + 1]
        oc = self.O[:, i * D:(i + 1) * D]
        Pv = self.Psb[:, par * KC:(par + 1) * KC]
        jm, mk, msem = mkh

        p.wait("vector", msem.at(), qk_pe)
        self.vb(p, lambda e, s0b=s0b, mk=mk: e.tensor_add(
            self.fbank(s0b)[:], self.fbank(s0b)[:], mk[:, 0:512]))
        self.vb(p, lambda e, s1b=s1b, mk=mk: e.tensor_add(
            self.fbank(s1b)[:], self.fbank(s1b)[:], mk[:, 512:1024]),
                inc=True)
        self.mkr.release(jm, self.dv)
        self.vb(p, lambda e, s0b=s0b, bm2=bm2: e.reduce_max(
            bm2[:, 0:1], self.fbank(s0b)[:], axis=AX.X))
        self.vb(p, lambda e, s1b=s1b, bm2=bm2: e.reduce_max(
            bm2[:, 1:2], self.fbank(s1b)[:], axis=AX.X))
        self.v(p, lambda e, bm=bm, bm2=bm2: e.reduce_max(bm, bm2, axis=AX.X))
        if first:
            self.ve(p, lambda e, mr=mr, bm=bm: e.tensor_copy(mr, bm))
        else:
            self.ve(p, lambda e, dold=dold, mr=mr, bm=bm:
                    e.tensor_sub(dold, mr, bm))
            self.ve(p, lambda e, dold=dold: e.tensor_scalar(
                dold, dold, self.cc(C_ZERO), None, ALU.min))
            self.ve(p, lambda e, mr=mr, bm=bm: e.tensor_max(mr, mr, bm))
        self.ve(p, lambda e, negmn=negmn, mr=mr: e.tensor_scalar(
            negmn, mr, self.cc(C_NEG1), None, ALU.mult))
        p.wait("scalar", self.dv.at())
        pvm = self.pv_mark.get(par)
        if pvm is not None:
            p.wait("scalar", pvm)
        if not first:
            self.a(p, lambda e, alpha=alpha, dold=dold:
                   e.activation(alpha, dold, AF.Exp))
        self.a(p, lambda e, s0b=s0b, negmn=negmn, lb=lb, Pv=Pv: e.activation(
            Pv[:, 0:512], self.fbank(s0b)[:], AF.Exp,
            bias=negmn, accum_out=lb[:, 0:1]))
        self.a(p, lambda e, s1b=s1b, negmn=negmn, lb=lb, Pv=Pv: e.activation(
            Pv[:, 512:1024], self.fbank(s1b)[:], AF.Exp,
            bias=negmn, accum_out=lb[:, 1:2]), inc=True)
        self.exp_mark[par] = self.ac.at()

    def flash_pv(self, p, i, c, first, jp, s0b, s1b, mkh, qk_pe):
        par = i % 2
        alpha = self.tsc[:, 4 + par:5 + par]
        lbs = self.tsc[:, 6 + par:7 + par]
        lb = self.tsc[:, 16 + 2 * par:18 + 2 * par]
        lc = self.lbuf[:, i:i + 1]
        oc = self.O[:, i * D:(i + 1) * D]
        Pv = self.Psb[:, par * KC:(par + 1) * KC]
        exp_m = self.exp_mark[par]
        p.wait("vector", exp_m)
        self.v(p, lambda e, lbs=lbs, lb=lb: e.reduce_sum(lbs, lb, axis=AX.X))
        if first:
            self.ve(p, lambda e, lc=lc, lbs=lbs: e.tensor_copy(lc, lbs))
        else:
            self.ve(p, lambda e, lc=lc, alpha=alpha, lbs=lbs:
                    e.scalar_tensor_tensor(lc, lc, alpha, lbs,
                                           ALU.mult, ALU.add))
            self.vb(p, lambda e, oc=oc, alpha=alpha: e.tensor_scalar(
                oc[:, 0:512], oc[:, 0:512], alpha, None, ALU.mult))
            self.vb(p, lambda e, oc=oc, alpha=alpha: e.tensor_scalar(
                oc[:, 512:1024], oc[:, 512:1024], alpha, None, ALU.mult),
                    inc=True)

        copy_marks = [None] * 8
        tsl = [None] * 8

        def emit_t(k):
            jt, q = self.tslot.take(p, "tensor")
            tsl[k] = jt
            sub = self.pt2[q][:, 0:128]
            p.wait("tensor", exp_m)
            self.t(p, lambda e, k=k, sub=sub, Pv=Pv:
                   e.transpose(sub, Pv[:, k * T:(k + 1) * T],
                               self.idn[:]), inc=True)
            p.wait("scalar", self.pe.at())
            self.a(p, lambda e, k=k, sub=sub, Pv=Pv:
                   e.copy(Pv[:, k * T:(k + 1) * T], sub), inc=True)
            self.tslot.release(tsl[k], self.ac)
            copy_marks[k] = self.ac.at()

        emit_t(0)
        emit_t(1)
        for k in range(8):
            if k + 2 < 8:
                emit_t(k + 2)
            p.wait("tensor", copy_marks[k])
            self.t(p, lambda e, k=k, s0b=s0b, Pv=Pv:
                   e.matmul(self.fbank(s0b)[:], Pv[:, k * T:(k + 1) * T],
                            self.Vsb[:, k * D:k * D + 512],
                            start=(k == 0), stop=(k == 7)))
            self.t(p, lambda e, k=k, s1b=s1b, Pv=Pv:
                   e.matmul(self.fbank(s1b)[:], Pv[:, k * T:(k + 1) * T],
                            self.Vsb[:, k * D + 512:k * D + 1024],
                            start=(k == 0), stop=(k == 7)),
                   inc=(k == 7))
        self.pv_mark[par] = self.pe.at()
        p.wait("vector", self.pe.at())
        if first:
            self.vb(p, lambda e, s0b=s0b, oc=oc:
                    e.tensor_copy(oc[:, 0:512], self.fbank(s0b)[:]))
            self.vb(p, lambda e, s1b=s1b, oc=oc:
                    e.tensor_copy(oc[:, 512:1024], self.fbank(s1b)[:]),
                    inc=True)
        else:
            self.vb(p, lambda e, s0b=s0b, oc=oc:
                    e.tensor_add(oc[:, 0:512], oc[:, 0:512],
                                 self.fbank(s0b)[:]))
            self.vb(p, lambda e, s1b=s1b, oc=oc:
                    e.tensor_add(oc[:, 512:1024], oc[:, 512:1024],
                                 self.fbank(s1b)[:]), inc=True)
        self.fpair.release(jp, self.dv)

    def p3(self):
        p = Ph("p3")
        self.ve(p, lambda e: e.tensor_scalar(self.lbuf[:], self.lbuf[:],
                                             self.cc(C_TINY), None, ALU.max))
        self.ve(p, lambda e: e.reciprocal(self.lbuf[:], self.lbuf[:]))
        if self.dbg:
            p.wait("sync", self.dv.at())
            self.dma(p, "sync", self.so, self.d_l, self.lbuf[:])
            self.dma(p, "sync", self.so, self.d_mr, self.mrun[:])
        for i in range(self.NT0):
            j, xo, xsem = self.xo2r.next(p, "sync")
            self.dma(p, "sync", xsem, xo[:], self.x_own[i * T:(i + 1) * T, :])
            p.wait("vector", xsem.at())
            self.v(p, lambda e, xo=xo, i=i: e.tensor_scalar(
                self.xsc[:], xo[:], self.r_own[:, i:i + 1],
                self.nm_own[:, i:i + 1], ALU.mult, ALU.add))
            self.v(p, lambda e, i=i: e.scalar_tensor_tensor(
                self.s0[:, i * D:(i + 1) * D], self.O[:, i * D:(i + 1) * D],
                self.lbuf[:, i:i + 1], self.xsc[:], ALU.mult, ALU.add),
                   inc=True)
            self.xo2r.release(j, self.dv)
        if self.dbg:
            p.wait("sync", self.dv.at())
            self.dma(p, "sync", self.so, self.d_s0, self.s0[:])
        p.emit(self.nc)

    def p4_main(self, p):
        R = self.rws
        c1 = lambda col: self.cst[0:1, col:col + 1]
        self.ve(p, lambda e: e.tensor_scalar(self.rtmp[:], self.rsum[:],
                                             self.cc(C_8192), None, ALU.mult))
        self.ve(p, lambda e: e.tensor_copy(self.meanbf[:], self.rtmp[:]))
        p.wait("tensor", self.dv.at())
        jpr, prb_r = self.fpair.take(p, "tensor")
        rps = self.fbank(prb_r[0])
        for ch in range(DCH):
            self.t(p, lambda e, ch=ch, rps=rps: e.matmul(
                rps[0:1, 0:E], self.meanbf[:, ch:ch + 1],
                self.wrs[:, ch * E:(ch + 1) * E],
                start=(ch == 0), stop=False))
        self.t(p, lambda e, rps=rps: e.matmul(rps[0:1, 0:E],
                                              self.on1[0:1, 0:1],
                                              self.brs[0:1, :],
                                              start=False, stop=True),
               inc=True)
        p.wait("vector", self.pe.at())
        self.ve(p, lambda e, rps=rps: e.reduce_max(R[:, 24:25],
                                                   rps[0:1, 0:E],
                                                   axis=AX.X))
        self.ve(p, lambda e: e.tensor_scalar(R[:, 25:26], R[:, 24:25],
                                             c1(C_NEG1), None, ALU.mult))
        p.wait("scalar", self.dv.at())
        self.a(p, lambda e, rps=rps: e.activation(self.rw[:, 0:E],
                                                  rps[0:1, 0:E],
                                                  AF.Exp, bias=R[:, 25:26],
                                                  accum_out=R[:, 26:27]),
               inc=True)
        self.fpair.release(jpr, self.ac)
        p.wait("vector", self.ac.at())
        self.ve(p, lambda e: e.reciprocal(R[:, 27:28], R[:, 26:27]))
        self.ve(p, lambda e: e.tensor_scalar(self.rw[:, 0:E], self.rw[:, 0:E],
                                             R[:, 27:28], None, ALU.mult))
        self.ve(p, lambda e: e.reduce_max(R[:, 24:25], self.rw[:, 0:E],
                                          axis=AX.X))
        self.ve(p, lambda e: e.tensor_scalar(R[:, 0:E], self.rw[:, 0:E],
                                             R[:, 24:25], None, ALU.is_ge))
        self.ve(p, lambda e: e.scalar_tensor_tensor(R[:, 8:16], R[:, 0:E],
                                                    c1(C_NEGBIG),
                                                    self.rw[:, 0:E],
                                                    ALU.mult, ALU.add))
        self.ve(p, lambda e: e.reduce_max(R[:, 25:26], R[:, 8:16], axis=AX.X))
        self.ve(p, lambda e: e.tensor_scalar(R[:, 16:24], R[:, 8:16],
                                             R[:, 25:26], None, ALU.is_ge))
        self.ve(p, lambda e: e.tensor_sub(R[:, 26:27], R[:, 25:26],
                                          R[:, 24:25]))
        p.wait("scalar", self.dv.at())
        self.a(p, lambda e: e.activation(R[:, 27:28], R[:, 26:27], AF.Exp),
               inc=True)
        p.wait("vector", self.ac.at())
        self.ve(p, lambda e: e.tensor_scalar(R[:, 28:29], R[:, 27:28],
                                             c1(C_ONE), None, ALU.add))
        self.ve(p, lambda e: e.reciprocal(R[:, 29:30], R[:, 28:29]))
        self.ve(p, lambda e: e.tensor_mul(R[:, 30:31], R[:, 27:28],
                                          R[:, 29:30]))
        self.ve(p, lambda e: e.tensor_scalar(self.rw[:, 8:16], R[:, 0:E],
                                             R[:, 29:30], None, ALU.mult))
        self.ve(p, lambda e: e.scalar_tensor_tensor(self.rw[:, 8:16],
                                                    R[:, 16:24], R[:, 30:31],
                                                    self.rw[:, 8:16],
                                                    ALU.mult, ALU.add))
        R2 = self.rws
        c1r = lambda col: self.cst[0:1, col:col + 1]
        EO = self.eoff
        self.ve(p, lambda e: e.tensor_copy(R2[:, 26:27], R2[:, 29:30]))
        self.ve(p, lambda e: e.tensor_copy(R2[:, 27:28], R2[:, 30:31]))
        self.ve(p, lambda e: e.tensor_mul(R2[:, 0:E], R2[:, 0:E],
                                          self.io8[0:1, 0:E]))
        self.ve(p, lambda e: e.reduce_sum(R2[:, 24:25], R2[:, 0:E],
                                          axis=AX.X))
        self.ve(p, lambda e: e.tensor_mul(R2[:, 16:24], R2[:, 16:24],
                                          self.io8[0:1, 0:E]))
        self.ve(p, lambda e: e.reduce_sum(R2[:, 25:26], R2[:, 16:24],
                                          axis=AX.X))
        self.ve(p, lambda e: e.tensor_scalar(EO[:, 0:2], R2[:, 24:26],
                                             c1r(C_32), None, ALU.mult))
        self.ve(p, lambda e: e.tensor_scalar(R2[:, 12:14], R2[:, 24:26],
                                             c1r(C_DFF), None, ALU.mult))
        self.ve(p, lambda e: e.tensor_scalar(EO[:, 2:10], self.io8[0:1, 0:E],
                                             c1r(C_512), None, ALU.mult))
        self.ve(p, lambda e: e.tensor_scalar(EO[:, 10:18], EO[:, 2:10],
                                             R2[:, 13:14], None, ALU.add))
        self.ve(p, lambda e: e.tensor_scalar(EO[:, 2:10], EO[:, 2:10],
                                             R2[:, 12:13], None, ALU.add))
        self.ve(p, lambda e: e.tensor_scalar(EO[:, 18:26], self.io8[0:1, 0:E],
                                             c1r(C_W2S), None, ALU.mult))
        self.ve(p, lambda e: e.tensor_scalar(EO[:, 26:34], EO[:, 18:26],
                                             R2[:, 13:14], None, ALU.add))
        self.ve(p, lambda e: e.tensor_scalar(EO[:, 18:26], EO[:, 18:26],
                                             R2[:, 12:13], None, ALU.add))
        self.ve(p, lambda e: e.tensor_copy(self.eidx[0:1, :], EO[0:1, :]))
        self.eidx_mark = self.dv.at()

        p.wait("gpsimd", self.dv.at())
        self.dma(p, "gpsimd", self.s_sc, self.scr2, R2[:, 26:28])
        p.wait("gpsimd", self.s_sc.at())
        self.dma(p, "gpsimd", self.s_sc, self.cwb2[:],
                 self.scr2.broadcast_to([T, 2]))
        self.dma(p, "gpsimd", self.s_sc, self.scr8, self.rw[:, 8:16])
        p.wait("gpsimd", self.s_sc.at())
        self.dma(p, "gpsimd", self.s_sc, self.cwb[:],
                 self.scr8.broadcast_to([T, E]))
        if self.dbg:
            self.dma(p, "sync", self.so, self.d_rw, self.rw[:])
            p.wait("gpsimd", self.s_sc.at())
            self.dma(p, "sync", self.so, self.d_cw, self.cwb[:])
        self.dma(p, "gpsimd", self.s_cw, self.cwcol[:],
                 self.scr8.rearrange("o e -> e o"), guard=self.s_sc.at())

    def p4(self):
        p = Ph("p4t")
        if self.B2S_LATE:
            self.dma(p, "gpsimd", self.s_cw, self.b2s[:], self.b2)
        p.wait("tensor", self.s_sc.at(), self.s_cw.at())
        jb0, b0 = self.bank.take(p, "tensor")
        jb1, b1 = self.bank.take(p, "tensor")
        self.t(p, lambda e, b0=b0: e.matmul(self.psb[b0][0:1, :],
                                            self.cwcol[:], self.b2s[:, 0:512],
                                            start=True, stop=True))
        self.t(p, lambda e, b1=b1: e.matmul(self.psb[b1][0:1, :],
                                            self.cwcol[:],
                                            self.b2s[:, 512:1024],
                                            start=True, stop=True), inc=True)
        p.wait("scalar", self.pe.at())
        self.a(p, lambda e, b0=b0: e.copy(self.bsum_b[:, 0:512],
                                          self.psb[b0][0:1, :]))
        self.a(p, lambda e, b1=b1: e.copy(self.bsum_b[:, 512:1024],
                                          self.psb[b1][0:1, :]), inc=True)
        self.bank.release(jb0, self.ac)
        self.bank.release(jb1, self.ac)
        p.wait("tensor", self.ac.at())
        jc0, cb0 = self.bank.take(p, "tensor")
        jc1, cb1 = self.bank.take(p, "tensor")
        self.t(p, lambda e, cb0=cb0: e.matmul(self.psb[cb0][:],
                                              self.on1[0:1, :],
                                              self.bsum_b[:, 0:512],
                                              start=True, stop=True))
        self.t(p, lambda e, cb1=cb1: e.matmul(self.psb[cb1][:],
                                              self.on1[0:1, :],
                                              self.bsum_b[:, 512:1024],
                                              start=True, stop=True), inc=True)
        p.wait("vector", self.pe.at())
        self.v(p, lambda e, cb0=cb0: e.tensor_copy(self.bsb[:, 0:512],
                                                   self.psb[cb0][:]))
        self.v(p, lambda e, cb1=cb1: e.tensor_copy(self.bsb[:, 512:1024],
                                                   self.psb[cb1][:]),
               inc=True)
        self.bank.release(jc0, self.dv)
        self.bank.release(jc1, self.dv)
        p.emit(self.nc)

    def p5(self):
        p = Ph("p5")
        p.wait("gpsimd", self.eidx_mark)
        w1v = self.w1.rearrange("e (c p) f -> e p c f", p=T, c=DCH)[0]
        w2v = self.w2.rearrange("e (c p) d -> e p c d", p=T, c=FF // T)[0]
        b1v = self.b1r
        for s_i in range(2):
            incf = self.s_ix.inc(16)
            def g_b1(e, s_i=s_i, incf=incf):
                v = e.value_load(self.eidx[0:1, s_i:s_i + 1])
                srcb = dataclasses.replace(b1v[:, 0:32], offset=v)
                incf(e.dma_start(out=self.b1s[:, s_i * 32:(s_i + 1) * 32],
                                 in_=srcb))
                e.free_register(v.val)
            p.op("gpsimd", g_b1)
        p.wait("scalar", self.s_ix.at())
        hpool = Pool(list(range(len(self.hsb))))
        for e_i in range(2):
            for w in range(NWIN):
                firstw = (e_i == 0 and w == 0)
                j1, w1s, w1sem = self.w1r.next(p, "gpsimd")
                incf1 = w1sem.inc(16)
                def g_w1(e, e_i=e_i, w=w, w1s=w1s, incf1=incf1):
                    v = e.value_load(
                        self.eidx[0:1, 2 + e_i * 8 + w:3 + e_i * 8 + w])
                    src1 = dataclasses.replace(w1v[:, :, 0:512], offset=v)
                    incf1(e.dma_start(
                        out=w1s[:].rearrange("p (c f) -> p c f", c=DCH),
                        in_=src1))
                    e.free_register(v.val)
                p.op("gpsimd", g_w1)
                j2, w2s, w2sem = self.w2r.next(p, "gpsimd")
                incf2 = w2sem.inc(16)
                def g_w2(e, e_i=e_i, w=w, w2s=w2s, incf2=incf2):
                    v = e.value_load(
                        self.eidx[0:1, 18 + e_i * 8 + w:19 + e_i * 8 + w])
                    src2 = dataclasses.replace(w2v[:, 0:4, :], offset=v)
                    incf2(e.dma_start(
                        out=w2s[:].rearrange("p (c d) -> p c d", c=4),
                        in_=src2))
                    e.free_register(v.val)
                p.op("gpsimd", g_w2)
                hslots = []
                p.wait("tensor", w1sem.at(), w2sem.at())
                twins = [(g, min(512, self.NTT - g))
                         for g in range(0, self.NTT, 512)]
                for fc in range(4):
                    jh, hi = hpool.take(p, "scalar")
                    hslots.append(hi)
                    hs = self.hsb[hi]
                    for g0, gn in twins:
                        jb, bi = self.bank.take(p, "tensor")
                        bk = self.psb[bi]
                        for dn in range(DCH):
                            self.t(p, lambda e, dn=dn, fc=fc, g0=g0, gn=gn,
                                   w1s=w1s, bk=bk:
                                   e.matmul(bk[:, 0:gn],
                                            w1s[:,
                                                dn * 512 + fc * T:
                                                dn * 512 + fc * T + T],
                                            self.xnfm[:,
                                                      dn * self.NTT + g0:
                                                      dn * self.NTT + g0 + gn],
                                            start=(dn == 0),
                                            stop=(dn == DCH - 1)),
                                   inc=(dn == DCH - 1))
                        p.wait("scalar", self.pe.at())
                        bcol = self.b1s[:, e_i * 32 + w * 4 + fc:
                                        e_i * 32 + w * 4 + fc + 1]
                        self.a(p, lambda e, bk=bk, hs=hs, g0=g0, gn=gn,
                               bcol=bcol:
                               e.activation(hs[:, g0:g0 + gn],
                                            bk[:, 0:gn], AF.Gelu, bias=bcol),
                               inc=True)
                        self.bank.release(jb, self.ac)
                for i in range(self.NT0):
                    jy0, y0 = self.bank.take(p, "tensor")
                    jy1, y1 = self.bank.take(p, "tensor")
                    p.wait("tensor", self.ac.at())
                    for fc in range(4):
                        hs = self.hsb[hslots[fc]]
                        self.t(p, lambda e, fc=fc, i=i, w2s=w2s, hs=hs, y0=y0:
                               e.matmul(self.psb[y0][:],
                                        hs[:, i * T:(i + 1) * T],
                                        w2s[:, fc * D:fc * D + 512],
                                        start=(fc == 0), stop=(fc == 3)))
                        self.t(p, lambda e, fc=fc, i=i, w2s=w2s, hs=hs, y1=y1:
                               e.matmul(self.psb[y1][:],
                                        hs[:, i * T:(i + 1) * T],
                                        w2s[:, fc * D + 512:
                                            fc * D + 1024],
                                        start=(fc == 0), stop=(fc == 3)),
                               inc=(fc == 3))
                    p.wait("vector", self.pe.at())
                    oc = self.O[:, i * D:(i + 1) * D]
                    cwc = self.cwb2[:, e_i:e_i + 1]
                    if firstw:
                        self.vb(p, lambda e, y0=y0, oc=oc, cwc=cwc:
                                e.tensor_scalar(oc[:, 0:512],
                                                self.psb[y0][:],
                                                cwc, None, ALU.mult))
                        self.vb(p, lambda e, y1=y1, oc=oc, cwc=cwc:
                                e.tensor_scalar(oc[:, 512:1024],
                                                self.psb[y1][:],
                                                cwc, None, ALU.mult),
                                inc=True)
                    else:
                        self.vb(p, lambda e, y0=y0, oc=oc, cwc=cwc:
                                e.scalar_tensor_tensor(oc[:, 0:512],
                                                       self.psb[y0][:], cwc,
                                                       oc[:, 0:512],
                                                       ALU.mult, ALU.add))
                        self.vb(p, lambda e, y1=y1, oc=oc, cwc=cwc:
                                e.scalar_tensor_tensor(oc[:, 512:1024],
                                                       self.psb[y1][:], cwc,
                                                       oc[:, 512:1024],
                                                       ALU.mult, ALU.add),
                                inc=True)
                    self.bank.release(jy0, self.dv)
                    self.bank.release(jy1, self.dv)
                for hi in hslots:
                    hpool.release(hi, self.pe)
                self.w1r.release(j1, self.pe)
                self.w2r.release(j2, self.pe)
        if self.dbg:
            p.wait("sync", self.dv.at())
            self.dma(p, "sync", self.so, self.d_y[:, 0:NOWN * D // 2],
                     self.O[:, 0:NOWN * D // 2])
            self.dma(p, "sync", self.so, self.d_y[:, NOWN * D // 2:],
                     self.O[:, NOWN * D // 2:])
            p.wait("sync", self.so.at())
        p.emit(self.nc)

    def p6(self):
        p = Ph("p6")
        self.dma(p, "sync", self.s_gb, self.gm[:], self.gmb)
        self.dma(p, "sync", self.s_gb, self.bt[:], self.btb)
        p.wait("vector", self.s_gb.at())
        for i in range(self.NT0):
            j, xo, xsem = self.xo2r.next(p, "sync")
            self.dma(p, "sync", xsem, xo[:],
                     self.x_own[i * T:(i + 1) * T, :])
            p.wait("vector", xsem.at())
            oc = self.O[:, i * D:(i + 1) * D]
            self.v(p, lambda e, xo=xo, i=i: e.tensor_scalar(
                self.sqscr6[:], xo[:], self.r_own[:, i:i + 1],
                self.nm_own[:, i:i + 1], ALU.mult, ALU.add))
            self.v(p, lambda e: e.tensor_add(self.sqscr6[:], self.sqscr6[:],
                                             self.bsb[:]))
            self.v(p, lambda e, oc=oc: e.tensor_add(oc, oc, self.sqscr6[:]),
                   inc=True)
            self.xo2r.release(j, self.dv)
            self.v(p, lambda e, oc=oc, i=i: e.reduce_sum(
                self.sxs[:, i:i + 1], oc, axis=AX.X))
            self.v(p, lambda e, oc=oc: e.tensor_mul(self.sqscr6[:], oc, oc))
            self.v(p, lambda e, i=i: e.reduce_sum(self.sqs[:, i:i + 1],
                                                  self.sqscr6[:], axis=AX.X),
                   inc=True)
        for lo in range(0, self.NT0, 8):
            self.ln_chain(p, lo, min(8, self.NT0 - lo),
                          self.r_own, self.nm_own)
        for i in range(self.NT0):
            oc = self.O[:, i * D:(i + 1) * D]
            sc = self.s0[:, i * D:(i + 1) * D]
            self.v(p, lambda e, oc=oc, i=i: e.tensor_scalar(
                self.gscr[:], oc, self.r_own[:, i:i + 1],
                self.nm_own[:, i:i + 1], ALU.mult, ALU.add))
            self.v(p, lambda e: e.tensor_mul(self.gscr[:], self.gscr[:],
                                             self.gm[:]))
            self.v(p, lambda e: e.tensor_add(self.gscr[:], self.gscr[:],
                                             self.bt[:]), inc=True)
            p.wait("scalar", self.dv.at())
            self.a(p, lambda e: e.activation(self.sqscr6[:], self.gscr[:],
                                             AF.Silu), inc=True)
            p.wait("vector", self.ac.at())
            self.v(p, lambda e, sc=sc: e.tensor_mul(sc, self.sqscr6[:], sc),
                   inc=True)
            j, xo, xsem = self.xo2r.next(p, "sync")
            p.wait("sync", self.so.at())
            self.dma(p, "sync", xsem, xo[:],
                     self.x_own[i * T:(i + 1) * T, :])
            p.wait("vector", xsem.at())
            self.v(p, lambda e, xo=xo, sc=sc: e.tensor_add(xo[:], xo[:], sc),
                   inc=True)
            p.wait("sync", self.dv.at())
            self.dma(p, "sync", self.so, self.x1o[i * T:(i + 1) * T, :], xo[:])
        p.wait("sync", self.so.at())
        p.emit(self.nc)


def build_l0(dbg=False, upto=6):
    k = L0(dbg=dbg)
    nc = k.build(upto=upto)
    k.ctx.__exit__(None, None, None)
    return nc


def host_common(inputs):
    f32 = lambda a: np.ascontiguousarray(np.asarray(a, np.float32))
    d = {}
    d["wt"] = f32(np.asarray(inputs["L0_attn_W"], np.float32).T)
    d["w1"] = f32(inputs["L0_ffn_W1"])
    b1 = np.asarray(inputs["L0_ffn_b1"], np.float32)
    d["b1r"] = f32(b1.reshape(E, FF // T, T).transpose(2, 0, 1).reshape(T, -1))
    d["w2"] = f32(inputs["L0_ffn_W2"])
    d["b2"] = f32(inputs["L0_ffn_b2"])
    d["wr"] = f32(inputs["L0_ffn_Wr"])
    d["br"] = f32(np.asarray(inputs["L0_ffn_br"], np.float32).reshape(1, E))
    d["gmb"] = f32(np.tile(np.asarray(inputs["L0_ffn_gamma"],
                                      np.float32)[None, :], (T, 1)))
    d["btb"] = f32(np.tile(np.asarray(inputs["L0_ffn_beta"],
                                      np.float32)[None, :], (T, 1)))
    d["cst_in"] = f32(np.tile(np.array([CONST_VALS], np.float32), (T, 1)))
    d["iota8"] = np.arange(16, dtype=np.float32).reshape(1, 16)
    d["idn"] = np.eye(T, dtype=NPBF)
    d["on1"] = np.ones((1, T), dtype=NPBF)
    return d


def host_l0_inmaps(inputs):
    com = host_common(inputs)
    x0 = np.ascontiguousarray(np.asarray(inputs["L0_attn_x"], np.float32))
    qi = np.arange(T)
    ki = np.arange(KC)
    ci = np.arange(NCHUNK)
    ti = np.arange(NOWN)
    in_maps = []
    for core in range(8):
        b, k = divmod(core, 4)
        qg = (ti[:, None, None, None] + 16 * k) * T + qi[None, None, :, None]
        kg = ci[None, :, None, None] * KC + ki[None, None, None, :]
        cm = np.where(kg <= qg, np.float32(0), np.float32(-1e9)).astype(NPBF)
        m = dict(com)
        m["x_seq"] = x0[b]
        m["x_own"] = np.ascontiguousarray(x0[b, 2048 * k:2048 * (k + 1)])
        m["cmask"] = cm
        in_maps.append(m)
    return in_maps


NOH = 18
NOHT = NOH * T


class L1(L0):

    def __init__(self, dbg=False):
        self.dbg = dbg
        nc = bass.Bass("TRN2", target_bir_lowering=False)
        self.nc = nc
        self.ctx = contextlib.ExitStack()
        self.ctx.__enter__()

        inp = lambda n, s, d=F32: nc.dram_tensor(n, s, d, kind="ExternalInput").ap()
        outp = lambda n, s, d=F32: nc.dram_tensor(n, s, d, kind="ExternalOutput").ap()
        self.x1f = inp("x1f", [NOHT, D])
        self.bmsk = inp("bmsk", [NOWN, 3, T, T], BF16)
        self.wqkv = inp("wqkv", [D, 3 * D])
        self.bqkvr = inp("bqkvr", [T, 24])
        self.bqv = inp("bqv", [1, D])
        self.lnwb = inp("lnwb", [T, D])
        self.lnbb = inp("lnbb", [T, D])
        self.wg = inp("wg", [D, E])
        self.w1 = inp("w1", [E, D, FF])
        self.b1r = inp("b1r", [T, E * (FF // T)])
        self.w2 = inp("w2", [E, FF, D])
        self.b2 = inp("b2", [E, D])
        self.cst_in = inp("cst_in", [T, NCONST])
        self.idn_in = inp("idn", [T, T], BF16)
        self.on1_in = inp("on1", [1, T], BF16)
        self.x2o = outp("x2o", [NOT, D])
        if dbg:
            self.d_xnfm = outp("d_xnfm", [T, DCH * NOHT], BF16)
            self.d_gm = outp("d_gm", [T, NOWN * E])
            self.d_y = outp("d_y", [T, NOWN * D])
            self.d_s1 = outp("d_s1", [T, NOWN * D])
            self.d_att = outp("d_att", [T, NOWN * D])

        c = self.ctx
        sb = lambda n, s, d=F32: c.enter_context(nc.sbuf_tensor(n, s, d))
        pt = lambda n, s, d=F32: c.enter_context(nc.psum_tensor(n, s, d))
        self.cst = sb("cst", [T, NCONST])
        self.idn = sb("idn_s", [T, T], BF16)
        self.on1 = sb("on1_s", [1, T], BF16)
        self.b2row = [sb(f"b2row{i}", [1, D], BF16) for i in range(2)]
        self.wgs = sb("wgs", [T, DCH * E], BF16)
        self.bqvb = sb("bqvb", [1, D], BF16)
        self.bqr = sb("bqr", [T, 24])
        self.xnfm = sb("xnfm", [T, DCH * NOHT], BF16)
        self.r_oh = sb("r_oh", [T, NOH])
        self.nm_oh = sb("nm_oh", [T, NOH])
        self.gmask = sb("gmask", [T, NOWN * E])
        self.tsc = sb("tsc", [T, 20])
        self.tscl = sb("tscl", [T, 48])
        self.sxs = sb("sxs", [T, NOH])
        self.sqs = sb("sqs", [T, NOH])

        self.psb = [pt(f"psf{i}", [T, 512]) for i in range(5)]
        self.pt2 = [pt(f"pt2_{i}", [T, 1024], BF16) for i in range(2)]
        self.ps_r = pt("ps_r", [T, 512])

        self.s_c0 = Sem(nc, c, "s_c0")
        self.s_g0 = Sem(nc, c, "s_g0")
        self.s_b1 = Sem(nc, c, "s_b1")
        self.s_gb = Sem(nc, c, "s_gb")
        self.pe = Sem(nc, c, "pe")
        self.dv = Sem(nc, c, "dv")
        self.ac = Sem(nc, c, "ac")
        self.so = Sem(nc, c, "so")

        self.bank = Pool(list(range(5)))
        self.tslot = Pool(list(range(2)))
        self.b2r = Ring(nc, c, "rb2", self.b2row)

    def build(self, upto=9, q4upto=9):
        nc = self.nc
        self.q4upto = q4upto
        self.q0()
        with contextlib.ExitStack() as sA:
            sbA = lambda n, s, d=F32: self.sbt(sA, n, s, d)
            self.sqscr = sbA("sqscr", [T, D])
            self.xntm = sbA("xntm", [T, D], BF16)
            self.xs = [sbA(f"xs{i}", [T, D]) for i in range(2)]
            self.xsr = Ring(nc, self.ctx, "rxs", self.xs)
            self.g1 = sbA("g1", [T, NOWN * D], BF16)
            with contextlib.ExitStack() as sO:
                sbO = lambda n, s, d=F32: self.sbt(sO, n, s, d)
                self.O = sbO("O", [T, NOWN * D])
                self.q1()
                if upto < 2:
                    return nc
                with contextlib.ExitStack() as s5:
                    sb5 = lambda n, s, d=F32: self.sbt(s5, n, s, d)
                    self.b1s = sb5("b1s", [T, E * (FF // T)])
                    self.b2bb = sb5("b2bb", [T, D])
                    self.w1w = [sb5(f"w1w{i}", [T, DCH * 512], BF16)
                                for i in range(2)]
                    self.w2w = [sb5(f"w2w{i}", [T, 4 * D], BF16)
                                for i in range(2)]
                    self.w1r = Ring(nc, self.ctx, "rw1", self.w1w)
                    self.w2r = Ring(nc, self.ctx, "rw2", self.w2w)
                    self.hsb = [sb5(f"hsb{i}", [T, NOT], BF16)
                                for i in range(4)]
                    self.q2()
                if upto < 3:
                    return nc
                self.q3()
            if upto < 4:
                return nc
            with contextlib.ExitStack() as s4:
                sb4 = lambda n, s, d=F32: self.sbt(s4, n, s, d)
                self.kfm = sb4("kfm", [T, DCH * NOHT], BF16)
                self.vtm = sb4("vtm", [T, NOH * D], BF16)
                self.wqs = sb4("wqs", [T, DCH * D], BF16)
                self.bmr_sl = [sb4(f"bmsl{i}", [T, 3 * T], BF16)
                               for i in range(2)]
                self.bmr = Ring(nc, self.ctx, "rbm", self.bmr_sl)
                self.lnw = sb4("lnw", [T, D])
                self.lnb = sb4("lnb", [T, D])
                self.qf = sb4("qf", [T, 2 * DCH * T], BF16)
                self.sTs = sb4("sTs", [T, 3 * T], BF16)
                self.gscr = self.sqscr
                self.x1s = [sb4(f"x1s{i}", [T, D]) for i in range(2)]
                self.x1r = Ring(nc, self.ctx, "rx1", self.x1s)
                self.q4(q4upto=self.q4upto)
        return nc

    def q0(self):
        p = Ph("q0")
        for dst, srca in [(self.cst, self.cst_in), (self.idn, self.idn_in),
                          (self.on1, self.on1_in), (self.bqr, self.bqkvr)]:
            self.dma(p, "sync", self.s_c0, dst[:], srca)
        self.dma(p, "gpsimd", self.s_g0,
                 self.wgs[:].rearrange("p (c e) -> p c e", c=DCH),
                 self.wg.rearrange("(c p) e -> p c e", p=T, c=DCH))
        self.dma(p, "gpsimd", self.s_g0, self.bqvb[:], self.bqv)
        p.wait("vector", self.s_c0.at(), self.s_g0.at())
        p.wait("scalar", self.s_c0.at())
        p.wait("tensor", self.s_c0.at(), self.s_g0.at())
        p.emit(self.nc)

    def q1(self):
        p = Ph("q1")
        p.wait("sync", self.so.at())
        for i in range(NOH):
            j, xsl, xsem = self.xsr.next(p, "sync")
            self.dma(p, "sync", xsem, xsl[:],
                     self.x1f[i * T:(i + 1) * T, :])
            p.wait("vector", xsem.at())
            self.stats_tile(p, xsl[:], i)
            self.xsr.release(j, self.dv)
        self.ln_chain(p, 0, NOH, self.r_oh, self.nm_oh)
        for i in range(NOH):
            j, xsl, xsem = self.xsr.next(p, "sync")
            self.dma(p, "sync", xsem, xsl[:],
                     self.x1f[i * T:(i + 1) * T, :])
            p.wait("vector", xsem.at())
            self.fm_tile(p, xsl[:],
                         self.r_oh[:, i:i + 1], self.nm_oh[:, i:i + 1],
                         self.xnfm, lambda dch, i=i: dch * NOHT + i * T)
            self.xsr.release(j, self.dv)
        for i in range(NOWN):
            oi = i + 2
            p.wait("tensor", self.ac.at())
            jg, gbi = self.bank.take(p, "tensor")
            gb = self.psb[gbi]
            for dn in range(DCH):
                self.t(p, lambda e, dn=dn, oi=oi, gb=gb: e.matmul(
                    gb[:, 0:E],
                    self.xnfm[:, dn * NOHT + oi * T:dn * NOHT + oi * T + T],
                    self.wgs[:, dn * E:(dn + 1) * E],
                    start=(dn == 0), stop=(dn == DCH - 1)),
                    inc=(dn == DCH - 1))
            p.wait("vector", self.pe.at())
            gm = self.gmask[:, i * E:(i + 1) * E]
            self.v(p, lambda e, gb=gb: e.reduce_max(self.tsc[:, 0:1],
                                                    gb[:, 0:E], axis=AX.X))
            self.v(p, lambda e, gm=gm, gb=gb: e.tensor_scalar(
                gm, gb[:, 0:E], self.tsc[:, 0:1], None, ALU.is_ge))
            self.bank.release(jg, self.dv)
        if self.dbg:
            p.wait("sync", self.ac.at(), self.dv.at())
            self.dma(p, "sync", self.so, self.d_xnfm, self.xnfm[:])
            self.dma(p, "sync", self.so, self.d_gm, self.gmask[:])
            p.wait("sync", self.so.at())
        p.emit(self.nc)

    def q2(self):
        p = Ph("q2")
        self.dma(p, "sync", self.s_b1, self.b1s[:], self.b1r)
        b2pend = self.b2r.next(p, "gpsimd")
        self.dma(p, "gpsimd", b2pend[2], b2pend[1][:], self.b2[0:1, :])
        p.wait("scalar", self.s_b1.at())
        hpool = Pool(list(range(4)))
        for e_i in range(E):
            jb2, b2sl, b2sem = b2pend
            if e_i + 1 < E:
                b2pend = self.b2r.next(p, "gpsimd")
                self.dma(p, "gpsimd", b2pend[2], b2pend[1][:],
                         self.b2[e_i + 1:e_i + 2, :])
            jbb, bb = self.bank.take(p, "tensor")
            jbb1, bb1 = self.bank.take(p, "tensor")
            p.wait("tensor", b2sem.at())
            self.t(p, lambda e, bb=bb, b2sl=b2sl: e.matmul(
                self.psb[bb][:], self.on1[0:1, :],
                b2sl[0:1, 0:512], start=True, stop=True))
            self.t(p, lambda e, bb1=bb1, b2sl=b2sl: e.matmul(
                self.psb[bb1][:], self.on1[0:1, :],
                b2sl[0:1, 512:1024], start=True, stop=True), inc=True)
            self.b2r.release(jb2, self.pe)
            p.wait("vector", self.pe.at())
            self.v(p, lambda e, bb=bb: e.tensor_copy(self.b2bb[:, 0:512],
                                                     self.psb[bb][:]))
            self.v(p, lambda e, bb1=bb1: e.tensor_copy(self.b2bb[:, 512:1024],
                                                       self.psb[bb1][:]))
            self.bank.release(jbb, self.dv)
            self.bank.release(jbb1, self.dv)
            for w in range(NWIN):
                firstw = (e_i == 0 and w == 0)
                j1, w1s, w1sem = self.w1r.next(p, "gpsimd")
                self.dma(p, "gpsimd", w1sem,
                         w1s[:].rearrange("p (c f) -> p c f", c=DCH),
                         self.w1[e_i, :, w * 512:(w + 1) * 512].rearrange(
                             "(c p) f -> p c f", p=T, c=DCH))
                j2, w2s, w2sem = self.w2r.next(p, "gpsimd")
                self.dma(p, "gpsimd", w2sem,
                         w2s[:].rearrange("p (c d) -> p c d", c=4),
                         self.w2[e_i, w * 512:(w + 1) * 512, :].rearrange(
                             "(c p) d -> p c d", p=T, c=4))
                hslots = []
                p.wait("tensor", w1sem.at(), w2sem.at())
                for fc in range(4):
                    jh, hi = hpool.take(p, "scalar")
                    hslots.append(hi)
                    hs = self.hsb[hi]
                    for tg in range(4):
                        jb, bi = self.bank.take(p, "tensor")
                        bk = self.psb[bi]
                        for dn in range(DCH):
                            self.t(p, lambda e, dn=dn, fc=fc, tg=tg, w1s=w1s,
                                   bk=bk:
                                   e.matmul(bk[:],
                                            w1s[:, dn * 512 + fc * T:
                                                dn * 512 + fc * T + T],
                                            self.xnfm[:,
                                                      dn * NOHT + 2 * T +
                                                      tg * 512:
                                                      dn * NOHT + 2 * T +
                                                      tg * 512 + 512],
                                            start=(dn == 0),
                                            stop=(dn == DCH - 1)),
                                   inc=(dn == DCH - 1))
                        p.wait("scalar", self.pe.at())
                        bcol = self.b1s[:, e_i * 32 + w * 4 + fc:
                                        e_i * 32 + w * 4 + fc + 1]
                        self.a(p, lambda e, bk=bk, hs=hs, tg=tg, bcol=bcol:
                               e.activation(hs[:, tg * 512:(tg + 1) * 512],
                                            bk[:], AF.Gelu, bias=bcol),
                               inc=True)
                        self.bank.release(jb, self.ac)
                for i in range(NOWN):
                    jy0, y0 = self.bank.take(p, "tensor")
                    jy1, y1 = self.bank.take(p, "tensor")
                    p.wait("tensor", self.ac.at())
                    for fc in range(4):
                        hs = self.hsb[hslots[fc]]
                        self.t(p, lambda e, fc=fc, i=i, w2s=w2s, hs=hs, y0=y0:
                               e.matmul(self.psb[y0][:],
                                        hs[:, i * T:(i + 1) * T],
                                        w2s[:, fc * D:fc * D + 512],
                                        start=(fc == 0), stop=(fc == 3)))
                        self.t(p, lambda e, fc=fc, i=i, w2s=w2s, hs=hs, y1=y1:
                               e.matmul(self.psb[y1][:],
                                        hs[:, i * T:(i + 1) * T],
                                        w2s[:, fc * D + 512:fc * D + 1024],
                                        start=(fc == 0), stop=(fc == 3)),
                               inc=(fc == 3))
                    p.wait("vector", self.pe.at())
                    oc = self.O[:, i * D:(i + 1) * D]
                    gc = self.gmask[:, i * E + e_i:i * E + e_i + 1]
                    if firstw:
                        self.vb(p, lambda e, y0=y0, oc=oc, gc=gc:
                                e.tensor_scalar(oc[:, 0:512],
                                                self.psb[y0][:],
                                                gc, None, ALU.mult))
                        self.vb(p, lambda e, y1=y1, oc=oc, gc=gc:
                                e.tensor_scalar(oc[:, 512:1024],
                                                self.psb[y1][:],
                                                gc, None, ALU.mult))
                    else:
                        self.vb(p, lambda e, y0=y0, oc=oc, gc=gc:
                                e.scalar_tensor_tensor(oc[:, 0:512],
                                                       self.psb[y0][:], gc,
                                                       oc[:, 0:512],
                                                       ALU.mult, ALU.add))
                        self.vb(p, lambda e, y1=y1, oc=oc, gc=gc:
                                e.scalar_tensor_tensor(oc[:, 512:1024],
                                                       self.psb[y1][:], gc,
                                                       oc[:, 512:1024],
                                                       ALU.mult, ALU.add))
                    self.bank.release(jy0, self.dv)
                    self.bank.release(jy1, self.dv)
                for hi in hslots:
                    hpool.release(hi, self.pe)
                self.w1r.release(j1, self.pe)
                self.w2r.release(j2, self.pe)
            for i in range(NOWN):
                oc = self.O[:, i * D:(i + 1) * D]
                gc = self.gmask[:, i * E + e_i:i * E + e_i + 1]
                self.v(p, lambda e, oc=oc, gc=gc: e.scalar_tensor_tensor(
                    oc, self.b2bb[:], gc, oc, ALU.mult, ALU.add))
        if self.dbg:
            p.wait("sync", self.dv.at())
            self.dma(p, "sync", self.so, self.d_y[:, 0:NOWN * D // 2],
                     self.O[:, 0:NOWN * D // 2])
            self.dma(p, "sync", self.so, self.d_y[:, NOWN * D // 2:],
                     self.O[:, NOWN * D // 2:])
            p.wait("sync", self.so.at())
        p.emit(self.nc)

    def q3(self):
        p = Ph("q3")
        p.wait("scalar", self.dv.at())
        for i in range(NOWN):
            self.a(p, lambda e, i=i: e.activation(
                self.g1[:, i * D:(i + 1) * D], self.O[:, i * D:(i + 1) * D],
                AF.Silu))
        p.emit(self.nc)

    def q4(self, q4upto=9):
        nc = self.nc
        p = Ph("q4")
        self.dma(p, "sync", self.s_gb, self.lnw[:], self.lnwb)
        self.dma(p, "sync", self.s_gb, self.lnb[:], self.lnbb)
        p.wait("vector", self.s_gb.at())
        qp = Pool(list(range(6)))
        qp.marks = [self.bank.marks[j] for j in range(5)] + [self.dv.at()]
        qb = lambda j: self.psb[j] if j < 5 else self.ps_r
        self.dma(p, "gpsimd", self.s_g0,
                 self.wqs[:].rearrange("p (c n) -> p c n", c=DCH),
                 self.wqkv[:, D:2 * D].rearrange("(c p) n -> p c n", p=T,
                                                 c=DCH))
        p.wait("tensor", self.s_g0.at(), self.ac.at())
        wins = [(0, 512), (512, 512), (1024, 512), (1536, 512), (2048, 256)]
        for dk in range(DCH):
            for w0, wn in wins:
                jb, bi = qp.take(p, "tensor")
                bk = qb(bi)
                for dn in range(DCH):
                    self.t(p, lambda e, dn=dn, dk=dk, w0=w0, wn=wn, bk=bk:
                           e.matmul(bk[:, 0:wn],
                                    self.wqs[:, dn * D + dk * T:
                                             dn * D + dk * T + T],
                                    self.xnfm[:, dn * NOHT + w0:
                                              dn * NOHT + w0 + wn],
                                    start=(dn == 0), stop=(dn == DCH - 1)),
                           inc=(dn == DCH - 1))
                p.wait("vector", self.pe.at())
                bcol = self.bqr[:, 8 + dk:9 + dk]
                self.vb(p, lambda e, bk=bk, dk=dk, w0=w0, wn=wn,
                        bcol=bcol:
                        e.tensor_scalar(self.kfm[:, dk * NOHT + w0:
                                                 dk * NOHT + w0 + wn],
                                        bk[:, 0:wn], bcol, None, ALU.add))
                qp.release(jb, self.dv)
        if q4upto < 2:
            p.emit(nc)
            return
        self.dma(p, "gpsimd", self.s_g0,
                 self.wqs[:].rearrange("p (c n) -> p c n", c=DCH),
                 self.wqkv[:, 2 * D:3 * D].rearrange("(c p) n -> p c n",
                                                     p=T, c=DCH),
                 guard=self.dv.at())
        p.wait("tensor", self.s_g0.at())
        for oi in range(NOH):
            jb0, b0i = qp.take(p, "tensor")
            jb1, b1i = qp.take(p, "tensor")
            b0, b1 = qb(b0i), qb(b1i)
            for dn in range(DCH):
                self.t(p, lambda e, dn=dn, oi=oi, b0=b0: e.matmul(
                    b0[:],
                    self.xnfm[:, dn * NOHT + oi * T:dn * NOHT + oi * T + T],
                    self.wqs[:, dn * D:dn * D + 512],
                    start=(dn == 0), stop=False))
                self.t(p, lambda e, dn=dn, oi=oi, b1=b1: e.matmul(
                    b1[:],
                    self.xnfm[:, dn * NOHT + oi * T:dn * NOHT + oi * T + T],
                    self.wqs[:, dn * D + 512:dn * D + 1024],
                    start=(dn == 0), stop=False))
            self.t(p, lambda e, b0=b0: e.matmul(
                b0[:], self.on1[0:1, :], self.bqvb[:, 0:512],
                start=False, stop=True))
            self.t(p, lambda e, b1=b1: e.matmul(
                b1[:], self.on1[0:1, :], self.bqvb[:, 512:1024],
                start=False, stop=True), inc=True)
            p.wait("vector", self.pe.at())
            self.vb(p, lambda e, b0=b0, oi=oi: e.tensor_copy(
                self.vtm[:, oi * D:oi * D + 512], b0[:]))
            self.vb(p, lambda e, b1=b1, oi=oi: e.tensor_copy(
                self.vtm[:, oi * D + 512:oi * D + 1024], b1[:]))
            qp.release(jb0, self.dv)
            qp.release(jb1, self.dv)
        if q4upto < 3:
            p.emit(nc)
            return
        self.dma(p, "gpsimd", self.s_g0,
                 self.wqs[:].rearrange("p (c n) -> p c n", c=DCH),
                 self.wqkv[:, 0:D].rearrange("(c p) n -> p c n", p=T, c=DCH),
                 guard=self.dv.at())
        p.wait("tensor", self.s_g0.at())

        def stage_a(i):
            oi = i + 2
            par = i % 2
            jbm, bmsl, bmsem = self.bmr.next(p, "sync")
            self.dma(p, "sync", bmsem,
                     bmsl[:].rearrange("p (o q) -> p o q", o=3),
                     self.bmsk[i].rearrange("o k q -> k o q"))
            qfp = self.qf[:, par * D:(par + 1) * D]
            for half in range(2):
                jb, bi = qp.take(p, "tensor")
                bk = qb(bi)
                for dq in range(4):
                    dqg = half * 4 + dq
                    for dn in range(DCH):
                        self.t(p, lambda e, dn=dn, dqg=dqg, dq=dq, oi=oi,
                               bk=bk:
                               e.matmul(bk[:, dq * T:(dq + 1) * T],
                                        self.wqs[:, dn * D + dqg * T:
                                                 dn * D + dqg * T + T],
                                        self.xnfm[:, dn * NOHT + oi * T:
                                                  dn * NOHT + oi * T + T],
                                        start=(dn == 0),
                                        stop=(dn == DCH - 1)),
                               inc=(dn == DCH - 1))
                p.wait("vector", self.pe.at())
                for dq in range(4):
                    dqg = half * 4 + dq
                    bcol = self.bqr[:, dqg:dqg + 1]
                    self.vb(p, lambda e, bk=bk, dq=dq, dqg=dqg,
                            bcol=bcol, qfp=qfp:
                            e.tensor_scalar(qfp[:, dqg * T:(dqg + 1) * T],
                                            bk[:, dq * T:(dq + 1) * T],
                                            bcol, None, ALU.add))
                qp.release(jb, self.dv)
            return (i, jbm, bmsl, bmsem, self.dv.at(), qfp)

        def stage_bc(st):
            i, jbm, bmsl, bmsem, evA, qfp = st
            oi = i + 2
            for o in range(3):
                kt = oi - (2 - o)
                jb, bi = qp.take(p, "tensor")
                bk = qb(bi)
                if o == 0:
                    p.wait("tensor", evA)
                for dn in range(DCH):
                    self.t(p, lambda e, dn=dn, kt=kt, bk=bk, qfp=qfp:
                           e.matmul(
                        bk[:, 0:T],
                        self.kfm[:, dn * NOHT + kt * T:dn * NOHT + kt * T + T],
                        qfp[:, dn * T:(dn + 1) * T],
                        start=(dn == 0), stop=(dn == DCH - 1)),
                        inc=(dn == DCH - 1))
                p.wait("vector", self.pe.at(), bmsem.at())
                bmc = bmsl[:, o * T:o * T + T]
                self.vb(p, lambda e, bk=bk, o=o, bmc=bmc:
                        e.tensor_tensor(
                    self.sTs[:, o * T:(o + 1) * T], bk[:, 0:T], bmc,
                    ALU.mult))
                qp.release(jb, self.dv)
            self.bmr.release(jbm, self.dv)
            mm_mark = self.dv.at()
            ja0, a0i = qp.take(p, "tensor")
            ja1, a1i = qp.take(p, "tensor")
            a0b, a1b = qb(a0i), qb(a1i)
            p.wait("tensor", mm_mark)
            for o in range(3):
                kt = oi - (2 - o)
                self.t(p, lambda e, o=o, kt=kt, a0b=a0b: e.matmul(
                    a0b[:], self.sTs[:, o * T:(o + 1) * T],
                    self.vtm[:, kt * D:kt * D + 512],
                    start=(o == 0), stop=(o == 2)))
                self.t(p, lambda e, o=o, kt=kt, a1b=a1b: e.matmul(
                    a1b[:], self.sTs[:, o * T:(o + 1) * T],
                    self.vtm[:, kt * D + 512:kt * D + 1024],
                    start=(o == 0), stop=(o == 2)), inc=(o == 2))
            p.wait("vector", self.pe.at())
            sx = self.tsc[:, 0:1]
            sq = self.tsc[:, 1:2]
            t0 = self.tsc[:, 2:3]
            t1 = self.tsc[:, 3:4]
            self.v(p, lambda e, a0b=a0b: e.reduce_sum(t0, a0b[:], axis=AX.X))
            self.v(p, lambda e, a1b=a1b: e.reduce_sum(t1, a1b[:], axis=AX.X))
            self.ve(p, lambda e: e.tensor_add(sx, t0, t1))
            p.wait("scalar", self.dv.at())
            self.a(p, lambda e, a0b=a0b: e.activation(
                self.sqscr[:, 0:512], a0b[:], AF.Square))
            self.a(p, lambda e, a1b=a1b: e.activation(
                self.sqscr[:, 512:1024], a1b[:], AF.Square))
            p.wait("vector", self.ac.at())
            self.v(p, lambda e: e.reduce_sum(sq, self.sqscr[:], axis=AX.X))
            mu = self.tsc[:, 4:5]
            var = self.tsc[:, 5:6]
            rr = self.tsc[:, 6:7]
            nm = self.tsc[:, 7:8]
            self.ve(p, lambda e: e.tensor_scalar(mu, sx, self.cc(C_1024),
                                                 None, ALU.mult))
            self.ve(p, lambda e: e.tensor_scalar(var, sq, self.cc(C_1024),
                                                 self.cc(C_EPS), ALU.mult,
                                                 ALU.add))
            self.ve(p, lambda e: e.tensor_mul(t0, mu, mu))
            self.ve(p, lambda e: e.tensor_sub(var, var, t0))
            p.wait("scalar", self.dv.at())
            self.a(p, lambda e: e.activation(rr, var, AF.Sqrt))
            p.wait("vector", self.ac.at())
            self.ve(p, lambda e: e.reciprocal(rr, rr))
            self.ve(p, lambda e: e.tensor_mul(nm, mu, rr))
            self.ve(p, lambda e: e.tensor_scalar(nm, nm, self.cc(C_NEG1),
                                                 None, ALU.mult))
            self.v(p, lambda e, a0b=a0b: e.tensor_scalar(
                self.gscr[:, 0:512], a0b[:], rr, nm, ALU.mult, ALU.add))
            self.v(p, lambda e, a1b=a1b: e.tensor_scalar(
                self.gscr[:, 512:1024], a1b[:], rr, nm, ALU.mult, ALU.add))
            qp.release(ja0, self.dv)
            qp.release(ja1, self.dv)
            self.v(p, lambda e: e.tensor_mul(self.gscr[:], self.gscr[:],
                                             self.lnw[:]))
            self.v(p, lambda e: e.tensor_add(self.gscr[:], self.gscr[:],
                                             self.lnb[:]))
            jx, x1t, x1sem = self.x1r.next(p, "sync")
            p.wait("sync", self.so.at())
            self.dma(p, "sync", x1sem, x1t[:],
                     self.x1f[(i + 2) * T:(i + 3) * T, :])
            self.v(p, lambda e, i=i: e.tensor_mul(
                self.gscr[:], self.gscr[:], self.g1[:, i * D:(i + 1) * D]))
            p.wait("vector", x1sem.at())
            self.v(p, lambda e, x1t=x1t: e.tensor_add(x1t[:], x1t[:],
                                                      self.gscr[:]))
            p.wait("sync", self.dv.at())
            self.dma(p, "sync", self.so, self.x2o[i * T:(i + 1) * T, :],
                     x1t[:])

        st = stage_a(0)
        for i in range(NOWN):
            nxt = stage_a(i + 1) if i + 1 < NOWN else None
            stage_bc(st)
            st = nxt
        p.wait("sync", self.so.at())
        p.emit(nc)


def build_l1(dbg=False, upto=9, q4upto=9):
    k = L1(dbg=dbg)
    nc = k.build(upto=upto, q4upto=q4upto)
    k.ctx.__exit__(None, None, None)
    return nc


def host_l1_common(inputs):
    f32 = lambda a: np.ascontiguousarray(np.asarray(a, np.float32))
    d = {}
    d["wqkv"] = f32(inputs["L1_attn_W_qkv"])
    bq = np.asarray(inputs["L1_attn_b_qkv"], np.float32)
    d["bqkvr"] = f32(bq.reshape(24, T).T)
    d["bqv"] = f32(bq[2 * D:3 * D].reshape(1, D))
    d["lnwb"] = f32(np.tile(np.asarray(inputs["L1_attn_ln_w"],
                                       np.float32)[None, :], (T, 1)))
    d["lnbb"] = f32(np.tile(np.asarray(inputs["L1_attn_ln_b"],
                                       np.float32)[None, :], (T, 1)))
    d["wg"] = f32(np.asarray(inputs["L1_ffn_Wg"], np.float32).T)
    d["w1"] = f32(inputs["L1_ffn_W1"])
    b1 = np.asarray(inputs["L1_ffn_b1"], np.float32)
    d["b1r"] = f32(b1.reshape(E, FF // T, T).transpose(2, 0, 1).reshape(T, -1))
    d["w2"] = f32(inputs["L1_ffn_W2"])
    d["b2"] = f32(inputs["L1_ffn_b2"])
    d["cst_in"] = f32(np.tile(np.array([CONST_VALS], np.float32), (T, 1)))
    d["idn"] = np.eye(T, dtype=NPBF)
    d["on1"] = np.ones((1, T), dtype=NPBF)
    return d


def host_l1_inmaps(inputs, x1_full):
    com = host_l1_common(inputs)
    kk = np.arange(T)
    qq = np.arange(T)
    in_maps = []
    for core in range(8):
        b, k = divmod(core, 4)
        lo = 2048 * k - 256
        if lo < 0:
            halo = np.zeros((256, D), np.float32)
        else:
            halo = x1_full[b, lo:2048 * k]
        x1f = np.ascontiguousarray(
            np.concatenate([halo, x1_full[b, 2048 * k:2048 * (k + 1)]],
                           axis=0).astype(np.float32))
        bm = np.zeros((NOWN, 3, T, T), np.float32)
        for i in range(NOWN):
            gt = 16 * k + i
            for o in range(3):
                gk = gt - (2 - o)
                if gk < 0:
                    continue
                qg = gt * T + qq[None, :]
                kg = gk * T + kk[:, None]
                bm[i, o] = np.where((qg - 256 < kg) & (kg <= qg),
                                   np.float32(1.0 / 512.0), np.float32(0))
        m = dict(com)
        m["x1f"] = x1f
        m["bmsk"] = bm.astype(NPBF)
        in_maps.append(m)
    return in_maps



class LF(L1):

    B2S_LATE = True

    def __init__(self):
        self.dbg = False
        self.NT0 = NOH
        self.NTT = NOH * T
        nc = bass.Bass("TRN2", target_bir_lowering=False)
        self.nc = nc
        self.ctx = contextlib.ExitStack()
        self.ctx.__enter__()
        inp = lambda n, s, d=F32: nc.dram_tensor(n, s, d, kind="ExternalInput").ap()
        outp = lambda n, s, d=F32: nc.dram_tensor(n, s, d, kind="ExternalOutput").ap()
        self.x_seq = inp("x_seq", [S, D])
        self.x_oh = inp("x_oh", [NOH * T, D])
        self.cmask = inp("cmask", [NOH, NCHUNK, T, KC], BF16)
        self.wt = inp("wt", [D, 2 * D])
        self.w1a = inp("w1a", [E, D, FF])
        self.b1ra = inp("b1ra", [T, E * (FF // T)])
        self.w2a = inp("w2a", [E, FF, D])
        self.b2a = inp("b2a", [E, D])
        self.wr = inp("wr", [D, E])
        self.br = inp("br", [1, E])
        self.gmb = inp("gmb", [T, D])
        self.btb = inp("btb", [T, D])
        self.bmsk = inp("bmsk", [NOWN, 3, T, T], BF16)
        self.wqkv = inp("wqkv", [D, 3 * D])
        self.bqkvr = inp("bqkvr", [T, 24])
        self.bqv = inp("bqv", [1, D])
        self.lnwb = inp("lnwb", [T, D])
        self.lnbb = inp("lnbb", [T, D])
        self.wg = inp("wg", [D, E])
        self.w1b = inp("w1b", [E, D, FF])
        self.b1rb = inp("b1rb", [T, E * (FF // T)])
        self.w2b = inp("w2b", [E, FF, D])
        self.b2b = inp("b2b", [E, D])
        self.iota8 = inp("iota8", [1, 16])
        self.cst_in = inp("cst_in", [T, NCONST])
        self.idn_in = inp("idn", [T, T], BF16)
        self.on1_in = inp("on1", [1, T], BF16)
        self.x2o = outp("x2o", [NOT, D])
        self.x1scr = nc.dram_tensor("x1scr", [NOH * T, D], F32,
                                    kind="Internal").ap()
        self.scr8 = nc.dram_tensor("scr8", [1, E], F32, kind="Internal").ap()

        c = self.ctx
        sb = lambda n, s, d=F32: c.enter_context(nc.sbuf_tensor(n, s, d))
        pt = lambda n, s, d=F32: c.enter_context(nc.psum_tensor(n, s, d))
        self.cst = sb("cst", [T, NCONST])
        self.idn = sb("idn_s", [T, T], BF16)
        self.on1 = sb("on1_s", [1, T], BF16)
        self.tsc = sb("tsc", [T, 20])
        self.tscl = sb("tscl", [T, 48])
        self.sxs = sb("sxs", [T, 18])
        self.sqs = sb("sqs", [T, 18])
        self.psb = [pt(f"psf{i}", [T, 512]) for i in range(5)]
        self.pt2 = [pt(f"pt2_{i}", [T, 1024], BF16) for i in range(2)]
        self.ps_r = pt("ps_r", [T, 512])
        self.s_c0 = Sem(nc, c, "s_c0")
        self.s_g0 = Sem(nc, c, "s_g0")
        self.s_sc = Sem(nc, c, "s_sc")
        self.s_cw = Sem(nc, c, "s_cw")
        self.s_b1 = Sem(nc, c, "s_b1")
        self.s_gb = Sem(nc, c, "s_gb")
        self.s_ix = Sem(nc, c, "s_ix")
        self.pe = Sem(nc, c, "pe")
        self.dv = Sem(nc, c, "dv")
        self.ac = Sem(nc, c, "ac")
        self.so = Sem(nc, c, "so")
        self.bank = Pool(list(range(5)))
        self.tslot = Pool(list(range(2)))
        self.io8 = sb("io8", [1, 16])
        self.eidx = sb("eidx", [1, 34], mybir.dt.int32)
        self.eoff = sb("eoff", [1, 34])
        self.cwb2 = sb("cwb2", [T, 2])
        self.scr2 = nc.dram_tensor("scr2", [1, 2], F32, kind="Internal").ap()

    def build_fused(self):
        nc = self.nc
        self.x_own = self.x_oh
        self.x1o = self.x1scr
        self.w1, self.b1r = self.w1a, self.b1ra
        self.w2, self.b2 = self.w2a, self.b2a
        with contextlib.ExitStack() as lb:
            sbL = lambda n, s, d=F32: self.sbt(lb, n, s, d)
            self.wrs = sbL("wrs", [T, DCH * E], BF16)
            self.brs = sbL("brs", [1, E], BF16)
            self.xnfm = sbL("xnfm0", [T, DCH * self.NTT], BF16)
            self.r_own = sbL("r_own", [T, self.NT0])
            self.nm_own = sbL("nm_own", [T, self.NT0])
            self.r_kv = sbL("r_kv", [T, 16])
            self.nm_kv = sbL("nm_kv", [T, 16])
            self.O = sbL("O0", [T, self.NT0 * D])
            self.lbuf = sbL("lbuf", [T, self.NT0])
            self.mrun = sbL("mrun", [T, self.NT0])
            self.rsum = sbL("rsum", [T, DCH])
            self.rtmp = sbL("rtmp", [T, DCH])
            self.rw = sbL("rw", [1, 2 * E])
            self.rws = sbL("rws", [1, 32])
            self.meanbf = sbL("meanbf2", [T, DCH], BF16)
            self.cwb = sbL("cwb", [T, E])
            self.cwcol = sbL("cwcol", [E, 1], BF16)
            self.p0()
            with contextlib.ExitStack() as s12:
                sb2 = lambda n, s, d=F32: self.sbt(s12, n, s, d)
                self.sqscr = sb2("sqscr", [T, D])
                self.wkv = sb2("wkv", [T, DCH * 2 * D], BF16)
                self.Ksb = sb2("Ksb", [T, DCH * KC], BF16)
                self.Vsb = sb2("Vsb", [T, 8 * D], BF16)
                self.kvfm = sb2("kvfm", [T, DCH * 512], BF16)
                self.xs = [sb2(f"xs{i}", [T, D]) for i in range(2)]
                self.xntm = sb2("xntm", [T, D], BF16)
                self.mks = [sb2(f"mk{i}", [T, KC], BF16) for i in range(3)]
                self.Psb = sb2("Psb", [T, 2 * KC], BF16)
                self.xsr = Ring(nc, self.ctx, "rxs", self.xs)
                self.mkr = Ring(nc, self.ctx, "rmk", self.mks)
                self.p1()
                self.p2()
            with contextlib.ExitStack() as s36:
                sb3 = lambda n, s, d=F32: self.sbt(s36, n, s, d)
                self.s0 = sb3("s0", [T, self.NT0 * D], BF16)
                self.xo2 = [sb3(f"xo2{i}", [T, D]) for i in range(1)]
                self.xo2r = Ring(nc, self.ctx, "rxo", self.xo2)
                self.bsum_b = sb3("bsum_b", [1, D], BF16)
                self.bsb = sb3("bsb", [T, D])
                with contextlib.ExitStack() as s34:
                    sb34 = lambda n, s, d=F32: self.sbt(s34, n, s, d)
                    self.xsc = sb34("xsc", [T, D])
                    self.b2s = sb34("b2s", [E, D], BF16)
                    self.p3()
                    self.p4()
                with contextlib.ExitStack() as s5:
                    sb5 = lambda n, s, d=F32: self.sbt(s5, n, s, d)
                    self.b1s = sb5("b1s", [T, 2 * (FF // T)])
                    self.w1w = [sb5(f"w1w{i}", [T, DCH * 512], BF16)
                                for i in range(2)]
                    self.w2w = [sb5(f"w2w{i}", [T, 4 * D], BF16)
                                for i in range(2)]
                    self.w1r = Ring(nc, self.ctx, "rw1", self.w1w)
                    self.w2r = Ring(nc, self.ctx, "rw2", self.w2w)
                    self.hsb = [sb5(f"hsb{i}", [T, self.NTT], BF16)
                                for i in range(4)]
                    self.p5()
                with contextlib.ExitStack() as s6:
                    sb6 = lambda n, s, d=F32: self.sbt(s6, n, s, d)
                    self.sqscr6 = sb6("sqscr6", [T, D])
                    self.gscr = sb6("gscr", [T, D])
                    self.gm = sb6("gm", [T, D])
                    self.bt = sb6("bt", [T, D])
                    self.p6()
        self.NMPfx = "q_"
        self.x1f = self.x1scr
        self.w1, self.b1r = self.w1b, self.b1rb
        self.w2, self.b2 = self.w2b, self.b2b
        with contextlib.ExitStack() as l1b:
            sbM = lambda n, s, d=F32: self.sbt(l1b, n, s, d)
            self.b2row = [sbM(f"b2row{i}", [1, D], BF16) for i in range(2)]
            self.b2r = Ring(nc, self.ctx, "rb2", self.b2row)
            self.wgs = sbM("wgs", [T, DCH * E], BF16)
            self.bqvb = sbM("bqvb", [1, D], BF16)
            self.bqr = sbM("bqr", [T, 24])
            self.xnfm = sbM("xnfm1", [T, DCH * NOHT], BF16)
            self.r_oh = sbM("r_oh", [T, NOH])
            self.nm_oh = sbM("nm_oh", [T, NOH])
            self.gmask = sbM("gmask", [T, NOWN * E])
            self.q0()
            with contextlib.ExitStack() as sA:
                sbA = lambda n, s, d=F32: self.sbt(sA, n, s, d)
                self.sqscr = sbA("sqscr", [T, D])
                self.xntm = sbA("xntm", [T, D], BF16)
                self.xs = [sbA(f"xs{i}", [T, D]) for i in range(2)]
                self.xsr = Ring(nc, self.ctx, "rxs", self.xs)
                self.g1 = sbA("g1", [T, NOWN * D], BF16)
                with contextlib.ExitStack() as sO:
                    sbO = lambda n, s, d=F32: self.sbt(sO, n, s, d)
                    self.O = sbO("O", [T, NOWN * D])
                    self.q1()
                    with contextlib.ExitStack() as s5:
                        sb5 = lambda n, s, d=F32: self.sbt(s5, n, s, d)
                        self.b1s = sb5("b1s", [T, E * (FF // T)])
                        self.b2bb = sb5("b2bb", [T, D])
                        self.w1w = [sb5(f"w1w{i}", [T, DCH * 512], BF16)
                                    for i in range(2)]
                        self.w2w = [sb5(f"w2w{i}", [T, 4 * D], BF16)
                                    for i in range(2)]
                        self.w1r = Ring(nc, self.ctx, "rw1", self.w1w)
                        self.w2r = Ring(nc, self.ctx, "rw2", self.w2w)
                        self.hsb = [sb5(f"hsb{i}", [T, NOT], BF16)
                                    for i in range(4)]
                        self.q2()
                    self.q3()
                with contextlib.ExitStack() as s4:
                    sb4 = lambda n, s, d=F32: self.sbt(s4, n, s, d)
                    self.kfm = sb4("kfm", [T, DCH * NOHT], BF16)
                    self.vtm = sb4("vtm", [T, NOH * D], BF16)
                    self.wqs = sb4("wqs", [T, DCH * D], BF16)
                    self.bmr_sl = [sb4(f"bmsl{i}", [T, 3 * T], BF16)
                                   for i in range(2)]
                    self.bmr = Ring(nc, self.ctx, "rbm", self.bmr_sl)
                    self.lnw = sb4("lnw", [T, D])
                    self.lnb = sb4("lnb", [T, D])
                    self.qf = sb4("qf", [T, 2 * DCH * T], BF16)
                    self.sTs = sb4("sTs", [T, 3 * T], BF16)
                    self.gscr = self.sqscr
                    self.x1s = [sb4(f"x1s{i}", [T, D]) for i in range(2)]
                    self.x1r = Ring(nc, self.ctx, "rx1", self.x1s)
                    self.q4(q4upto=9)
        return nc


def build_fused():
    k = LF()
    nc = k.build_fused()
    k.ctx.__exit__(None, None, None)
    return nc


def host_fused_inmaps(inputs):
    com0 = host_common(inputs)
    com1 = host_l1_common(inputs)
    x0 = np.ascontiguousarray(np.asarray(inputs["L0_attn_x"], np.float32))
    qi = np.arange(T)
    ki = np.arange(KC)
    ci = np.arange(NCHUNK)
    ti = np.arange(NOH)
    kk = np.arange(T)
    qq = np.arange(T)
    in_maps = []
    for core in range(8):
        b, k = divmod(core, 4)
        lo = 2048 * k - 256
        halo = (np.zeros((256, D), np.float32) if lo < 0
                else x0[b, lo:2048 * k])
        x_oh = np.ascontiguousarray(np.concatenate(
            [halo, x0[b, 2048 * k:2048 * (k + 1)]], axis=0))
        qg = (ti[:, None, None, None] + 16 * k - 2) * T + qi[None, None, :, None]
        kg = ci[None, :, None, None] * KC + ki[None, None, None, :]
        cm = np.where((kg <= qg), np.float32(0), np.float32(-1e9)).astype(NPBF)
        bm = np.zeros((NOWN, 3, T, T), np.float32)
        for i in range(NOWN):
            gt = 16 * k + i
            for o in range(3):
                gk = gt - (2 - o)
                if gk < 0:
                    continue
                qg2 = gt * T + qq[None, :]
                kg2 = gk * T + kk[:, None]
                bm[i, o] = np.where((qg2 - 256 < kg2) & (kg2 <= qg2),
                                    np.float32(1.0 / 512.0), np.float32(0))
        m = {}
        m["x_seq"] = x0[b]
        m["x_oh"] = x_oh
        m["cmask"] = cm
        m["wt"] = com0["wt"]
        m["w1a"] = com0["w1"]
        m["b1ra"] = com0["b1r"]
        m["w2a"] = com0["w2"]
        m["b2a"] = com0["b2"]
        m["wr"] = com0["wr"]
        m["br"] = com0["br"]
        m["gmb"] = com0["gmb"]
        m["btb"] = com0["btb"]
        m["bmsk"] = bm.astype(NPBF)
        for key in ("wqkv", "bqkvr", "bqv", "lnwb", "lnbb", "wg"):
            m[key] = com1[key]
        m["w1b"] = com1["w1"]
        m["b1rb"] = com1["b1r"]
        m["w2b"] = com1["w2"]
        m["b2b"] = com1["b2"]
        m["cst_in"] = com0["cst_in"]
        m["iota8"] = com0["iota8"]
        m["idn"] = com0["idn"]
        m["on1"] = com0["on1"]
        in_maps.append(m)
    return in_maps


from concourse.bass_utils import run_bass_kernel_spmd

_NC_CACHE = {}


def _get_nc():
    if "f" not in _NC_CACHE:
        _NC_CACHE["f"] = build_fused()
    return _NC_CACHE["f"]


def kernel(**inputs):
    nc = _get_nc()
    im = host_fused_inmaps(inputs)
    res = run_bass_kernel_spmd(nc, im, core_ids=list(range(8)))
    out = np.empty((B, S, D), np.float32)
    for core in range(8):
        b, k = divmod(core, 4)
        out[b, 2048 * k:2048 * (k + 1)] = res.results[core]["x2o"]
    return out
